# version 09
# baseline (speedup 1.0000x reference)
import math
from contextlib import ExitStack

import numpy as np

import concourse.bass as bass
import concourse.mybir as mybir
from concourse.bass_utils import run_bass_kernel_spmd

F32 = mybir.dt.float32
BF16 = mybir.dt.bfloat16
AF = mybir.ActivationFunctionType
ALU = mybir.AluOpType
AX = mybir.AxisListType

D = 1024
S = 2048
B_CORE = 2
NCH = 8
NT = S // 128
H = 16
HD = 64
DL = 256
LAM = 0.01
EPS = 1e-5


class Sched:
    ENGINES = ("sync", "tensor", "vector", "scalar", "gpsimd")

    LOCAL_SEMS = ("ve", "se", "gp", "pe", "dma", "dmb", "str")

    def __init__(self, nc, ctx):
        self.nc = nc
        self.ctx = ctx
        self.q = {e: [] for e in self.ENGINES}
        self._sems = {}
        self.guard = {}
        self.pfx = ""

    def set_prefix(self, p):
        self.pfx = p

    def _resolve(self, name):
        return (self.pfx + name) if name in self.LOCAL_SEMS else name

    def _sem(self, name):
        if name not in self._sems:
            self._sems[name] = [self.ctx.enter_context(self.nc.semaphore(name)), 0]
        return self._sems[name]

    def barrier(self, new_pfx=None):
        if new_pfx is not None:
            self.pfx = new_pfx
        toks = [(nm, ent[1]) for nm, ent in self._sems.items()]
        toks = [t for t in toks if t[1] > 0]
        for eng in self.ENGINES:
            if eng in ("sync", "tensor"):
                self.on(eng, lambda e: e.nop(), wait=toks,
                        inc=("pe" if eng == "tensor" else None), is_dma=True)
            else:
                self.on(eng,
                        lambda e: (e.engine_nop() if hasattr(e, "engine_nop")
                                   else e.nop()),
                        wait=toks)
        self.guard.clear()

    CHAINED = {"vector": "ve", "scalar": "se", "gpsimd": "gp"}

    def on(self, eng, fn, wait=(), inc=None, inc_by=1, is_dma=False):
        ch = self.CHAINED.get(eng)
        wait = list(wait)
        if inc is not None:
            inc = self._resolve(inc)
        if ch is not None and not is_dma:
            ch = self._resolve(ch)
            wait.append(self.tok(ch))
            if inc is None:
                inc, inc_by = ch, 1
            elif inc != ch:
                raise ValueError(f"{eng} ops must inc {ch}")
        tok = None
        if inc is not None:
            entry = self._sem(inc)
            entry[1] += inc_by
            tok = (inc, entry[1])
        wmax = {}
        for w in wait:
            if w is None:
                continue
            s, v = w
            if v > wmax.get(s, -1):
                wmax[s] = v
        whandles = [(self._sems[s][0], v) for s, v in wmax.items()]
        ihandle = self._sems[inc][0] if inc is not None else None

        def thunk(e, fn=fn, whandles=whandles, ihandle=ihandle, inc_by=inc_by):
            for (h, v) in whandles:
                e.wait_ge(h, v)
            ins = fn(e)
            if ihandle is not None:
                ins.then_inc(ihandle, inc_by)
        self.q[eng].append(thunk)
        return tok

    def dma(self, eng, out, in_, wait=(), inc="dma", **kw):
        return self.on(eng, lambda e: e.dma_start(out=out, in_=in_, **kw),
                       wait=wait, inc=inc, inc_by=16, is_dma=True)

    def tok(self, name):
        name = self._resolve(name)
        entry = self._sem(name)
        return (name, entry[1])

    def g(self, key):
        return self.guard.get(key)

    def setg(self, key, tok):
        self.guard[key] = tok

    def finalize(self, block):
        for name in self.ENGINES:
            thunks = self.q[name]
            if not thunks:
                continue

            def runner(e, ts=tuple(thunks)):
                for t in ts:
                    t(e)
            getattr(block, name)(runner)


def mm_chain(S_, psum_ap, pairs, wait=(), inc="pe", start=True, stop=True):
    n = len(pairs)

    def fn(e):
        ins = None
        for i, (l, r) in enumerate(pairs):
            ins = e.matmul(psum_ap, l, r,
                           start=(start and i == 0),
                           stop=(stop and i == n - 1),
                           skip_group_check=True)
        return ins
    return S_.on("tensor", fn, wait=wait, inc=inc)


def make_identity(S_, ident_ap):
    t0 = S_.on("gpsimd", lambda e: e.memset(ident_ap, 0.0), inc="gp")
    return S_.on("gpsimd",
                 lambda e: e.affine_select(out=ident_ap, in_=ident_ap,
                                           compare_op=ALU.not_equal, fill=1.0,
                                           base=0, pattern=[[-1, 128]],
                                           channel_multiplier=1),
                 wait=[t0], inc="gp")


def load_w_half(S_, w_dram, wstage, wbf, chunks, dout, wait=(), into_off=0):
    rounds = [(i, min(2, chunks - i)) for i in range(0, chunks, 2)]
    wv = w_dram[:, :].rearrange("(c p) n -> p c n", p=128)
    t_c = None
    for (c0, cs) in rounds:
        t_d = S_.dma("sync", wstage[:, 0:cs, 0:dout], wv[:, c0:c0 + cs, :],
                     wait=list(wait) + [S_.g("wstage")])
        t_c = S_.on("vector",
                    lambda e, c0=c0, cs=cs: e.tensor_copy(
                        wbf[:, into_off + c0:into_off + c0 + cs, 0:dout],
                        wstage[:, 0:cs, 0:dout]),
                    wait=[t_d], inc="ve")
        S_.setg("wstage", t_c)
    return t_c


def ln_tile(S_, st, t, x_ap, sq_scr, var_col, wait, from_psum=False):
    if not isinstance(sq_scr, bass.AP):
        sq_scr = sq_scr[:, :]
    if from_psum:
        t1 = S_.on("scalar",
                   lambda e: e.activation(sq_scr, x_ap, AF.Copy,
                                          accum_out=st["sum"][:, t:t + 1]),
                   wait=wait, inc="se")
    else:
        t1 = S_.on("vector",
                   lambda e: e.reduce_sum(st["sum"][:, t:t + 1], x_ap,
                                          axis=AX.X),
                   wait=wait, inc="ve")
    t2 = S_.on("scalar",
               lambda e: e.activation(sq_scr, x_ap, AF.Square,
                                      accum_out=st["sumq"][:, t:t + 1]),
               wait=wait, inc="se")

    S_.on("vector",
          lambda e: e.tensor_scalar(st["negm"][:, t:t + 1], st["sum"][:, t:t + 1],
                                    -1.0 / D, None, ALU.mult),
          wait=[t1])
    S_.on("vector",
          lambda e: e.tensor_scalar(var_col, st["negm"][:, t:t + 1],
                                    st["negm"][:, t:t + 1], -EPS,
                                    ALU.mult, ALU.add))
    t3 = S_.on("vector",
               lambda e: e.scalar_tensor_tensor(var_col, st["sumq"][:, t:t + 1],
                                                1.0 / D, var_col, ALU.mult,
                                                ALU.subtract),
               wait=[t2])
    t4 = S_.on("scalar",
               lambda e: e.activation(var_col, var_col, AF.Sqrt),
               wait=[t3], inc="se")
    S_.on("vector", lambda e: e.reciprocal(st["r"][:, t:t + 1], var_col),
          wait=[t4])
    return S_.on("vector",
                 lambda e: e.tensor_tensor(st["nmr"][:, t:t + 1],
                                           st["negm"][:, t:t + 1],
                                           st["r"][:, t:t + 1], ALU.mult))


def new_ln(nc, ctx, name, ntiles=NT):
    sb = lambda n, sh: ctx.enter_context(nc.sbuf_tensor(name + n, sh, F32))
    return {"sum": sb("_s", [128, ntiles]), "sumq": sb("_q", [128, ntiles]),
            "negm": sb("_m", [128, ntiles]), "r": sb("_r", [128, ntiles]),
            "nmr": sb("_n", [128, ntiles])}


ATTN_W = {True: [("wq", [D, D]), ("wk", [D, D]), ("wv", [D, D]),
                 ("wo", [D, D]), ("wdn", [D, DL]),
                 ("wuk", [DL, D]), ("wuv", [DL, D])],
          False: [("wqkv", [D, D]), ("wo", [D, D])]}


def build_attn(mla, sim_debug=False, phases=5, nbatch=None, debug_dump=()):
    nc = bass.Bass("TRN2", target_bir_lowering=False,
                   detect_race_conditions=not sim_debug)
    io = {"x": nc.dram_tensor("x", [B_CORE, S, D], F32, kind="ExternalInput"),
          "y": nc.dram_tensor("y", [B_CORE, S, D], F32, kind="ExternalOutput")}
    for k, sh in ATTN_W[mla]:
        io[k] = nc.dram_tensor(k, sh, F32, kind="ExternalInput")
    with ExitStack() as ctx:
        S_ = Sched(nc, ctx)
        block = ctx.enter_context(nc.Block())
        emit_attn(nc, S_, io, mla, phases=phases, nbatch=nbatch,
                  debug_dump=debug_dump)
        S_.on("gpsimd", lambda e_: e_.engine_nop(), wait=[S_.tok("str")])
        S_.finalize(block)
    return nc


def emit_attn(nc, S_, io, mla, pfx="", phases=5, nbatch=None, debug_dump=(),
              y_view=None):
    x_hbm = io["x"]
    y_hbm = io["y"]
    W = io

    with ExitStack() as ctx:
        def e(cm):
            return ctx.enter_context(cm)
        xres = e(nc.sbuf_tensor(pfx + "xres", [128, 2, D], F32))
        sq_scr = e(nc.sbuf_tensor(pfx + "sq_scr", [128, D], BF16))
        xhat_tok = e(nc.sbuf_tensor(pfx + "xhat_tok", [128, 2, D], BF16))
        xhatF = e(nc.sbuf_tensor(pfx + "xhatF", [128, NCH, S], BF16))
        qF = e(nc.sbuf_tensor(pfx + "qF", [128, NCH, S], BF16))
        poolA = e(nc.sbuf_tensor(pfx + "poolA", [128, 20736], BF16))
        latF = poolA[:, 0:4096].rearrange("p (c n) -> p c n", c=2)
        vTo = poolA[:, 4096:20736].rearrange("p (t h c) -> p t h c", t=NT, h=H)
        poolB = e(nc.sbuf_tensor(pfx + "poolB", [128, NCH, S], BF16))
        kvF = poolB
        OF = poolB
        wstage = e(nc.sbuf_tensor(pfx + "wstage", [128, 2, D], F32))
        wbf = e(nc.sbuf_tensor(pfx + "wbf", [128, NCH, D], BF16))
        ident_bf = e(nc.sbuf_tensor(pfx + "ident_bf", [128, 128], BF16))
        ident_f = e(nc.sbuf_tensor(pfx + "ident_f", [128, 128], F32))
        ones1 = e(nc.sbuf_tensor(pfx + "ones1", [1, 64], F32))
        ln1 = new_ln(nc, ctx, pfx + "ln1")
        ln2 = new_ln(nc, ctx, pfx + "ln2")
        var_s = e(nc.sbuf_tensor(pfx + "var_s", [128, 1], F32))
        decay = e(nc.sbuf_tensor(pfx + "decay", [128, 3968], F32)) if mla else None
        expS = e(nc.sbuf_tensor(pfx + "expS", [128, 2, 1024], BF16))
        rbc = e(nc.sbuf_tensor(pfx + "rbc", [64, 1024], F32))
        aos = e(nc.sbuf_tensor(pfx + "aos", [128, 2, D], F32))
        outt = aos
        psT = e(nc.psum_tensor(pfx + "psT", [128, 1024], BF16))
        psA = e(nc.psum_tensor(pfx + "psA", [128, 512], F32))
        psB = e(nc.psum_tensor(pfx + "psB", [128, 512], F32))
        psS = e(nc.psum_tensor(pfx + "psS", [128, 1024], F32))
        psAVe = e(nc.psum_tensor(pfx + "psAVe", [65, 512], F32))
        psAVo = e(nc.psum_tensor(pfx + "psAVo", [65, 512], F32))
        psBC = e(nc.psum_tensor(pfx + "psBC", [64, 512], F32))

        t_id = make_identity(S_, ident_bf[:, :])
        t_idf = make_identity(S_, ident_f[:, :])
        t_on = S_.on("gpsimd", lambda e_: e_.memset(ones1[:, :], 1.0), inc="gp")
        t_dec = None
        if mla:
            t_i = S_.on("gpsimd",
                        lambda e_: e_.iota(decay[:, :], [[1, 3968]], base=-1920,
                                           channel_multiplier=-1,
                                           allow_small_or_imprecise_dtypes=True),
                        inc="gp")
            t_ab = S_.on("scalar",
                         lambda e_: e_.activation(decay[:, :], decay[:, :], AF.Abs),
                         wait=[t_i], inc="se")
            t_dec = S_.on("scalar",
                          lambda e_: e_.activation(decay[:, :], decay[:, :],
                                                   AF.Exp, scale=-LAM),
                          wait=[t_ab], inc="se")

        for b in range(B_CORE if nbatch is None else nbatch):
            emit_attn_batch(S_, mla, b, x_hbm, y_hbm, W, locals(), phases,
                            y_view)

        if phases < 4.8 or (nbatch is not None and nbatch < B_CORE):
            t_z = S_.on("vector",
                        lambda e_: e_.tensor_copy(aos[:, 0, :], xres[:, 0, :]),
                        wait=[S_.tok("ve"), S_.tok("se"), S_.tok("pe"),
                              S_.tok("gp")], inc="ve")
            S_.dma("gpsimd", y_hbm[0][0:128, :], aos[:, 0, :], wait=[t_z],
                   inc="str")
        if debug_dump:
            allw = [S_.tok(s) for s in ("ve", "se", "pe", "gp", "dma")]
            Lx = locals()
            prev = None
            for nm in debug_dump:
                if "." in nm:
                    a_, b_ = nm.split(".")
                    ap = Lx[a_][b_]
                    nm = a_ + "_" + b_
                else:
                    ap = Lx[nm]
                if not isinstance(ap, bass.AP):
                    ap = ap[tuple(slice(None) for _ in ap.shape)]
                if len(ap.shape) > 2:
                    dims = " ".join(f"a{i}" for i in range(len(ap.shape) - 1))
                    ap = ap.rearrange(f"p {dims} -> p ({dims})")
                ncols = ap.shape[1]
                dbg = nc.dram_tensor(pfx + "dbg_" + nm, [128, ncols], F32,
                                     kind="ExternalOutput")
                for c0 in range(0, ncols, 2048):
                    cw = min(2048, ncols - c0)
                    t_c = S_.on("vector",
                                lambda e, ap=ap, c0=c0, cw=cw: e.tensor_copy(
                                    aos[:, 0, 0:cw] if cw < 2048 else
                                    aos[:, :, :].rearrange("p a b -> p (a b)")[:, 0:cw],
                                    ap[:, c0:c0 + cw]),
                                wait=allw + ([prev] if prev else []), inc="ve")
                    prev = S_.dma("gpsimd", dbg[:, c0:c0 + cw],
                                  aos[:, 0, 0:cw] if cw < 2048 else
                                  aos[:, :, :].rearrange("p a b -> p (a b)")[:, 0:cw],
                                  wait=[t_c], inc="str")


def emit_attn_batch(S_, mla, b, x_hbm, y_hbm, W, L, phases=5,
                    y_view=None):
    xres, sq_scr, xhat_tok = L["xres"], L["sq_scr"], L["xhat_tok"]
    xhatF, qF, latF, vTo = L["xhatF"], L["qF"], L["latF"], L["vTo"]
    kvF, OF = L["kvF"], L["OF"]
    wstage, wbf = L["wstage"], L["wbf"]
    ident_bf, ident_f, ones1 = L["ident_bf"], L["ident_f"], L["ones1"]
    ln1, ln2, var_s = L["ln1"], L["ln2"], L["var_s"]
    decay, expS, rbc = L["decay"], L["expS"], L["rbc"]
    aos, outt = L["aos"], L["outt"]
    psT, psA, psB, psS = L["psT"], L["psA"], L["psB"], L["psS"]
    psAVe, psAVo, psBC = L["psAVe"], L["psAVo"], L["psBC"]
    t_id, t_idf, t_on, t_dec = L["t_id"], L["t_idf"], L["t_on"], L["t_dec"]
    kRF = xhatF if mla else qF
    x_b = x_hbm[b]
    y_b = y_hbm[b]
    if y_view is None:
        y_view = lambda bb, tt: y_hbm[bb][tt * 128:(tt + 1) * 128, :]
    SCALE = (1.0 / 32.0) if mla else (1.0 / 8.0)

    for t in range(NT):
        rot = t % 2
        t_x = S_.dma("sync", xres[:, rot, :], x_b[t * 128:(t + 1) * 128, :],
                     wait=[S_.g(("xres", rot))])
        t_ln = ln_tile(S_, ln1, t, xres[:, rot, :], sq_scr[:, :], var_s[:, :],
                       wait=[t_x])
        t_ap = S_.on("vector",
                     lambda e, t=t, rot=rot: e.tensor_scalar(
                         xhat_tok[:, rot, :], xres[:, rot, :],
                         ln1["r"][:, t:t + 1], ln1["nmr"][:, t:t + 1],
                         ALU.mult, ALU.add),
                     wait=[t_ln, S_.g(("xh", rot))], inc="ve")
        S_.setg(("xres", rot), t_ap)

        def ftp(e, rot=rot):
            ins = None
            for c in range(NCH):
                ins = e.matmul(psT[:, c * 128:(c + 1) * 128],
                               xhat_tok[:, rot, c * 128:(c + 1) * 128],
                               ident_bf[:, :], is_transpose=True,
                               start=(c == 0), stop=(c == NCH - 1),
                               skip_group_check=True)
            return ins
        t_tp = S_.on("tensor", ftp, wait=[t_ap, t_id, S_.g("psT")], inc="pe")
        S_.setg(("xh", rot), t_tp)
        t_ev = S_.on("vector",
                     lambda e, t=t: e.tensor_copy(
                         xhatF[:, :, t * 128:(t + 1) * 128], psT[:, :].rearrange(
                             "p (c n) -> p c n", c=NCH)),
                     wait=[t_tp], inc="ve")
        S_.setg("psT", t_ev)
    t_xhatF = S_.tok("ve")
    if phases < 2:
        return

    def proj(dst, n_dout, wt_tok, src, src_nc, extra=(), accum=False):
        toks = []
        for mc in range(n_dout // 128):
            for nt4 in range(S // 512):
                psu, key = (psA, "psA") if (mc * 4 + nt4) % 2 == 0 else (psB, "psB")
                pairs = [(wbf[:, kc, mc * 128:(mc + 1) * 128],
                          src[:, kc, nt4 * 512:(nt4 + 1) * 512])
                         for kc in range(src_nc)]
                t_mm = mm_chain(S_, psu[:, :], pairs,
                                wait=[wt_tok, S_.g(key)] + list(extra))
                if accum:
                    t_e = S_.on("vector",
                                lambda e, dst=dst, mc=mc, nt4=nt4, psu=psu:
                                e.tensor_tensor(
                                    dst[:, mc, nt4 * 512:(nt4 + 1) * 512],
                                    dst[:, mc, nt4 * 512:(nt4 + 1) * 512],
                                    psu[:, :], ALU.add),
                                wait=[t_mm], inc="ve")
                else:
                    t_e = S_.on("vector",
                                lambda e, dst=dst, mc=mc, nt4=nt4, psu=psu:
                                e.tensor_copy(
                                    dst[:, mc, nt4 * 512:(nt4 + 1) * 512],
                                    psu[:, :]),
                                wait=[t_mm], inc="ve")
                S_.setg(key, t_e)
                toks.append(t_e)
        return toks

    if mla:
        t_w = load_w_half(S_, W["wq"], wstage, wbf, NCH, D,
                          wait=[S_.g("wbf")])
        tq = proj(qF, D, t_w, xhatF, NCH, extra=[t_xhatF, S_.g("qF")])
        S_.setg("wbf", S_.tok("pe"))
        t_w = load_w_half(S_, W["wk"], wstage, wbf, NCH, D, wait=[S_.g("wbf")])
        tk_ = proj(kvF, D, t_w, xhatF, NCH, extra=[S_.g("kvF")])
        S_.setg("wbf", S_.tok("pe"))
        t_w = load_w_half(S_, W["wv"], wstage, wbf, NCH, D, wait=[S_.g("wbf")])
        tv = proj(kvF, D, t_w, xhatF, NCH, extra=[tk_[-1]], accum=True)
        S_.setg("wbf", S_.tok("pe"))
        t_w = load_w_half(S_, W["wdn"], wstage, wbf, NCH, DL, wait=[S_.g("wbf")])
        tl = proj(latF, DL, t_w, kvF, NCH, extra=[tv[-1]])
        S_.setg("wbf", S_.tok("pe"))
        t_w = load_w_half(S_, W["wuk"], wstage, wbf, 2, D, wait=[S_.g("wbf")])
        tkr = proj(kRF, D, t_w, latF, 2, extra=[tl[-1]])
        S_.setg("wbf", S_.tok("pe"))
        t_w = load_w_half(S_, W["wuv"], wstage, wbf, 2, D, wait=[S_.g("wbf")])
        for tt in range(NT):
            for nh in range(2):
                psu, key = (psA, "psA") if (tt * 2 + nh) % 2 == 0 else (psB, "psB")
                pairs = [(latF[:, kc, tt * 128:(tt + 1) * 128],
                          wbf[:, kc, nh * 512:(nh + 1) * 512]) for kc in range(2)]
                t_mm = mm_chain(S_, psu[:, :], pairs,
                                wait=[t_w, tl[-1], S_.g(key)])
                t_e = S_.on("vector",
                            lambda e, tt=tt, nh=nh, psu=psu: e.tensor_copy(
                                vTo[:, tt, nh * 8:(nh + 1) * 8, 0:64],
                                psu[:, :].rearrange("p (h c) -> p h c", h=8)),
                            wait=[t_mm], inc="ve")
                S_.setg(key, t_e)
        t_keys_done = tkr[-1]
        t_qdone = tq[-1]
    else:
        t_w = load_w_half(S_, W["wqkv"], wstage, wbf, NCH, D, wait=[S_.g("wbf")])
        tq = proj(qF, D, t_w, xhatF, NCH, extra=[t_xhatF, S_.g("qF")])
        S_.setg("wbf", S_.tok("pe"))
        for tt in range(NT):
            def ftv(e, tt=tt):
                ins = None
                for c in range(NCH):
                    ins = e.matmul(psT[:, c * 128:(c + 1) * 128],
                                   qF[:, c, tt * 128:(tt + 1) * 128],
                                   ident_bf[:, :], is_transpose=True,
                                   start=(c == 0), stop=(c == NCH - 1),
                                   skip_group_check=True)
                return ins
            t_tp = S_.on("tensor", ftv,
                         wait=[tq[-1], t_id, S_.g("psT")], inc="pe")
            t_e = S_.on("vector",
                        lambda e, tt=tt: e.tensor_copy(
                            vTo[:, tt, :, 0:64],
                            psT[:, :].rearrange("p (h c) -> p h c", h=H)),
                        wait=[t_tp], inc="ve")
            S_.setg("psT", t_e)
        t_keys_done = tq[-1]
        t_qdone = tq[-1]
    t_vto = S_.tok("ve")
    t_ones = S_.on("gpsimd", lambda e: e.memset(vTo[:, :, :, 64:65], 1.0),
                   wait=[t_vto], inc="gp")

    if phases < 3:
        return
    sbank = {(0, 0): (psS[:, 0:512], "psS0"),
             (0, 1): (psS[:, 512:1024], "psS1"),
             (1, 0): (psA[:, :], "psA"), (1, 1): (psB[:, :], "psB")}
    for pr in range(NCH):
        for qc in range(4):
            q0 = qc * 512
            t_es = {}

            def emit_score_chain(half, tki, pr=pr, q0=q0, t_es=t_es):
                k0 = tki * 128
                rot = tki % 2
                bank, bkey = sbank[(half, rot)]
                p0 = half * 64
                t_s = S_.on("tensor",
                            lambda e, bank=bank, p0=p0, k0=k0: e.matmul(
                                bank, kRF[p0:p0 + 64, pr, k0:k0 + 128],
                                qF[p0:p0 + 64, pr, q0:q0 + 512],
                                start=True, stop=True, skip_group_check=True),
                            wait=[t_keys_done, t_qdone, S_.g(bkey)], inc="pe")
                if mla:
                    dv = decay[:, q0 - k0 + 1920:q0 - k0 + 1920 + 512]
                    t_d = S_.on("vector",
                                lambda e, bank=bank, dv=dv: e.tensor_tensor(
                                    bank, bank, dv, ALU.mult),
                                wait=[t_s, t_dec], inc="ve")
                else:
                    t_d = t_s
                t_e = S_.on("scalar",
                            lambda e, bank=bank, rot=rot, half=half:
                            e.activation(
                                expS[:, rot, half * 512:half * 512 + 512],
                                bank, AF.Exp, scale=SCALE),
                            wait=[t_d, S_.g(("expS", half, rot))], inc="se")
                S_.setg(bkey, t_e)
                t_es[(half, tki)] = t_e

            def emit_av(half, tki, pr=pr, t_es=t_es):
                rot = tki % 2
                psu = psAVe if half == 0 else psAVo
                wait_av = [t_es[(half, tki)], t_ones]
                if tki == 0:
                    wait_av += [S_.g("psAV")]
                t_av = S_.on("tensor",
                             lambda e, psu=psu, tki=tki, rot=rot, half=half:
                             e.matmul(psu[:, :], vTo[:, tki, 2 * pr + half, :],
                                      expS[:, rot, half * 512:half * 512 + 512],
                                      start=(tki == 0), stop=(tki == NT - 1),
                                      skip_group_check=True),
                             wait=wait_av, inc="pe")
                S_.setg(("expS", half, rot), t_av)
                return t_av

            for tki in range(NT):
                emit_score_chain(0, tki)
                emit_score_chain(1, tki)
                if tki > 0:
                    emit_av(0, tki - 1)
                    emit_av(1, tki - 1)
            emit_av(0, NT - 1)
            t_av = emit_av(1, NT - 1)
            t_rc = S_.on("vector",
                         lambda e: (e.reciprocal(rbc[0:1, 0:512],
                                                 psAVe[64:65, :]),
                                    e.reciprocal(rbc[0:1, 512:1024],
                                                 psAVo[64:65, :]))[-1],
                         wait=[t_av], inc="ve")
            t_b1 = S_.on("tensor",
                         lambda e: e.matmul(psBC[:, :], ones1[:, :],
                                            rbc[0:1, 0:512], start=True,
                                            stop=True, skip_group_check=True),
                         wait=[t_rc, t_on, S_.g("psBC")], inc="pe")
            t_c1 = S_.on("scalar",
                         lambda e: e.copy(rbc[:, 0:512], psBC[:, :]),
                         wait=[t_b1, S_.g("rbc")], inc="se")
            t_b2 = S_.on("tensor",
                         lambda e: e.matmul(psBC[:, :], ones1[:, :],
                                            rbc[0:1, 512:1024], start=True,
                                            stop=True, skip_group_check=True),
                         wait=[t_c1], inc="pe")
            t_c2 = S_.on("scalar",
                         lambda e: e.copy(rbc[:, 512:1024], psBC[:, :]),
                         wait=[t_b2], inc="se")
            S_.setg("psBC", t_c2)

            def fnrm(e, pr=pr, qc=qc):
                e.tensor_tensor(OF[0:64, pr, qc * 512:(qc + 1) * 512],
                                psAVe[0:64, :], rbc[:, 0:512], ALU.mult)
                return e.tensor_tensor(OF[64:128, pr, qc * 512:(qc + 1) * 512],
                                       psAVo[0:64, :], rbc[:, 512:1024],
                                       ALU.mult)
            t_nm = S_.on("vector", fnrm, wait=[t_c1, t_c2], inc="ve")
            S_.setg("psAV", t_nm)
            S_.setg("rbc", t_nm)
    t_attn = S_.tok("ve")

    if phases < 4:
        return
    S_.setg("wbf", S_.tok("pe"))
    t_wo = load_w_half(S_, W["wo"], wstage, wbf, NCH, D, wait=[S_.g("wbf")])
    for t in range(NT):
        rot = t % 2
        evs = []
        for half in range(2):
            psu, key = (psA, "psA") if half == 0 else (psB, "psB")

            def fwo(e, half=half, t=t, psu=psu):
                ins = None
                for mci in range(4):
                    mc = half * 4 + mci
                    for kc in range(NCH):
                        ins = e.matmul(psu[:, mci * 128:(mci + 1) * 128],
                                       wbf[:, kc, mc * 128:(mc + 1) * 128],
                                       OF[:, kc, t * 128:(t + 1) * 128],
                                       start=(kc == 0 and mci == 0),
                                       stop=(kc == NCH - 1 and mci == 3),
                                       skip_group_check=True)
                return ins
            t_mm = S_.on("tensor", fwo,
                         wait=[t_wo, t_attn, S_.g(key)], inc="pe")
            t_ev = S_.on("vector",
                         lambda e, half=half, rot=rot, psu=psu: e.tensor_copy(
                             aos[:, rot, half * 512:(half + 1) * 512],
                             psu[:, :]),
                         wait=[t_mm, S_.g(("aos", rot))], inc="ve")
            S_.setg(key, t_ev)
            evs.append(t_ev)
        if phases < 4.2:
            S_.setg(("aos", rot), evs[-1])
            continue

        def ftp2(e, rot=rot):
            ins = None
            for c in range(NCH):
                ins = e.matmul(psS[:, c * 128:(c + 1) * 128],
                               aos[:, rot, c * 128:(c + 1) * 128],
                               ident_f[:, :], is_transpose=True,
                               start=(c % 4 == 0), stop=(c % 4 == 3),
                               skip_group_check=True)
            return ins
        t_tp = S_.on("tensor", ftp2, wait=evs + [t_idf, S_.g("psS")], inc="pe")
        S_.setg(("aos", rot), t_tp)
        if phases < 4.4:
            S_.setg("psS", t_tp)
            continue
        t_ln = ln_tile(S_, ln2, t, psS[:, :], sq_scr[:, :], var_s[:, :],
                       wait=[t_tp], from_psum=True)
        if phases < 4.6:
            S_.setg("psS", t_ln)
            continue
        t_x = S_.dma("sync", xres[:, rot, :], x_b[t * 128:(t + 1) * 128, :],
                     wait=[S_.g(("xres", rot))])

        S_.on("vector",
              lambda e, t=t, rot=rot: e.tensor_scalar(
                  outt[:, rot, :], psS[:, :],
                  ln2["r"][:, t:t + 1], ln2["nmr"][:, t:t + 1],
                  ALU.mult, ALU.add),
              wait=[t_ln, S_.g(("outt", rot))])
        t_ap = S_.on("vector",
                     lambda e, rot=rot: e.tensor_tensor(
                         outt[:, rot, :], outt[:, rot, :],
                         xres[:, rot, :], ALU.add),
                     wait=[t_x])
        S_.setg("psS", t_ap)
        S_.setg(("xres", rot), t_ap)
        if phases < 4.8:
            S_.setg(("outt", rot), t_ap)
            S_.setg(("aos", rot), t_ap)
            continue
        t_st = S_.dma("gpsimd", y_view(b, t), outt[:, rot, :],
                      wait=[t_ap], inc="str")
        S_.setg(("outt", rot), t_st)
        S_.setg(("aos", rot), t_st)
    S_.setg("qF", S_.tok("pe"))
    S_.setg("kvF", S_.tok("pe"))


def load_w(S_, wv, wstage, wdst, chunks, dout, wait=(), stage_cols=1024):
    per = max(1, stage_cols // dout)
    t_c = None
    c0 = 0
    while c0 < chunks:
        cs = min(per, chunks - c0)
        sview = wstage[:, 0:cs * dout].rearrange("p (c n) -> p c n", c=cs)
        t_d = S_.dma("sync", sview, wv[:, c0:c0 + cs, :],
                     wait=list(wait) + [S_.g("wstage")])
        t_c = S_.on("vector",
                    lambda e, c0=c0, cs=cs, sview=sview: e.tensor_copy(
                        wdst[:, c0:c0 + cs, :], sview),
                    wait=[t_d])
        S_.setg("wstage", t_c)
        c0 += cs
    return t_c


def load_w2(S_, wv, stA, stB, wdst, chunks, wait=()):
    t_c = None
    for c in range(chunks):
        ap, key, q, sem = (stA, "wsA", "sync", "dma") if c % 2 == 0 else \
            (stB, "wsB", "gpsimd", "dmb")
        t_d = S_.dma(q, ap, wv[:, c, :], wait=list(wait) + [S_.g(key)],
                     inc=sem)
        t_c = S_.on("vector",
                    lambda e, c=c, ap=ap: e.tensor_copy(wdst[:, c, :], ap),
                    wait=[t_d])
        S_.setg(key, t_c)
    return t_c


FFN0_W = [("gT", [D, 8]), ("w1T", [8, D, D]), ("w2T", [8, D, D]),
          ("w3T", [8, D, D]), ("b1T", [D, 8]), ("b2T", [D, 8]),
          ("b3T", [D, 8]), ("sw1T", [D, 2 * D]), ("sw2T", [D, 2 * D]),
          ("sw3T", [2 * D, D])]


def build_l0ffn(sim_debug=False, nbatch=None, debug_dump=()):
    nc = bass.Bass("TRN2", target_bir_lowering=False,
                   detect_race_conditions=not sim_debug)
    io = {"x": nc.dram_tensor("x", [B_CORE, S, D], F32, kind="ExternalInput"),
          "y": nc.dram_tensor("y", [B_CORE, S, D], F32, kind="ExternalOutput")}
    for k, sh in FFN0_W:
        io[k] = nc.dram_tensor(k, sh, F32, kind="ExternalInput")
    with ExitStack() as ctx:
        S_ = Sched(nc, ctx)
        block = ctx.enter_context(nc.Block())
        emit_l0ffn(nc, S_, io, nbatch=nbatch, debug_dump=debug_dump)
        S_.on("gpsimd", lambda e_: e_.engine_nop(), wait=[S_.tok("str")])
        S_.finalize(block)
    return nc


def emit_l0ffn(nc, S_, io, pfx="", nbatch=None, debug_dump=()):
    x_hbm = io["x"]
    y_hbm = io["y"]
    gT_h, w1_h, w2_h, w3_h = io["gT"], io["w1T"], io["w2T"], io["w3T"]
    b1_h, b2_h, b3_h = io["b1T"], io["b2T"], io["b3T"]
    sw1_h, sw2_h, sw3_h = io["sw1T"], io["sw2T"], io["sw3T"]

    with ExitStack() as ctx:
        def e(cm):
            return ctx.enter_context(cm)
        AR = 16384 + 24576 + 4096 + 4096 + 2048 + 256 + 1024 + 4096 + 512 + 512
        arena = e(nc.sbuf_tensor(pfx + "arena", [128, AR], BF16))
        off = [0]

        def carve(n):
            o = off[0]
            off[0] += n
            return o
        o_xsF = carve(16384)
        xhatF = arena[:, o_xsF:o_xsF + 16384].rearrange("p (c n) -> p c n", c=NCH)
        o_w = carve(24576)
        wr1 = arena[:, o_w:o_w + 8192].rearrange("p (c n) -> p c n", c=NCH)
        wr3 = arena[:, o_w + 8192:o_w + 16384].rearrange("p (c n) -> p c n", c=NCH)
        wr2 = arena[:, o_w + 16384:o_w + 24576].rearrange("p (c n) -> p c n", c=NCH)
        swh = wr1
        g1 = arena[:, o_w + 8192:o_w + 24576].rearrange("p (c n) -> p c n", c=NCH)
        o_h = carve(4096)
        h_nt = arena[:, o_h:o_h + 4096].rearrange("p (c n) -> p c n", c=NCH)
        o_x3 = carve(4096)
        x3m = arena[:, o_x3:o_x3 + 4096].rearrange("p (c n) -> p c n", c=NCH)
        o_mf = carve(2048)
        maskF = arena[0:16, o_mf:o_mf + 2048]
        o_sel = carve(256)
        selb = arena[0:16, o_sel:o_sel + 256].rearrange("p (r n) -> p r n",
                                                        r=2)
        o_sq = carve(1024)
        sq_scr = arena[:, o_sq:o_sq + 1024]
        o_xscr = carve(4096)
        xs_scr = arena[:, o_xscr:o_xscr + 4096].rearrange("p (c n) -> p c n",
                                                          c=NCH)
        o_mb = carve(512)
        mb_sb = arena[:, o_mb:o_mb + 512]
        o_sbc = carve(512)
        sbc_sb = arena[:, o_sbc:o_sbc + 512]
        y_sb = e(nc.sbuf_tensor(pfx + "y_sb", [128, NCH, S], F32))
        xres = e(nc.sbuf_tensor(pfx + "xres", [128, 2, D], F32))
        scr32 = e(nc.sbuf_tensor(pfx + "scr32", [128, 3072], F32))
        xhat32 = scr32[:, 0:2048].rearrange("p (r n) -> p r n", r=2)
        xf32 = scr32[:, 2048:3072].rearrange("p (c n) -> p c n", c=NCH)
        outt = scr32[:, 0:2048].rearrange("p (r n) -> p r n", r=2)
        wstage = e(nc.sbuf_tensor(pfx + "wstage", [128, 1024], F32))
        wstgB = scr32[:, 0:1024]
        tmp32 = e(nc.sbuf_tensor(pfx + "tmp32", [128, 512], F32))
        logits = e(nc.sbuf_tensor(pfx + "logits", [128, NT, 8], F32))
        exps = e(nc.sbuf_tensor(pfx + "exps", [128, NT, 8], F32))
        cmb = e(nc.sbuf_tensor(pfx + "cmb", [128, NT, 16], BF16))
        ident_bf = e(nc.sbuf_tensor(pfx + "ident_bf", [128, 128], BF16))
        rst = e(nc.sbuf_tensor(pfx + "rst", [128, 6, NT], F32))
        nmx, den, m1, m2 = rst[:, 0, :], rst[:, 1, :], rst[:, 2, :], rst[:, 3, :]
        w1s, w2s = rst[:, 4, :], rst[:, 5, :]
        l8sb = e(nc.sbuf_tensor(pfx + "l8sb", [8, 128], F32))
        ident_f = e(nc.sbuf_tensor(pfx + "ident_f", [128, 128], F32))
        gT_sb = e(nc.sbuf_tensor(pfx + "gT_sb", [128, NCH, 8], F32))
        b1F = e(nc.sbuf_tensor(pfx + "b1F", [128, NCH, 8], F32))
        b2F = e(nc.sbuf_tensor(pfx + "b2F", [128, NCH, 8], F32))
        b3F = e(nc.sbuf_tensor(pfx + "b3F", [128, NCH, 8], F32))
        ln1 = new_ln(nc, ctx, pfx + "ln1")
        ln2 = ln1
        var_s = e(nc.sbuf_tensor(pfx + "var_s", [128, 1], F32))
        psA = e(nc.psum_tensor(pfx + "psA", [128, 512], F32))
        psB = e(nc.psum_tensor(pfx + "psB", [128, 512], F32))
        psC = e(nc.psum_tensor(pfx + "psC", [128, 512], F32))
        psD = e(nc.psum_tensor(pfx + "psD", [128, 512], F32))
        psE = e(nc.psum_tensor(pfx + "psE", [128, 512], F32))
        psF = e(nc.psum_tensor(pfx + "psF", [128, 512], F32))
        psW = e(nc.psum_tensor(pfx + "psW", [128, 1024], F32))

        t_idf = make_identity(S_, ident_f[:, :])
        t_idb = make_identity(S_, ident_bf[:, :])
        t_g = S_.dma("sync", gT_sb[:, :, :],
                     gT_h[:, :].rearrange("(c p) e -> p c e", p=128))
        t_b1 = S_.dma("sync", b1F[:, :, :],
                      b1_h[:, :].rearrange("(c p) e -> p c e", p=128))
        t_b2 = S_.dma("sync", b2F[:, :, :],
                      b2_h[:, :].rearrange("(c p) e -> p c e", p=128))
        t_b3 = S_.dma("sync", b3F[:, :, :],
                      b3_h[:, :].rearrange("(c p) e -> p c e", p=128))

        for b in range(B_CORE if nbatch is None else nbatch):
            x_b = x_hbm[b]
            y_b = y_hbm[b]
            for t in range(NT):
                rot = t % 2
                t_x = S_.dma("sync", xres[:, rot, :],
                             x_b[t * 128:(t + 1) * 128, :],
                             wait=[S_.g(("xres", rot))])
                t_ln = ln_tile(S_, ln1, t, xres[:, rot, :], sq_scr, var_s[:, :],
                               wait=[t_x])
                t_ap = S_.on("vector",
                             lambda e_, t=t, rot=rot: e_.tensor_scalar(
                                 xhat32[:, rot, :], xres[:, rot, :],
                                 ln1["r"][:, t:t + 1], ln1["nmr"][:, t:t + 1],
                                 ALU.mult, ALU.add),
                             wait=[t_ln, S_.g(("xh", rot)),
                                   S_.g(("outt", rot))])
                S_.setg(("xres", rot), t_ap)

                def ftp(e_, rot=rot):
                    ins = None
                    for c in range(NCH):
                        ins = e_.matmul(psW[:, c * 128:(c + 1) * 128],
                                        xhat32[:, rot, c * 128:(c + 1) * 128],
                                        ident_f[:, :], is_transpose=True,
                                        start=(c % 4 == 0), stop=(c % 4 == 3),
                                        skip_group_check=True)
                    return ins
                t_tp = S_.on("tensor", ftp,
                             wait=[t_ap, t_idf, S_.g("psW")], inc="pe")
                S_.setg(("xh", rot), t_tp)
                t_e1 = S_.on("vector",
                             lambda e_, t=t: e_.tensor_copy(
                                 xhatF[:, :, t * 128:(t + 1) * 128],
                                 psW[:, :].rearrange("p (c n) -> p c n", c=NCH)),
                             wait=[t_tp])
                t_e2 = S_.on("vector",
                             lambda e_: e_.tensor_copy(
                                 xf32[:, :, :],
                                 psW[:, :].rearrange("p (c n) -> p c n", c=NCH)),
                             wait=[S_.g("xf32")])
                S_.setg("psW", t_e2)
                pairs = [(gT_sb[:, kc, :], xf32[:, kc, :]) for kc in range(NCH)]
                t_rm = mm_chain(S_, psA[0:8, 0:128], pairs,
                                wait=[t_e2, t_g, S_.g("psA")])
                t_l8 = S_.on("scalar",
                             lambda e_: e_.copy(l8sb[:, :], psA[0:8, 0:128]),
                             wait=[t_rm], inc="se")
                S_.setg("psA", t_l8)
                t_lt = S_.on("tensor",
                             lambda e_: e_.matmul(psB[0:128, 0:8], l8sb[:, :],
                                                  ident_f[0:8, 0:8], start=True,
                                                  stop=True,
                                                  skip_group_check=True),
                             wait=[t_l8, t_idf, S_.g("psB")], inc="pe")
                t_lg = S_.on("vector",
                             lambda e_, t=t: e_.tensor_copy(
                                 logits[:, t, :], psB[0:128, 0:8]),
                             wait=[t_lt])
                S_.setg("psB", t_lg)
                S_.setg("xf32", t_rm)
            S_.setg("wsB", S_.tok("pe"))
            S_.on("vector", lambda e_: e_.reduce_max(nmx, logits[:, :, :],
                                                     axis=AX.X))
            S_.on("vector", lambda e_: e_.tensor_scalar(nmx, nmx, -1.0, None,
                                                        ALU.mult))
            tv = S_.tok("ve")
            for t in range(NT):
                S_.on("scalar",
                      lambda e_, t=t: e_.activation(
                          exps[:, t, :], logits[:, t, :], AF.Exp,
                          bias=nmx[:, t:t + 1],
                          accum_out=den[:, t:t + 1]),
                      wait=[tv], inc="se")
            tse = S_.tok("se")
            S_.on("vector", lambda e_: e_.reciprocal(den, den), wait=[tse])
            S_.on("vector", lambda e_: e_.reduce_max(m1, exps[:, :, :],
                                                     axis=AX.X))
            for t in range(NT):
                S_.on("vector",
                      lambda e_, t=t: e_.tensor_scalar(
                          cmb[:, t, 0:8], exps[:, t, :], m1[:, t:t + 1], None,
                          ALU.is_ge))
            S_.on("vector",
                  lambda e_: e_.scalar_tensor_tensor(
                      logits[:, :, :], cmb[:, :, 0:8], -1e30, exps[:, :, :],
                      ALU.mult, ALU.add))
            S_.on("vector", lambda e_: e_.reduce_max(m2, logits[:, :, :],
                                                     axis=AX.X))
            for t in range(NT):
                S_.on("vector",
                      lambda e_, t=t: e_.tensor_scalar(
                          cmb[:, t, 8:16], logits[:, t, :], m2[:, t:t + 1],
                          None, ALU.is_ge))
            S_.on("vector", lambda e_: e_.tensor_tensor(w1s, m1, den, ALU.mult))
            S_.on("vector", lambda e_: e_.tensor_tensor(w2s, m2, den, ALU.mult))
            for t in range(NT):
                S_.on("vector",
                      lambda e_, t=t: e_.tensor_scalar(
                          exps[:, t, :], cmb[:, t, 0:8], w1s[:, t:t + 1], None,
                          ALU.mult))
                S_.on("vector",
                      lambda e_, t=t: e_.tensor_tensor(
                          cmb[:, t, 0:8], cmb[:, t, 0:8], cmb[:, t, 8:16],
                          ALU.add))
                S_.on("vector",
                      lambda e_, t=t: e_.scalar_tensor_tensor(
                          cmb[:, t, 8:16], cmb[:, t, 8:16], w2s[:, t:t + 1],
                          exps[:, t, :], ALU.mult, ALU.add))
            t_rr = S_.tok("ve")
            for t in range(NT):
                t_mt = S_.on("tensor",
                             lambda e_, t=t: e_.matmul(
                                 psA[0:16, 0:128], cmb[:, t, :], ident_bf[:, :],
                                 start=True, stop=True, skip_group_check=True),
                             wait=[t_rr, S_.g("psA"), t_idb], inc="pe")
                t_me = S_.on("vector",
                             lambda e_, t=t: e_.tensor_copy(
                                 maskF[:, t * 128:(t + 1) * 128],
                                 psA[0:16, 0:128]),
                             wait=[t_mt])
                S_.setg("psA", t_me)
            t_mask = S_.tok("ve")
            for hh in range(2):
                w1v = sw1_h[:, :].rearrange("(c p) n -> p c n", p=128)[
                    :, :, hh * D:(hh + 1) * D]
                t_w = load_w2(S_, w1v, wstage[:, :], wstgB, swh, NCH,
                              wait=[S_.g("swh"), S_.tok("str")])
                for mcl in range(NCH):
                    for nt in range(4):
                        psu, key = (psC, "psC") if (mcl * 4 + nt) % 2 == 0 \
                            else (psD, "psD")
                        pairs = [(swh[:, kc, mcl * 128:(mcl + 1) * 128],
                                  xhatF[:, kc, nt * 512:(nt + 1) * 512])
                                 for kc in range(NCH)]
                        t_mm = mm_chain(S_, psu[:, :], pairs,
                                        wait=[t_w, S_.g(key)])
                        t_ev = S_.on("scalar",
                                     lambda e_, mcl=mcl, nt=nt, psu=psu:
                                     e_.activation(
                                         g1[:, mcl, nt * 512:(nt + 1) * 512],
                                         psu[:, :], AF.Silu),
                                     wait=[t_mm], inc="se")
                        S_.setg(key, t_ev)
                S_.setg("swh", S_.tok("pe"))
                w2v = sw2_h[:, :].rearrange("(c p) n -> p c n", p=128)[
                    :, :, hh * D:(hh + 1) * D]
                t_w = load_w2(S_, w2v, wstage[:, :], wstgB, swh, NCH,
                              wait=[S_.g("swh")])
                for mcl in range(NCH):
                    for nt in range(4):
                        psu, key = (psC, "psC") if (mcl * 4 + nt) % 2 == 0 \
                            else (psD, "psD")
                        pairs = [(swh[:, kc, mcl * 128:(mcl + 1) * 128],
                                  xhatF[:, kc, nt * 512:(nt + 1) * 512])
                                 for kc in range(NCH)]
                        t_mm = mm_chain(S_, psu[:, :], pairs,
                                        wait=[t_w, S_.g(key)])
                        t_ev = S_.on("vector",
                                     lambda e_, mcl=mcl, nt=nt, psu=psu:
                                     e_.tensor_tensor(
                                         g1[:, mcl, nt * 512:(nt + 1) * 512],
                                         g1[:, mcl, nt * 512:(nt + 1) * 512],
                                         psu[:, :], ALU.mult),
                                     wait=[t_mm])
                        S_.setg(key, t_ev)
                S_.setg("swh", S_.tok("pe"))
                w3v = sw3_h[:, :].rearrange("(c p) n -> p c n", p=128)[
                    :, hh * NCH:(hh + 1) * NCH, :]
                t_w = load_w2(S_, w3v, wstage[:, :], wstgB, swh, NCH,
                              wait=[S_.g("swh")])
                for mc in range(NCH):
                    for nt in range(4):
                        psu, key = (psC, "psC") if (mc * 4 + nt) % 2 == 0 \
                            else (psD, "psD")
                        pairs = [(swh[:, kc, mc * 128:(mc + 1) * 128],
                                  g1[:, kc, nt * 512:(nt + 1) * 512])
                                 for kc in range(NCH)]
                        t_mm = mm_chain(S_, psu[:, :], pairs,
                                        wait=[t_w, S_.g(key)])
                        if hh == 0:
                            t_ev = S_.on("vector",
                                         lambda e_, mc=mc, nt=nt, psu=psu:
                                         e_.tensor_copy(
                                             y_sb[:, mc, nt * 512:(nt + 1) * 512],
                                             psu[:, :]),
                                         wait=[t_mm])
                        else:
                            t_ev = S_.on("vector",
                                         lambda e_, mc=mc, nt=nt, psu=psu:
                                         e_.tensor_tensor(
                                             y_sb[:, mc, nt * 512:(nt + 1) * 512],
                                             y_sb[:, mc, nt * 512:(nt + 1) * 512],
                                             psu[:, :], ALU.add),
                                         wait=[t_mm])
                        S_.setg(key, t_ev)
                S_.setg("swh", S_.tok("pe"))
            t_xhat = S_.tok("ve")
            for ei in range(8):
                t_s0 = S_.on("gpsimd", lambda e_: e_.memset(selb, 0.0),
                             wait=[S_.g("selb")])
                S_.on("gpsimd",
                      lambda e_, ei=ei: e_.affine_select(
                          out=selb[:, 0, :], in_=selb[:, 0, :],
                          compare_op=ALU.not_equal, fill=1.0, base=-ei,
                          pattern=[[0, 128]], channel_multiplier=1),
                      wait=[t_s0])
                t_sel = S_.on("gpsimd",
                              lambda e_, ei=ei: e_.affine_select(
                                  out=selb[:, 1, :], in_=selb[:, 1, :],
                                  compare_op=ALU.not_equal, fill=1.0,
                                  base=-(8 + ei), pattern=[[0, 128]],
                                  channel_multiplier=1))
                t_w1 = load_w2(S_, w1_h[ei].rearrange("(c p) n -> p c n",
                                                      p=128),
                               wstage[:, :], wstgB, wr1, NCH,
                               wait=[S_.g("wr1"), S_.g("swh")])
                t_w3 = load_w2(S_, w3_h[ei].rearrange("(c p) n -> p c n",
                                                      p=128),
                               wstage[:, :], wstgB, wr3, NCH,
                               wait=[S_.g("wr3"), S_.g("swh")])
                t_w2 = load_w2(S_, w2_h[ei].rearrange("(c p) n -> p c n",
                                                      p=128),
                               wstage[:, :], wstgB, wr2, NCH,
                               wait=[S_.g("wr2"), S_.g("swh")])
                for nt in range(4):
                    n0 = nt * 512
                    t_mb = S_.on("tensor",
                                 lambda e_, n0=n0: e_.matmul(
                                     psE[:, :], selb[:, 0, :],
                                     maskF[:, n0:n0 + 512],
                                     start=True, stop=True,
                                     skip_group_check=True),
                                 wait=[t_mask, t_sel, S_.g("psE")], inc="pe")
                    t_mbe = S_.on("scalar",
                                  lambda e_: e_.copy(mb_sb, psE[:, :]),
                                  wait=[t_mb, S_.g("mb_sb")], inc="se")
                    S_.setg("psE", t_mbe)
                    t_sb = S_.on("tensor",
                                 lambda e_, n0=n0: e_.matmul(
                                     psF[:, :], selb[:, 1, :],
                                     maskF[:, n0:n0 + 512],
                                     start=True, stop=True,
                                     skip_group_check=True),
                                 wait=[t_mask, t_sel, S_.g("psF")], inc="pe")
                    t_sbe = S_.on("scalar",
                                  lambda e_: e_.copy(sbc_sb, psF[:, :]),
                                  wait=[t_sb, S_.g("sbc_sb")], inc="se")
                    S_.setg("psF", t_sbe)
                    t_bld = None
                    for kc in range(NCH):
                        t_bld = S_.on("vector",
                                      lambda e_, kc=kc, n0=n0: e_.tensor_tensor(
                                          xs_scr[:, kc, :],
                                          xhatF[:, kc, n0:n0 + 512],
                                          sbc_sb, ALU.mult),
                                      wait=[t_sbe, t_xhat, S_.g("xs_scr")])
                    S_.setg("sbc_sb", t_bld)
                    for mc in range(NCH):
                        psu, key = (psA, "psA") if mc % 2 == 0 else (psB, "psB")
                        pairs = [(wr1[:, kc, mc * 128:(mc + 1) * 128],
                                  xs_scr[:, kc, :])
                                 for kc in range(NCH)]
                        t_mm = mm_chain(S_, psu[:, :], pairs,
                                        wait=[t_w1, t_bld, S_.g(key)])
                        t_h = S_.on("scalar",
                                    lambda e_, mc=mc, psu=psu, ei=ei:
                                    e_.activation(
                                        h_nt[:, mc, :], psu[:, :], AF.Silu,
                                        bias=b1F[:, mc, ei:ei + 1]),
                                    wait=[t_mm, t_b1, S_.g("h_nt")],
                                    inc="se")
                        S_.setg(key, t_h)
                    if nt == 3:
                        S_.setg("wr1", S_.tok("pe"))
                    for mc in range(NCH):
                        psu, key = (psC, "psC") if mc % 2 == 0 else (psD, "psD")
                        pairs = [(wr3[:, kc, mc * 128:(mc + 1) * 128],
                                  xs_scr[:, kc, :])
                                 for kc in range(NCH)]
                        t_mm = mm_chain(S_, psu[:, :], pairs,
                                        wait=[t_w3, t_bld, S_.g(key)])
                        t_x3m = S_.on("vector",
                                      lambda e_, mc=mc, psu=psu, ei=ei:
                                      e_.scalar_tensor_tensor(
                                          x3m[:, mc, :], psu[:, :],
                                          b3F[:, mc, ei:ei + 1], mb_sb,
                                          ALU.add, ALU.mult),
                                      wait=[t_mm, t_b3, t_mbe,
                                            S_.g("x3m")])
                        S_.setg(key, t_x3m)
                    S_.setg("mb_sb", S_.tok("ve"))
                    S_.setg("xs_scr", S_.tok("pe"))
                    if nt == 3:
                        S_.setg("wr3", S_.tok("pe"))
                    for mc in range(NCH):
                        psu = psW[:, 0:512] if mc % 2 == 0 else psW[:, 512:1024]
                        key = "psW0" if mc % 2 == 0 else "psW1"
                        pairs = [(wr2[:, kc, mc * 128:(mc + 1) * 128],
                                  h_nt[:, kc, :])
                                 for kc in range(NCH)]
                        t_mm = mm_chain(S_, psu, pairs,
                                        wait=[t_w2, S_.g(key),
                                              S_.tok("se")])
                        t_st = S_.on("vector",
                                     lambda e_, mc=mc, psu=psu, ei=ei:
                                     e_.scalar_tensor_tensor(
                                         tmp32[:, :], psu,
                                         b2F[:, mc, ei:ei + 1],
                                         x3m[:, mc, :], ALU.add, ALU.mult),
                                     wait=[t_mm, t_b2])
                        t_ya = S_.on("vector",
                                     lambda e_, mc=mc, n0=n0:
                                     e_.tensor_tensor(
                                         y_sb[:, mc, n0:n0 + 512],
                                         y_sb[:, mc, n0:n0 + 512],
                                         tmp32[:, :], ALU.add))
                        S_.setg(key, t_ya)
                    S_.setg("h_nt", S_.tok("pe"))
                    S_.setg("x3m", S_.tok("ve"))
                S_.setg("wr2", S_.tok("pe"))
                S_.setg("selb", S_.tok("pe"))
            t_y = S_.tok("ve")
            for t in range(NT):
                rot = t % 2

                def ftp2(e_, t=t):
                    ins = None
                    for c in range(NCH):
                        ins = e_.matmul(psW[:, c * 128:(c + 1) * 128],
                                        y_sb[:, c, t * 128:(t + 1) * 128],
                                        ident_f[:, :], is_transpose=True,
                                        start=(c % 4 == 0), stop=(c % 4 == 3),
                                        skip_group_check=True)
                    return ins
                t_tp = S_.on("tensor", ftp2, wait=[t_y, t_idf, S_.g("psW")],
                             inc="pe")
                t_ln = ln_tile(S_, ln2, t, psW[:, :], sq_scr, var_s[:, :],
                               wait=[t_tp], from_psum=True)
                t_x = S_.dma("sync", xres[:, rot, :],
                             x_b[t * 128:(t + 1) * 128, :],
                             wait=[S_.g(("xres", rot))])
                S_.on("vector",
                      lambda e_, t=t: e_.tensor_scalar(
                          outt[:, t % 2, :], psW[:, :],
                          ln2["r"][:, t:t + 1], ln2["nmr"][:, t:t + 1],
                          ALU.mult, ALU.add),
                      wait=[t_ln, S_.g(("outt", rot))])
                t_ap = S_.on("vector",
                             lambda e_, rot=rot: e_.tensor_tensor(
                                 outt[:, rot, :], outt[:, rot, :],
                                 xres[:, rot, :], ALU.add),
                             wait=[t_x])
                S_.setg("psW", t_ap)
                S_.setg(("xres", rot), t_ap)
                t_st = S_.dma("gpsimd", y_b[t * 128:(t + 1) * 128, :],
                              outt[:, rot, :], wait=[t_ap], inc="str")
                S_.setg(("outt", rot), t_st)
            S_.setg("y_sb", S_.tok("ve"))
            S_.setg(("xh", 0), S_.tok("pe"))
            S_.setg(("xh", 1), S_.tok("pe"))

        if debug_dump:
            allw = [S_.tok(s) for s in ("ve", "se", "pe", "gp", "dma")]
            Lx = locals()
            prev = None
            for nm in debug_dump:
                if "." in nm:
                    a_, b_ = nm.split(".")
                    ap = Lx[a_][b_]
                    nm = a_ + "_" + b_
                else:
                    ap = Lx[nm]
                if not isinstance(ap, bass.AP):
                    ap = ap[tuple(slice(None) for _ in ap.shape)]
                if len(ap.shape) > 2:
                    dims = " ".join(f"a{i}" for i in range(len(ap.shape) - 1))
                    ap = ap.rearrange(f"p {dims} -> p ({dims})")
                ncols = ap.shape[1]
                nparts = ap.shape[0]
                dbg = nc.dram_tensor(pfx + "dbg_" + nm, [nparts, ncols], F32,
                                     kind="ExternalOutput")
                for c0 in range(0, ncols, 2048):
                    cw = min(2048, ncols - c0)
                    t_c = S_.on("vector",
                                lambda e_, ap=ap, c0=c0, cw=cw, nparts=nparts:
                                e_.tensor_copy(tmp32[0:nparts, 0:min(cw, 512)]
                                               if cw <= 512 else
                                               y_sb[0:nparts, 0, 0:cw],
                                               ap[:, c0:c0 + cw]),
                                wait=allw + ([prev] if prev else []))
                    prev = S_.dma("gpsimd", dbg[:, c0:c0 + cw],
                                  tmp32[0:nparts, 0:min(cw, 512)]
                                  if cw <= 512 else y_sb[0:nparts, 0, 0:cw],
                                  wait=[t_c], inc="str")


NPOS = 256
NSTEP = 16
NT4 = 32


FFN1_WN = ["wfT", "ufT", "wiT", "uiT", "wcT", "ucT", "woT", "uoT"]
FFN1_BN = ["bf", "bi", "bc", "bo", "in_c"]


def build_l1ffn(sim_debug=False, nstep=None, debug_dump=()):
    nc = bass.Bass("TRN2", target_bir_lowering=False,
                   detect_race_conditions=not sim_debug)
    io = {"x": nc.dram_tensor("x", [NSTEP, NPOS, D], F32, kind="ExternalInput"),
          "y": nc.dram_tensor("y", [NSTEP, NPOS, D], F32, kind="ExternalOutput")}
    for k in FFN1_WN:
        io[k] = nc.dram_tensor(k, [D, D], F32, kind="ExternalInput")
    for k in FFN1_BN:
        io[k] = nc.dram_tensor(k, [D], F32, kind="ExternalInput")
    with ExitStack() as ctx:
        S_ = Sched(nc, ctx)
        block = ctx.enter_context(nc.Block())
        emit_l1ffn(nc, S_, io, nstep=nstep, debug_dump=debug_dump)
        S_.on("gpsimd", lambda e_: e_.engine_nop(), wait=[S_.tok("str")])
        S_.finalize(block)
    return nc


def emit_l1ffn(nc, S_, io, pfx="", nstep=None, debug_dump=()):
    x_hbm = io["x"]
    y_hbm = io["y"]
    Wh = io
    Bh = io

    with ExitStack() as ctx:
        def e(cm):
            return ctx.enter_context(cm)
        wreg = e(nc.sbuf_tensor(pfx + "wreg", [128, 6, NCH, D], BF16))
        wv = {k: wreg[:, i, :, :] for i, k in
              enumerate(["wfT", "ufT", "wiT", "uiT", "wcT", "ucT"])}
        wv["woT"] = wv["wfT"]
        wv["uoT"] = wv["ufT"]
        wstage = e(nc.sbuf_tensor(pfx + "wstage", [128, 1024], F32))
        xres = e(nc.sbuf_tensor(pfx + "xres", [128, 2, D], F32))
        xhat_tok = e(nc.sbuf_tensor(pfx + "xhat_tok", [128, 2, D], BF16))
        xFs = e(nc.sbuf_tensor(pfx + "xFs", [128, 2, NCH, NPOS], BF16))
        c_sb = e(nc.sbuf_tensor(pfx + "c_sb", [128, NCH, NPOS], F32))
        cbf = e(nc.sbuf_tensor(pfx + "cbf", [128, NCH, NPOS], BF16))
        gf = e(nc.sbuf_tensor(pfx + "gf", [128, NCH, NPOS], F32))
        gi = e(nc.sbuf_tensor(pfx + "gi", [128, NCH, NPOS], F32))
        gc = e(nc.sbuf_tensor(pfx + "gc", [128, NCH, NPOS], F32))
        cuo = gf
        hF = gi
        ones_t = e(nc.sbuf_tensor(pfx + "ones_t", [128, NPOS], F32))
        biasF = e(nc.sbuf_tensor(pfx + "biasF", [128, 5, NCH], F32))
        bvw = {k: biasF[:, i, :] for i, k in
               enumerate(["bf", "bi", "bc", "bo", "in_c"])}
        ln1 = new_ln(nc, ctx, pfx + "ln1", ntiles=NT4)
        ln2 = new_ln(nc, ctx, pfx + "ln2", ntiles=NT4)
        var_s = e(nc.sbuf_tensor(pfx + "var_s", [128, 1], F32))
        sq_scr = e(nc.sbuf_tensor(pfx + "sq_scr", [128, D], BF16))
        ident_f = e(nc.sbuf_tensor(pfx + "ident_f", [128, 128], F32))
        ident_b = e(nc.sbuf_tensor(pfx + "ident_b", [128, 128], BF16))
        outt = e(nc.sbuf_tensor(pfx + "outt", [128, 2, D], F32))
        psA = e(nc.psum_tensor(pfx + "psA", [128, 512], F32))
        psB = e(nc.psum_tensor(pfx + "psB", [128, 512], F32))
        psW = e(nc.psum_tensor(pfx + "psW", [128, 1024], F32))
        psT = e(nc.psum_tensor(pfx + "psT", [128, 1024], BF16))

        t_idf = make_identity(S_, ident_f[:, :])
        t_idb = make_identity(S_, ident_b[:, :])
        t_one = S_.on("gpsimd", lambda e_: e_.memset(ones_t[:, :], 1.0))
        for k in ["bf", "bi", "bc", "bo", "in_c"]:
            S_.dma("sync", bvw[k],
                   Bh[k][:].rearrange("(c p) -> p c", p=128),
                   allow_slow_non_contiguous=True)
        t_bias = S_.tok("dma")
        for k in ["wfT", "ufT", "wiT", "uiT", "wcT", "ucT"]:
            load_w(S_, Wh[k][:, :].rearrange("(c p) n -> p c n", p=128),
                   wstage, wv[k], NCH, D)
        t_wload = S_.tok("ve")
        for ch in range(NCH):
            S_.on("vector",
                  lambda e_, ch=ch: e_.tensor_scalar(
                      c_sb[:, ch, :], ones_t[:, 0:NPOS],
                      bvw["in_c"][:, ch:ch + 1], None, ALU.mult),
                  wait=[t_one, t_bias])
        t_c0 = S_.on("vector", lambda e_: e_.tensor_copy(
            cbf[:, :, :], c_sb[:, :, :]))

        def prep_xhat(ts, ln, rot):
            toks = []
            for half in range(2):
                tile = ts * 2 + half
                hrot = half
                t_x = S_.dma("sync", xres[:, hrot, :],
                             x_hbm[ts][half * 128:(half + 1) * 128, :],
                             wait=[S_.g(("xres", hrot))])
                if ln is ln1:
                    t_ln = ln_tile(S_, ln, tile, xres[:, hrot, :], sq_scr,
                                   var_s[:, :], wait=[t_x])
                else:
                    t_ln = t_x
                t_ap = S_.on("vector",
                             lambda e_, tile=tile, hrot=hrot: e_.tensor_scalar(
                                 xhat_tok[:, hrot, :], xres[:, hrot, :],
                                 ln1["r"][:, tile:tile + 1],
                                 ln1["nmr"][:, tile:tile + 1],
                                 ALU.mult, ALU.add),
                             wait=[t_ln, S_.g(("xht", hrot))])
                S_.setg(("xres", hrot), t_ap)

                def ftp(e_, hrot=hrot):
                    ins = None
                    for cc in range(NCH):
                        ins = e_.matmul(psT[:, cc * 128:(cc + 1) * 128],
                                        xhat_tok[:, hrot, cc * 128:(cc + 1) * 128],
                                        ident_b[:, :], is_transpose=True,
                                        start=(cc == 0), stop=(cc == NCH - 1),
                                        skip_group_check=True)
                    return ins
                t_tp = S_.on("tensor", ftp,
                             wait=[t_ap, t_idb, S_.g("psT")], inc="pe")
                S_.setg(("xht", hrot), t_tp)
                t_ev = S_.on("vector",
                             lambda e_, rot=rot, half=half: e_.tensor_copy(
                                 xFs[:, rot, :, half * 128:(half + 1) * 128],
                                 psT[:, :].rearrange("p (c n) -> p c n",
                                                     c=NCH)),
                             wait=[t_tp])
                S_.setg("psT", t_ev)
                toks.append(t_ev)
            return toks[-1]

        GATES = [("wfT", "ufT", "bf", AF.Sigmoid, gf),
                 ("wiT", "uiT", "bi", AF.Sigmoid, gi),
                 ("wcT", "ucT", "bc", AF.Tanh, gc)]
        t_cup = t_c0
        for ts in range(NSTEP if nstep is None else nstep):
            rot = ts % 2
            t_xf = prep_xhat(ts, ln1, rot)
            for (wk, uk, bk, act, dst) in GATES:
                for mc in range(NCH):
                    psu, key = (psA, "psA") if mc % 2 == 0 else (psB, "psB")
                    pairs = ([(wv[wk][:, kc, mc * 128:(mc + 1) * 128],
                               xFs[:, rot, kc, :]) for kc in range(NCH)] +
                             [(wv[uk][:, kc, mc * 128:(mc + 1) * 128],
                               cbf[:, kc, :]) for kc in range(NCH)])
                    t_mm = mm_chain(S_, psu[:, 0:NPOS], pairs,
                                    wait=[t_wload, t_xf, t_cup, S_.g(key)])
                    t_g = S_.on("scalar",
                                lambda e_, mc=mc, psu=psu, dst=dst, act=act,
                                bk=bk: e_.activation(
                                    dst[:, mc, :], psu[:, 0:NPOS], act,
                                    bias=bvw[bk][:, mc:mc + 1]),
                                wait=[t_mm, t_bias], inc="se")
                    S_.setg(key, t_g)
            t_gates = S_.tok("se")
            S_.on("vector", lambda e_: e_.tensor_tensor(
                gf[:, :, :], gf[:, :, :], c_sb[:, :, :], ALU.mult),
                wait=[t_gates])
            S_.on("vector", lambda e_: e_.tensor_tensor(
                gi[:, :, :], gi[:, :, :], gc[:, :, :], ALU.mult))
            S_.on("vector", lambda e_: e_.tensor_tensor(
                c_sb[:, :, :], gf[:, :, :], gi[:, :, :], ALU.add))
            t_cup = S_.on("vector", lambda e_: e_.tensor_copy(
                cbf[:, :, :], c_sb[:, :, :]))

        t_pe_done = S_.tok("pe")
        load_w(S_, Wh["woT"][:, :].rearrange("(c p) n -> p c n", p=128),
               wstage, wv["woT"], NCH, D, wait=[t_pe_done])
        load_w(S_, Wh["uoT"][:, :].rearrange("(c p) n -> p c n", p=128),
               wstage, wv["uoT"], NCH, D, wait=[t_pe_done])
        t_wo = S_.tok("ve")
        for mc in range(NCH):
            psu, key = (psA, "psA") if mc % 2 == 0 else (psB, "psB")
            pairs = [(wv["uoT"][:, kc, mc * 128:(mc + 1) * 128],
                      cbf[:, kc, :]) for kc in range(NCH)]
            t_mm = mm_chain(S_, psu[:, 0:NPOS], pairs,
                            wait=[t_wo, t_cup, S_.g(key)])
            t_cu = S_.on("vector",
                         lambda e_, mc=mc, psu=psu: e_.tensor_copy(
                             cuo[:, mc, :], psu[:, 0:NPOS]),
                         wait=[t_mm])
            S_.setg(key, t_cu)
        t_cuo = S_.tok("ve")
        for ts in range(NSTEP if nstep is None else nstep):
            rot = ts % 2
            t_xf = prep_xhat(ts, ln2, rot)
            for mc in range(NCH):
                psu, key = (psA, "psA") if mc % 2 == 0 else (psB, "psB")
                pairs = [(wv["woT"][:, kc, mc * 128:(mc + 1) * 128],
                          xFs[:, rot, kc, :]) for kc in range(NCH)]
                t_mm = mm_chain(S_, psu[:, 0:NPOS], pairs,
                                wait=[t_wo, t_xf, t_cuo, S_.g(key)])
                t_add = S_.on("vector",
                              lambda e_, mc=mc, psu=psu: e_.tensor_tensor(
                                  psu[:, 0:NPOS], psu[:, 0:NPOS],
                                  cuo[:, mc, :], ALU.add),
                              wait=[t_mm])
                t_h = S_.on("scalar",
                            lambda e_, mc=mc, psu=psu: e_.activation(
                                hF[:, mc, :], psu[:, 0:NPOS], AF.Tanh,
                                bias=bvw["bo"][:, mc:mc + 1]),
                            wait=[t_bias, t_add, S_.g("hF")], inc="se")
                S_.setg(key, t_h)
            t_hd = S_.tok("se")
            for half in range(2):
                tile = ts * 2 + half
                hrot = half

                def ftp2(e_, half=half):
                    ins = None
                    for cc in range(NCH):
                        ins = e_.matmul(psW[:, cc * 128:(cc + 1) * 128],
                                        hF[:, cc, half * 128:(half + 1) * 128],
                                        ident_f[:, :], is_transpose=True,
                                        start=(cc % 4 == 0),
                                        stop=(cc % 4 == 3),
                                        skip_group_check=True)
                    return ins
                t_tp = S_.on("tensor", ftp2,
                             wait=[t_hd, t_idf, S_.g("psW")], inc="pe")
                t_ln = ln_tile(S_, ln2, tile, psW[:, :], sq_scr, var_s[:, :],
                               wait=[t_tp], from_psum=True)
                t_x = S_.dma("sync", xres[:, hrot, :],
                             x_hbm[ts][half * 128:(half + 1) * 128, :],
                             wait=[S_.g(("xres", hrot))])
                S_.on("vector",
                      lambda e_, tile=tile, hrot=hrot: e_.tensor_scalar(
                          outt[:, hrot, :], psW[:, :],
                          ln2["r"][:, tile:tile + 1],
                          ln2["nmr"][:, tile:tile + 1],
                          ALU.mult, ALU.add),
                      wait=[t_ln, S_.g(("outt", hrot))])
                t_ap = S_.on("vector",
                             lambda e_, hrot=hrot: e_.tensor_tensor(
                                 outt[:, hrot, :], outt[:, hrot, :],
                                 xres[:, hrot, :], ALU.add),
                             wait=[t_x])
                S_.setg("psW", t_ap)
                S_.setg(("xres", hrot), t_ap)
                t_st = S_.dma("gpsimd", y_hbm[ts][half * 128:(half + 1) * 128, :],
                              outt[:, hrot, :], wait=[t_ap], inc="str")
                S_.setg(("outt", hrot), t_st)
            S_.setg("hF", S_.tok("pe"))


def build_fused(sim_debug=False):
    nc = bass.Bass("TRN2", target_bir_lowering=False,
                   detect_race_conditions=not sim_debug)
    io1 = {"x": nc.dram_tensor("x", [B_CORE, S, D], F32, kind="ExternalInput")}
    for k, sh in ATTN_W[True]:
        io1[k] = nc.dram_tensor("a1" + k, sh, F32, kind="ExternalInput")
    io2 = {}
    for k, sh in FFN0_W:
        io2[k] = nc.dram_tensor("f2" + k, sh, F32, kind="ExternalInput")
    io3 = {}
    for k, sh in ATTN_W[False]:
        io3[k] = nc.dram_tensor("a3" + k, sh, F32, kind="ExternalInput")
    io4 = {"y": nc.dram_tensor("y", [NSTEP, NPOS, D], F32,
                               kind="ExternalOutput")}
    for k in FFN1_WN:
        io4[k] = nc.dram_tensor("f4" + k, [D, D], F32, kind="ExternalInput")
    for k in FFN1_BN:
        io4[k] = nc.dram_tensor("f4" + k, [D], F32, kind="ExternalInput")
    x1_i = nc.dram_tensor("x1_i", [B_CORE, S, D], F32)
    x2_i = nc.dram_tensor("x2_i", [B_CORE, S, D], F32)
    cc_in = nc.dram_tensor("cc_in", [8, B_CORE * NPOS * D], F32)
    cc_out = nc.dram_tensor("cc_out", [8, B_CORE * NPOS * D], F32)
    io1["y"] = x1_i
    io2["x"] = x1_i
    io2["y"] = x2_i
    io3["x"] = x2_i
    io3["y"] = cc_in
    io4["x"] = cc_out.ap().rearrange("s (b p n) -> (s b) p n",
                                     b=B_CORE, p=NPOS)

    from concourse.aot_env import allow_ici_all_to_all
    with ExitStack() as ctx, allow_ici_all_to_all():
        S_ = Sched(nc, ctx)
        block = ctx.enter_context(nc.Block())
        S_.set_prefix("a1_")
        emit_attn(nc, S_, io1, mla=True, pfx="a1_")
        S_.barrier("f2_")
        emit_l0ffn(nc, S_, io2, pfx="f2_")
        S_.barrier("a3_")
        bnc = cc_in.ap().rearrange("j (b p n) -> j b p n", b=B_CORE, p=NPOS)

        def yv(bb, tt):
            return bnc[tt // 2, bb, (tt % 2) * 128:(tt % 2) * 128 + 128, :]
        emit_attn(nc, S_, io3, mla=False, pfx="a3_", y_view=yv)
        S_.barrier("cc_")
        S_.on("gpsimd",
              lambda e_: e_.collective_compute(
                  "AllToAll", ALU.bypass,
                  replica_groups=[list(range(8))],
                  ins=[cc_in.ap().opt()], outs=[cc_out.ap().opt()]),
              inc="cc", is_dma=True)
        S_.barrier("f4_")
        emit_l1ffn(nc, S_, io4, pfx="f4_")
        S_.barrier("zz_")
        S_.finalize(block)
    return nc


def kernel(**inp):
    from concourse.aot_env import allow_ici_all_to_all
    ncores = 8
    f = lambda a: np.ascontiguousarray(np.asarray(a, dtype=np.float32))
    tr = lambda a: np.ascontiguousarray(np.asarray(a, dtype=np.float32).T)
    x = f(inp["L0_attn_x"])
    w = {"a1wq": f(inp["L0_attn_W_q"]), "a1wk": f(inp["L0_attn_W_k"]),
         "a1wv": f(inp["L0_attn_W_v"]), "a1wo": f(inp["L0_attn_W_o"]),
         "a1wdn": f(inp["L0_attn_W_down_kv"]),
         "a1wuk": f(inp["L0_attn_W_up_k"]), "a1wuv": f(inp["L0_attn_W_up_v"]),
         "f2gT": tr(inp["L0_ffn_gate_w"]),
         "f2w1T": np.ascontiguousarray(
             np.asarray(inp["L0_ffn_w1"], np.float32).transpose(0, 2, 1)),
         "f2w2T": np.ascontiguousarray(
             np.asarray(inp["L0_ffn_w2"], np.float32).transpose(0, 2, 1)),
         "f2w3T": np.ascontiguousarray(
             np.asarray(inp["L0_ffn_w3"], np.float32).transpose(0, 2, 1)),
         "f2b1T": tr(inp["L0_ffn_b1"]), "f2b2T": tr(inp["L0_ffn_b2"]),
         "f2b3T": tr(inp["L0_ffn_b3"]),
         "f2sw1T": tr(inp["L0_ffn_sw1"]), "f2sw2T": tr(inp["L0_ffn_sw2"]),
         "f2sw3T": tr(inp["L0_ffn_sw3"]),
         "a3wqkv": tr(inp["L1_attn_W_qkv"]), "a3wo": tr(inp["L1_attn_W_o"]),
         "f4wfT": tr(inp["L1_ffn_wf"]), "f4ufT": tr(inp["L1_ffn_uf"]),
         "f4wiT": tr(inp["L1_ffn_wi"]), "f4uiT": tr(inp["L1_ffn_ui"]),
         "f4wcT": tr(inp["L1_ffn_wc"]), "f4ucT": tr(inp["L1_ffn_uc"]),
         "f4woT": tr(inp["L1_ffn_wo"]), "f4uoT": tr(inp["L1_ffn_uo"]),
         "f4bf": f(inp["L1_ffn_bf"]), "f4bi": f(inp["L1_ffn_bi"]),
         "f4bc": f(inp["L1_ffn_bc"]), "f4bo": f(inp["L1_ffn_bo"]),
         "f4in_c": f(inp["L1_ffn_in_c"])}
    nc = build_fused()
    in_maps = [{"x": x[2 * c:2 * c + 2], **w} for c in range(ncores)]
    with allow_ici_all_to_all():
        res = run_bass_kernel_spmd(nc, in_maps, core_ids=list(range(ncores)))
    out = np.concatenate([res.results[c]["y"] for c in range(ncores)], axis=1)
    return out


# version 10
# speedup vs baseline: 1.0307x; 1.0307x over previous
import math
from contextlib import ExitStack

import numpy as np

import concourse.bass as bass
import concourse.mybir as mybir
from concourse.bass_utils import run_bass_kernel_spmd

F32 = mybir.dt.float32
BF16 = mybir.dt.bfloat16
AF = mybir.ActivationFunctionType
ALU = mybir.AluOpType
AX = mybir.AxisListType

D = 1024
S = 2048
B_CORE = 2
NCH = 8
NT = S // 128
H = 16
HD = 64
DL = 256
LAM = 0.01
EPS = 1e-5


class Sched:
    ENGINES = ("sync", "tensor", "vector", "scalar", "gpsimd")

    LOCAL_SEMS = ("ve", "se", "gp", "pe", "dma", "dmb", "str")

    def __init__(self, nc, ctx):
        self.nc = nc
        self.ctx = ctx
        self.q = {e: [] for e in self.ENGINES}
        self._sems = {}
        self.guard = {}
        self.pfx = ""

    def set_prefix(self, p):
        self.pfx = p

    def _resolve(self, name):
        return (self.pfx + name) if name in self.LOCAL_SEMS else name

    def _sem(self, name):
        if name not in self._sems:
            self._sems[name] = [self.ctx.enter_context(self.nc.semaphore(name)), 0]
        return self._sems[name]

    def barrier(self, new_pfx=None):
        if new_pfx is not None:
            self.pfx = new_pfx
        toks = [(nm, ent[1]) for nm, ent in self._sems.items()]
        toks = [t for t in toks if t[1] > 0]
        for eng in self.ENGINES:
            if eng in ("sync", "tensor"):
                self.on(eng, lambda e: e.nop(), wait=toks,
                        inc=("pe" if eng == "tensor" else None), is_dma=True)
            else:
                self.on(eng,
                        lambda e: (e.engine_nop() if hasattr(e, "engine_nop")
                                   else e.nop()),
                        wait=toks)
        self.guard.clear()

    CHAINED = {"vector": "ve", "scalar": "se", "gpsimd": "gp"}

    def on(self, eng, fn, wait=(), inc=None, inc_by=1, is_dma=False):
        ch = self.CHAINED.get(eng)
        wait = list(wait)
        if inc is not None:
            inc = self._resolve(inc)
        if ch is not None and not is_dma:
            ch = self._resolve(ch)
            wait.append(self.tok(ch))
            if inc is None:
                inc, inc_by = ch, 1
            elif inc != ch:
                raise ValueError(f"{eng} ops must inc {ch}")
        tok = None
        if inc is not None:
            entry = self._sem(inc)
            entry[1] += inc_by
            tok = (inc, entry[1])
        wmax = {}
        for w in wait:
            if w is None:
                continue
            s, v = w
            if v > wmax.get(s, -1):
                wmax[s] = v
        whandles = [(self._sems[s][0], v) for s, v in wmax.items()]
        ihandle = self._sems[inc][0] if inc is not None else None

        def thunk(e, fn=fn, whandles=whandles, ihandle=ihandle, inc_by=inc_by):
            for (h, v) in whandles:
                e.wait_ge(h, v)
            ins = fn(e)
            if ihandle is not None:
                ins.then_inc(ihandle, inc_by)
        self.q[eng].append(thunk)
        return tok

    def dma(self, eng, out, in_, wait=(), inc="dma", **kw):
        return self.on(eng, lambda e: e.dma_start(out=out, in_=in_, **kw),
                       wait=wait, inc=inc, inc_by=16, is_dma=True)

    def tok(self, name):
        name = self._resolve(name)
        entry = self._sem(name)
        return (name, entry[1])

    def g(self, key):
        return self.guard.get(key)

    def setg(self, key, tok):
        self.guard[key] = tok

    def finalize(self, block):
        for name in self.ENGINES:
            thunks = self.q[name]
            if not thunks:
                continue

            def runner(e, ts=tuple(thunks)):
                for t in ts:
                    t(e)
            getattr(block, name)(runner)


def mm_chain(S_, psum_ap, pairs, wait=(), inc="pe", start=True, stop=True):
    n = len(pairs)

    def fn(e):
        ins = None
        for i, (l, r) in enumerate(pairs):
            ins = e.matmul(psum_ap, l, r,
                           start=(start and i == 0),
                           stop=(stop and i == n - 1),
                           skip_group_check=True)
        return ins
    return S_.on("tensor", fn, wait=wait, inc=inc)


def make_identity(S_, ident_ap):
    t0 = S_.on("gpsimd", lambda e: e.memset(ident_ap, 0.0), inc="gp")
    return S_.on("gpsimd",
                 lambda e: e.affine_select(out=ident_ap, in_=ident_ap,
                                           compare_op=ALU.not_equal, fill=1.0,
                                           base=0, pattern=[[-1, 128]],
                                           channel_multiplier=1),
                 wait=[t0], inc="gp")


def load_w_half(S_, w_dram, wstage, wbf, chunks, dout, wait=(), into_off=0):
    rounds = [(i, min(2, chunks - i)) for i in range(0, chunks, 2)]
    wv = w_dram[:, :].rearrange("(c p) n -> p c n", p=128)
    t_c = None
    for (c0, cs) in rounds:
        t_d = S_.dma("sync", wstage[:, 0:cs, 0:dout], wv[:, c0:c0 + cs, :],
                     wait=list(wait) + [S_.g("wstage")])
        t_c = S_.on("vector",
                    lambda e, c0=c0, cs=cs: e.tensor_copy(
                        wbf[:, into_off + c0:into_off + c0 + cs, 0:dout],
                        wstage[:, 0:cs, 0:dout]),
                    wait=[t_d], inc="ve")
        S_.setg("wstage", t_c)
    return t_c


def ln_tile(S_, st, t, x_ap, sq_scr, var_col, wait, from_psum=False):
    if not isinstance(sq_scr, bass.AP):
        sq_scr = sq_scr[:, :]
    if from_psum:
        t1 = S_.on("scalar",
                   lambda e: e.activation(sq_scr, x_ap, AF.Copy,
                                          accum_out=st["sum"][:, t:t + 1]),
                   wait=wait, inc="se")
    else:
        t1 = S_.on("vector",
                   lambda e: e.reduce_sum(st["sum"][:, t:t + 1], x_ap,
                                          axis=AX.X),
                   wait=wait, inc="ve")
    t2 = S_.on("scalar",
               lambda e: e.activation(sq_scr, x_ap, AF.Square,
                                      accum_out=st["sumq"][:, t:t + 1]),
               wait=wait, inc="se")

    S_.on("vector",
          lambda e: e.tensor_scalar(st["negm"][:, t:t + 1], st["sum"][:, t:t + 1],
                                    -1.0 / D, None, ALU.mult),
          wait=[t1])
    S_.on("vector",
          lambda e: e.tensor_scalar(var_col, st["negm"][:, t:t + 1],
                                    st["negm"][:, t:t + 1], -EPS,
                                    ALU.mult, ALU.add))
    t3 = S_.on("vector",
               lambda e: e.scalar_tensor_tensor(var_col, st["sumq"][:, t:t + 1],
                                                1.0 / D, var_col, ALU.mult,
                                                ALU.subtract),
               wait=[t2])
    t4 = S_.on("scalar",
               lambda e: e.activation(var_col, var_col, AF.Sqrt),
               wait=[t3], inc="se")
    S_.on("vector", lambda e: e.reciprocal(st["r"][:, t:t + 1], var_col),
          wait=[t4])
    return S_.on("vector",
                 lambda e: e.tensor_tensor(st["nmr"][:, t:t + 1],
                                           st["negm"][:, t:t + 1],
                                           st["r"][:, t:t + 1], ALU.mult))


def new_ln(nc, ctx, name, ntiles=NT):
    sb = lambda n, sh: ctx.enter_context(nc.sbuf_tensor(name + n, sh, F32))
    return {"sum": sb("_s", [128, ntiles]), "sumq": sb("_q", [128, ntiles]),
            "negm": sb("_m", [128, ntiles]), "r": sb("_r", [128, ntiles]),
            "nmr": sb("_n", [128, ntiles])}


ATTN_W = {True: [("wq", [D, D]), ("wk", [D, D]), ("wv", [D, D]),
                 ("wo", [D, D]), ("wdn", [D, DL]),
                 ("wuk", [DL, D]), ("wuv", [DL, D])],
          False: [("wqkv", [D, D]), ("wo", [D, D])]}


def build_attn(mla, sim_debug=False, phases=5, nbatch=None, debug_dump=()):
    nc = bass.Bass("TRN2", target_bir_lowering=False,
                   detect_race_conditions=not sim_debug)
    io = {"x": nc.dram_tensor("x", [B_CORE, S, D], F32, kind="ExternalInput"),
          "y": nc.dram_tensor("y", [B_CORE, S, D], F32, kind="ExternalOutput")}
    for k, sh in ATTN_W[mla]:
        io[k] = nc.dram_tensor(k, sh, F32, kind="ExternalInput")
    with ExitStack() as ctx:
        S_ = Sched(nc, ctx)
        block = ctx.enter_context(nc.Block())
        emit_attn(nc, S_, io, mla, phases=phases, nbatch=nbatch,
                  debug_dump=debug_dump)
        S_.on("gpsimd", lambda e_: e_.engine_nop(), wait=[S_.tok("str")])
        S_.finalize(block)
    return nc


def emit_attn(nc, S_, io, mla, pfx="", phases=5, nbatch=None, debug_dump=(),
              y_view=None):
    x_hbm = io["x"]
    y_hbm = io["y"]
    W = io

    with ExitStack() as ctx:
        def e(cm):
            return ctx.enter_context(cm)
        xres = e(nc.sbuf_tensor(pfx + "xres", [128, 2, D], F32))
        sq_scr = e(nc.sbuf_tensor(pfx + "sq_scr", [128, D], BF16))
        xhat_tok = e(nc.sbuf_tensor(pfx + "xhat_tok", [128, 2, D], BF16))
        xhatF = e(nc.sbuf_tensor(pfx + "xhatF", [128, NCH, S], BF16))
        qF = e(nc.sbuf_tensor(pfx + "qF", [128, NCH, S], BF16))
        poolA = e(nc.sbuf_tensor(pfx + "poolA", [128, 20736], BF16))
        latF = poolA[:, 0:4096].rearrange("p (c n) -> p c n", c=2)
        vTo = poolA[:, 4096:20736].rearrange("p (t h c) -> p t h c", t=NT, h=H)
        poolB = e(nc.sbuf_tensor(pfx + "poolB", [128, NCH, S], BF16))
        kvF = poolB
        OF = poolB
        wstage = e(nc.sbuf_tensor(pfx + "wstage", [128, 2, D], F32))
        wbf = e(nc.sbuf_tensor(pfx + "wbf", [128, NCH, D], BF16))
        ident_bf = e(nc.sbuf_tensor(pfx + "ident_bf", [128, 128], BF16))
        ident_f = e(nc.sbuf_tensor(pfx + "ident_f", [128, 128], F32))
        ones1 = e(nc.sbuf_tensor(pfx + "ones1", [1, 64], F32))
        ln1 = new_ln(nc, ctx, pfx + "ln1")
        ln2 = new_ln(nc, ctx, pfx + "ln2")
        var_s = e(nc.sbuf_tensor(pfx + "var_s", [128, 1], F32))
        decay = e(nc.sbuf_tensor(pfx + "decay", [128, 3968], F32)) if mla else None
        expS = e(nc.sbuf_tensor(pfx + "expS", [128, 2, 1024], BF16))
        rbc = e(nc.sbuf_tensor(pfx + "rbc", [64, 1024], F32))
        aos = e(nc.sbuf_tensor(pfx + "aos", [128, 2, D], F32))
        outt = aos
        psT = e(nc.psum_tensor(pfx + "psT", [128, 1024], BF16))
        psA = e(nc.psum_tensor(pfx + "psA", [128, 512], F32))
        psB = e(nc.psum_tensor(pfx + "psB", [128, 512], F32))
        psS = e(nc.psum_tensor(pfx + "psS", [128, 1024], F32))
        psAVe = e(nc.psum_tensor(pfx + "psAVe", [65, 512], F32))
        psAVo = e(nc.psum_tensor(pfx + "psAVo", [65, 512], F32))
        psBC = e(nc.psum_tensor(pfx + "psBC", [64, 512], F32))

        t_id = make_identity(S_, ident_bf[:, :])
        t_idf = make_identity(S_, ident_f[:, :])
        t_on = S_.on("gpsimd", lambda e_: e_.memset(ones1[:, :], 1.0), inc="gp")
        t_dec = None
        if mla:
            t_i = S_.on("gpsimd",
                        lambda e_: e_.iota(decay[:, :], [[1, 3968]], base=-1920,
                                           channel_multiplier=-1,
                                           allow_small_or_imprecise_dtypes=True),
                        inc="gp")
            t_ab = S_.on("scalar",
                         lambda e_: e_.activation(decay[:, :], decay[:, :], AF.Abs),
                         wait=[t_i], inc="se")
            t_dec = S_.on("scalar",
                          lambda e_: e_.activation(decay[:, :], decay[:, :],
                                                   AF.Exp, scale=-LAM),
                          wait=[t_ab], inc="se")

        for b in range(B_CORE if nbatch is None else nbatch):
            emit_attn_batch(S_, mla, b, x_hbm, y_hbm, W, locals(), phases,
                            y_view)

        if phases < 4.8 or (nbatch is not None and nbatch < B_CORE):
            t_z = S_.on("vector",
                        lambda e_: e_.tensor_copy(aos[:, 0, :], xres[:, 0, :]),
                        wait=[S_.tok("ve"), S_.tok("se"), S_.tok("pe"),
                              S_.tok("gp")], inc="ve")
            S_.dma("gpsimd", y_hbm[0][0:128, :], aos[:, 0, :], wait=[t_z],
                   inc="str")
        if debug_dump:
            allw = [S_.tok(s) for s in ("ve", "se", "pe", "gp", "dma")]
            Lx = locals()
            prev = None
            for nm in debug_dump:
                if "." in nm:
                    a_, b_ = nm.split(".")
                    ap = Lx[a_][b_]
                    nm = a_ + "_" + b_
                else:
                    ap = Lx[nm]
                if not isinstance(ap, bass.AP):
                    ap = ap[tuple(slice(None) for _ in ap.shape)]
                if len(ap.shape) > 2:
                    dims = " ".join(f"a{i}" for i in range(len(ap.shape) - 1))
                    ap = ap.rearrange(f"p {dims} -> p ({dims})")
                ncols = ap.shape[1]
                dbg = nc.dram_tensor(pfx + "dbg_" + nm, [128, ncols], F32,
                                     kind="ExternalOutput")
                for c0 in range(0, ncols, 2048):
                    cw = min(2048, ncols - c0)
                    t_c = S_.on("vector",
                                lambda e, ap=ap, c0=c0, cw=cw: e.tensor_copy(
                                    aos[:, 0, 0:cw] if cw < 2048 else
                                    aos[:, :, :].rearrange("p a b -> p (a b)")[:, 0:cw],
                                    ap[:, c0:c0 + cw]),
                                wait=allw + ([prev] if prev else []), inc="ve")
                    prev = S_.dma("gpsimd", dbg[:, c0:c0 + cw],
                                  aos[:, 0, 0:cw] if cw < 2048 else
                                  aos[:, :, :].rearrange("p a b -> p (a b)")[:, 0:cw],
                                  wait=[t_c], inc="str")


def emit_attn_batch(S_, mla, b, x_hbm, y_hbm, W, L, phases=5,
                    y_view=None):
    xres, sq_scr, xhat_tok = L["xres"], L["sq_scr"], L["xhat_tok"]
    xhatF, qF, latF, vTo = L["xhatF"], L["qF"], L["latF"], L["vTo"]
    kvF, OF = L["kvF"], L["OF"]
    wstage, wbf = L["wstage"], L["wbf"]
    ident_bf, ident_f, ones1 = L["ident_bf"], L["ident_f"], L["ones1"]
    ln1, ln2, var_s = L["ln1"], L["ln2"], L["var_s"]
    decay, expS, rbc = L["decay"], L["expS"], L["rbc"]
    aos, outt = L["aos"], L["outt"]
    psT, psA, psB, psS = L["psT"], L["psA"], L["psB"], L["psS"]
    psAVe, psAVo, psBC = L["psAVe"], L["psAVo"], L["psBC"]
    t_id, t_idf, t_on, t_dec = L["t_id"], L["t_idf"], L["t_on"], L["t_dec"]
    kRF = xhatF if mla else qF
    x_b = x_hbm[b]
    y_b = y_hbm[b]
    if y_view is None:
        y_view = lambda bb, tt: y_hbm[bb][tt * 128:(tt + 1) * 128, :]
    SCALE = (1.0 / 32.0) if mla else (1.0 / 8.0)

    for t in range(NT):
        rot = t % 2
        t_x = S_.dma("sync", xres[:, rot, :], x_b[t * 128:(t + 1) * 128, :],
                     wait=[S_.g(("xres", rot))])
        t_ln = ln_tile(S_, ln1, t, xres[:, rot, :], sq_scr[:, :], var_s[:, :],
                       wait=[t_x])
        t_ap = S_.on("vector",
                     lambda e, t=t, rot=rot: e.tensor_scalar(
                         xhat_tok[:, rot, :], xres[:, rot, :],
                         ln1["r"][:, t:t + 1], ln1["nmr"][:, t:t + 1],
                         ALU.mult, ALU.add),
                     wait=[t_ln, S_.g(("xh", rot))], inc="ve")
        S_.setg(("xres", rot), t_ap)

        def ftp(e, rot=rot):
            ins = None
            for c in range(NCH):
                ins = e.matmul(psT[:, c * 128:(c + 1) * 128],
                               xhat_tok[:, rot, c * 128:(c + 1) * 128],
                               ident_bf[:, :], is_transpose=True,
                               start=(c == 0), stop=(c == NCH - 1),
                               skip_group_check=True)
            return ins
        t_tp = S_.on("tensor", ftp, wait=[t_ap, t_id, S_.g("psT")], inc="pe")
        S_.setg(("xh", rot), t_tp)
        t_ev = S_.on("vector",
                     lambda e, t=t: e.tensor_copy(
                         xhatF[:, :, t * 128:(t + 1) * 128], psT[:, :].rearrange(
                             "p (c n) -> p c n", c=NCH)),
                     wait=[t_tp], inc="ve")
        S_.setg("psT", t_ev)
    t_xhatF = S_.tok("ve")
    if phases < 2:
        return

    def proj(dst, n_dout, wt_tok, src, src_nc, extra=(), accum=False):
        toks = []
        for mc in range(n_dout // 128):
            for nt4 in range(S // 512):
                psu, key = (psA, "psA") if (mc * 4 + nt4) % 2 == 0 else (psB, "psB")
                pairs = [(wbf[:, kc, mc * 128:(mc + 1) * 128],
                          src[:, kc, nt4 * 512:(nt4 + 1) * 512])
                         for kc in range(src_nc)]
                t_mm = mm_chain(S_, psu[:, :], pairs,
                                wait=[wt_tok, S_.g(key)] + list(extra))
                if accum:
                    t_e = S_.on("vector",
                                lambda e, dst=dst, mc=mc, nt4=nt4, psu=psu:
                                e.tensor_tensor(
                                    dst[:, mc, nt4 * 512:(nt4 + 1) * 512],
                                    dst[:, mc, nt4 * 512:(nt4 + 1) * 512],
                                    psu[:, :], ALU.add),
                                wait=[t_mm], inc="ve")
                else:
                    t_e = S_.on("vector",
                                lambda e, dst=dst, mc=mc, nt4=nt4, psu=psu:
                                e.tensor_copy(
                                    dst[:, mc, nt4 * 512:(nt4 + 1) * 512],
                                    psu[:, :]),
                                wait=[t_mm], inc="ve")
                S_.setg(key, t_e)
                toks.append(t_e)
        return toks

    if mla:
        t_w = load_w_half(S_, W["wq"], wstage, wbf, NCH, D,
                          wait=[S_.g("wbf")])
        tq = proj(qF, D, t_w, xhatF, NCH, extra=[t_xhatF, S_.g("qF")])
        S_.setg("wbf", S_.tok("pe"))
        t_w = load_w_half(S_, W["wk"], wstage, wbf, NCH, D, wait=[S_.g("wbf")])
        tk_ = proj(kvF, D, t_w, xhatF, NCH, extra=[S_.g("kvF")])
        S_.setg("wbf", S_.tok("pe"))
        t_w = load_w_half(S_, W["wv"], wstage, wbf, NCH, D, wait=[S_.g("wbf")])
        tv = proj(kvF, D, t_w, xhatF, NCH, extra=[tk_[-1]], accum=True)
        S_.setg("wbf", S_.tok("pe"))
        t_w = load_w_half(S_, W["wdn"], wstage, wbf, NCH, DL, wait=[S_.g("wbf")])
        tl = proj(latF, DL, t_w, kvF, NCH, extra=[tv[-1]])
        S_.setg("wbf", S_.tok("pe"))
        t_w = load_w_half(S_, W["wuk"], wstage, wbf, 2, D, wait=[S_.g("wbf")])
        tkr = proj(kRF, D, t_w, latF, 2, extra=[tl[-1]])
        S_.setg("wbf", S_.tok("pe"))
        t_w = load_w_half(S_, W["wuv"], wstage, wbf, 2, D, wait=[S_.g("wbf")])
        for tt in range(NT):
            for nh in range(2):
                psu, key = (psA, "psA") if (tt * 2 + nh) % 2 == 0 else (psB, "psB")
                pairs = [(latF[:, kc, tt * 128:(tt + 1) * 128],
                          wbf[:, kc, nh * 512:(nh + 1) * 512]) for kc in range(2)]
                t_mm = mm_chain(S_, psu[:, :], pairs,
                                wait=[t_w, tl[-1], S_.g(key)])
                t_e = S_.on("vector",
                            lambda e, tt=tt, nh=nh, psu=psu: e.tensor_copy(
                                vTo[:, tt, nh * 8:(nh + 1) * 8, 0:64],
                                psu[:, :].rearrange("p (h c) -> p h c", h=8)),
                            wait=[t_mm], inc="ve")
                S_.setg(key, t_e)
        t_keys_done = tkr[-1]
        t_qdone = tq[-1]
    else:
        t_w = load_w_half(S_, W["wqkv"], wstage, wbf, NCH, D, wait=[S_.g("wbf")])
        tq = proj(qF, D, t_w, xhatF, NCH, extra=[t_xhatF, S_.g("qF")])
        S_.setg("wbf", S_.tok("pe"))
        for tt in range(NT):
            def ftv(e, tt=tt):
                ins = None
                for c in range(NCH):
                    ins = e.matmul(psT[:, c * 128:(c + 1) * 128],
                                   qF[:, c, tt * 128:(tt + 1) * 128],
                                   ident_bf[:, :], is_transpose=True,
                                   start=(c == 0), stop=(c == NCH - 1),
                                   skip_group_check=True)
                return ins
            t_tp = S_.on("tensor", ftv,
                         wait=[tq[-1], t_id, S_.g("psT")], inc="pe")
            t_e = S_.on("vector",
                        lambda e, tt=tt: e.tensor_copy(
                            vTo[:, tt, :, 0:64],
                            psT[:, :].rearrange("p (h c) -> p h c", h=H)),
                        wait=[t_tp], inc="ve")
            S_.setg("psT", t_e)
        t_keys_done = tq[-1]
        t_qdone = tq[-1]
    t_vto = S_.tok("ve")
    t_ones = S_.on("gpsimd", lambda e: e.memset(vTo[:, :, :, 64:65], 1.0),
                   wait=[t_vto], inc="gp")

    if phases < 3:
        return
    sbank = {(0, 0): (psS[:, 0:512], "psS0"),
             (0, 1): (psS[:, 512:1024], "psS1"),
             (1, 0): (psA[:, :], "psA"), (1, 1): (psB[:, :], "psB")}
    for pr in range(NCH):
        for qc in range(4):
            q0 = qc * 512
            t_es = {}

            def emit_score_chain(half, tki, pr=pr, q0=q0, t_es=t_es):
                k0 = tki * 128
                rot = tki % 2
                bank, bkey = sbank[(half, rot)]
                p0 = half * 64
                t_s = S_.on("tensor",
                            lambda e, bank=bank, p0=p0, k0=k0: e.matmul(
                                bank, kRF[p0:p0 + 64, pr, k0:k0 + 128],
                                qF[p0:p0 + 64, pr, q0:q0 + 512],
                                start=True, stop=True, skip_group_check=True),
                            wait=[t_keys_done, t_qdone, S_.g(bkey)], inc="pe")
                if mla:
                    dv = decay[:, q0 - k0 + 1920:q0 - k0 + 1920 + 512]
                    t_d = S_.on("vector",
                                lambda e, bank=bank, dv=dv: e.tensor_tensor(
                                    bank, bank, dv, ALU.mult),
                                wait=[t_s, t_dec], inc="ve")
                else:
                    t_d = t_s
                t_e = S_.on("scalar",
                            lambda e, bank=bank, rot=rot, half=half:
                            e.activation(
                                expS[:, rot, half * 512:half * 512 + 512],
                                bank, AF.Exp, scale=SCALE),
                            wait=[t_d, S_.g(("expS", half, rot))], inc="se")
                S_.setg(bkey, t_e)
                t_es[(half, tki)] = t_e

            def emit_av(half, tki, pr=pr, t_es=t_es):
                rot = tki % 2
                psu = psAVe if half == 0 else psAVo
                wait_av = [t_es[(half, tki)], t_ones]
                if tki == 0:
                    wait_av += [S_.g("psAV")]
                t_av = S_.on("tensor",
                             lambda e, psu=psu, tki=tki, rot=rot, half=half:
                             e.matmul(psu[:, :], vTo[:, tki, 2 * pr + half, :],
                                      expS[:, rot, half * 512:half * 512 + 512],
                                      start=(tki == 0), stop=(tki == NT - 1),
                                      skip_group_check=True),
                             wait=wait_av, inc="pe")
                S_.setg(("expS", half, rot), t_av)
                return t_av

            for tki in range(NT):
                emit_score_chain(0, tki)
                emit_score_chain(1, tki)
                if tki > 0:
                    emit_av(0, tki - 1)
                    emit_av(1, tki - 1)
            emit_av(0, NT - 1)
            t_av = emit_av(1, NT - 1)
            t_rc = S_.on("vector",
                         lambda e: (e.reciprocal(rbc[0:1, 0:512],
                                                 psAVe[64:65, :]),
                                    e.reciprocal(rbc[0:1, 512:1024],
                                                 psAVo[64:65, :]))[-1],
                         wait=[t_av], inc="ve")
            t_b1 = S_.on("tensor",
                         lambda e: e.matmul(psBC[:, :], ones1[:, :],
                                            rbc[0:1, 0:512], start=True,
                                            stop=True, skip_group_check=True),
                         wait=[t_rc, t_on, S_.g("psBC")], inc="pe")
            t_c1 = S_.on("scalar",
                         lambda e: e.copy(rbc[:, 0:512], psBC[:, :]),
                         wait=[t_b1, S_.g("rbc")], inc="se")
            t_b2 = S_.on("tensor",
                         lambda e: e.matmul(psBC[:, :], ones1[:, :],
                                            rbc[0:1, 512:1024], start=True,
                                            stop=True, skip_group_check=True),
                         wait=[t_c1], inc="pe")
            t_c2 = S_.on("scalar",
                         lambda e: e.copy(rbc[:, 512:1024], psBC[:, :]),
                         wait=[t_b2], inc="se")
            S_.setg("psBC", t_c2)

            def fnrm(e, pr=pr, qc=qc):
                e.tensor_tensor(OF[0:64, pr, qc * 512:(qc + 1) * 512],
                                psAVe[0:64, :], rbc[:, 0:512], ALU.mult)
                return e.tensor_tensor(OF[64:128, pr, qc * 512:(qc + 1) * 512],
                                       psAVo[0:64, :], rbc[:, 512:1024],
                                       ALU.mult)
            t_nm = S_.on("vector", fnrm, wait=[t_c1, t_c2], inc="ve")
            S_.setg("psAV", t_nm)
            S_.setg("rbc", t_nm)
    t_attn = S_.tok("ve")

    if phases < 4:
        return
    S_.setg("wbf", S_.tok("pe"))
    t_wo = load_w_half(S_, W["wo"], wstage, wbf, NCH, D, wait=[S_.g("wbf")])
    for t in range(NT):
        rot = t % 2
        evs = []
        for half in range(2):
            psu, key = (psA, "psA") if half == 0 else (psB, "psB")

            def fwo(e, half=half, t=t, psu=psu):
                ins = None
                for mci in range(4):
                    mc = half * 4 + mci
                    for kc in range(NCH):
                        ins = e.matmul(psu[:, mci * 128:(mci + 1) * 128],
                                       wbf[:, kc, mc * 128:(mc + 1) * 128],
                                       OF[:, kc, t * 128:(t + 1) * 128],
                                       start=(kc == 0 and mci == 0),
                                       stop=(kc == NCH - 1 and mci == 3),
                                       skip_group_check=True)
                return ins
            t_mm = S_.on("tensor", fwo,
                         wait=[t_wo, t_attn, S_.g(key)], inc="pe")
            t_ev = S_.on("vector",
                         lambda e, half=half, rot=rot, psu=psu: e.tensor_copy(
                             aos[:, rot, half * 512:(half + 1) * 512],
                             psu[:, :]),
                         wait=[t_mm, S_.g(("aos", rot))], inc="ve")
            S_.setg(key, t_ev)
            evs.append(t_ev)
        if phases < 4.2:
            S_.setg(("aos", rot), evs[-1])
            continue

        def ftp2(e, rot=rot):
            ins = None
            for c in range(NCH):
                ins = e.matmul(psS[:, c * 128:(c + 1) * 128],
                               aos[:, rot, c * 128:(c + 1) * 128],
                               ident_f[:, :], is_transpose=True,
                               start=(c % 4 == 0), stop=(c % 4 == 3),
                               skip_group_check=True)
            return ins
        t_tp = S_.on("tensor", ftp2, wait=evs + [t_idf, S_.g("psS")], inc="pe")
        S_.setg(("aos", rot), t_tp)
        if phases < 4.4:
            S_.setg("psS", t_tp)
            continue
        t_ln = ln_tile(S_, ln2, t, psS[:, :], sq_scr[:, :], var_s[:, :],
                       wait=[t_tp], from_psum=True)
        if phases < 4.6:
            S_.setg("psS", t_ln)
            continue
        t_x = S_.dma("sync", xres[:, rot, :], x_b[t * 128:(t + 1) * 128, :],
                     wait=[S_.g(("xres", rot))])

        S_.on("vector",
              lambda e, t=t, rot=rot: e.tensor_scalar(
                  outt[:, rot, :], psS[:, :],
                  ln2["r"][:, t:t + 1], ln2["nmr"][:, t:t + 1],
                  ALU.mult, ALU.add),
              wait=[t_ln, S_.g(("outt", rot))])
        t_ap = S_.on("vector",
                     lambda e, rot=rot: e.tensor_tensor(
                         outt[:, rot, :], outt[:, rot, :],
                         xres[:, rot, :], ALU.add),
                     wait=[t_x])
        S_.setg("psS", t_ap)
        S_.setg(("xres", rot), t_ap)
        if phases < 4.8:
            S_.setg(("outt", rot), t_ap)
            S_.setg(("aos", rot), t_ap)
            continue
        t_st = S_.dma("gpsimd", y_view(b, t), outt[:, rot, :],
                      wait=[t_ap], inc="str")
        S_.setg(("outt", rot), t_st)
        S_.setg(("aos", rot), t_st)
    S_.setg("qF", S_.tok("pe"))
    S_.setg("kvF", S_.tok("pe"))


def load_w(S_, wv, wstage, wdst, chunks, dout, wait=(), stage_cols=1024):
    per = max(1, stage_cols // dout)
    t_c = None
    c0 = 0
    while c0 < chunks:
        cs = min(per, chunks - c0)
        sview = wstage[:, 0:cs * dout].rearrange("p (c n) -> p c n", c=cs)
        t_d = S_.dma("sync", sview, wv[:, c0:c0 + cs, :],
                     wait=list(wait) + [S_.g("wstage")])
        t_c = S_.on("vector",
                    lambda e, c0=c0, cs=cs, sview=sview: e.tensor_copy(
                        wdst[:, c0:c0 + cs, :], sview),
                    wait=[t_d])
        S_.setg("wstage", t_c)
        c0 += cs
    return t_c


def load_w2(S_, wv, stA, stB, wdst, chunks, wait=()):
    t_c = None
    for c in range(chunks):
        ap, key, q, sem = (stA, "wsA", "sync", "dma") if c % 2 == 0 else \
            (stB, "wsB", "gpsimd", "dmb")
        t_d = S_.dma(q, ap, wv[:, c, :], wait=list(wait) + [S_.g(key)],
                     inc=sem)
        t_c = S_.on("vector",
                    lambda e, c=c, ap=ap: e.tensor_copy(wdst[:, c, :], ap),
                    wait=[t_d])
        S_.setg(key, t_c)
    return t_c


FFN0_W = [("gT", [D, 8]), ("w1T", [8, D, D]), ("w2T", [8, D, D]),
          ("w3T", [8, D, D]), ("b1T", [D, 8]), ("b2T", [D, 8]),
          ("b3T", [D, 8]), ("sw1T", [D, 2 * D]), ("sw2T", [D, 2 * D]),
          ("sw3T", [2 * D, D])]


def build_l0ffn(sim_debug=False, nbatch=None, debug_dump=()):
    nc = bass.Bass("TRN2", target_bir_lowering=False,
                   detect_race_conditions=not sim_debug)
    io = {"x": nc.dram_tensor("x", [B_CORE, S, D], F32, kind="ExternalInput"),
          "y": nc.dram_tensor("y", [B_CORE, S, D], F32, kind="ExternalOutput")}
    for k, sh in FFN0_W:
        io[k] = nc.dram_tensor(k, sh, F32, kind="ExternalInput")
    with ExitStack() as ctx:
        S_ = Sched(nc, ctx)
        block = ctx.enter_context(nc.Block())
        emit_l0ffn(nc, S_, io, nbatch=nbatch, debug_dump=debug_dump)
        S_.on("gpsimd", lambda e_: e_.engine_nop(), wait=[S_.tok("str")])
        S_.finalize(block)
    return nc


def emit_l0ffn(nc, S_, io, pfx="", nbatch=None, debug_dump=()):
    x_hbm = io["x"]
    y_hbm = io["y"]
    gT_h, w1_h, w2_h, w3_h = io["gT"], io["w1T"], io["w2T"], io["w3T"]
    b1_h, b2_h, b3_h = io["b1T"], io["b2T"], io["b3T"]
    sw1_h, sw2_h, sw3_h = io["sw1T"], io["sw2T"], io["sw3T"]

    with ExitStack() as ctx:
        def e(cm):
            return ctx.enter_context(cm)
        AR = 16384 + 24576 + 4096 + 4096 + 2048 + 512 + 1024 + 4096 + 512 + 512
        arena = e(nc.sbuf_tensor(pfx + "arena", [128, AR], BF16))
        off = [0]

        def carve(n):
            o = off[0]
            off[0] += n
            return o
        o_xsF = carve(16384)
        xhatF = arena[:, o_xsF:o_xsF + 16384].rearrange("p (c n) -> p c n", c=NCH)
        o_w = carve(24576)
        wr1 = arena[:, o_w:o_w + 8192].rearrange("p (c n) -> p c n", c=NCH)
        wr3 = arena[:, o_w + 8192:o_w + 16384].rearrange("p (c n) -> p c n", c=NCH)
        wr2 = arena[:, o_w + 16384:o_w + 24576].rearrange("p (c n) -> p c n", c=NCH)
        swh = wr1
        g1 = arena[:, o_w + 8192:o_w + 24576].rearrange("p (c n) -> p c n", c=NCH)
        o_h = carve(4096)
        h_nt = arena[:, o_h:o_h + 4096].rearrange("p (c n) -> p c n", c=NCH)
        o_x3 = carve(4096)
        x3m = arena[:, o_x3:o_x3 + 4096].rearrange("p (c n) -> p c n", c=NCH)
        o_mf = carve(2048)
        maskF = arena[0:16, o_mf:o_mf + 2048]
        o_sel = carve(512)
        selb = arena[0:16, o_sel:o_sel + 512].rearrange(
            "p (s r n) -> p s r n", s=2, r=2)
        o_sq = carve(1024)
        sq_scr = arena[:, o_sq:o_sq + 1024]
        o_xscr = carve(4096)
        xs_scr = arena[:, o_xscr:o_xscr + 4096].rearrange("p (c n) -> p c n",
                                                          c=NCH)
        o_mb = carve(512)
        mb_sb = arena[:, o_mb:o_mb + 512]
        o_sbc = carve(512)
        sbc_sb = arena[:, o_sbc:o_sbc + 512]
        y_sb = e(nc.sbuf_tensor(pfx + "y_sb", [128, NCH, S], F32))
        xres = e(nc.sbuf_tensor(pfx + "xres", [128, 2, D], F32))
        scr32 = e(nc.sbuf_tensor(pfx + "scr32", [128, 3072], F32))
        xhat32 = scr32[:, 0:2048].rearrange("p (r n) -> p r n", r=2)
        xf32 = scr32[:, 2048:3072].rearrange("p (c n) -> p c n", c=NCH)
        outt = scr32[:, 0:2048].rearrange("p (r n) -> p r n", r=2)
        wstage = e(nc.sbuf_tensor(pfx + "wstage", [128, 1024], F32))
        wstgB = scr32[:, 0:1024]
        tmp32 = e(nc.sbuf_tensor(pfx + "tmp32", [128, 512], F32))
        logits = e(nc.sbuf_tensor(pfx + "logits", [128, NT, 8], F32))
        exps = e(nc.sbuf_tensor(pfx + "exps", [128, NT, 8], F32))
        cmb = e(nc.sbuf_tensor(pfx + "cmb", [128, NT, 16], BF16))
        ident_bf = e(nc.sbuf_tensor(pfx + "ident_bf", [128, 128], BF16))
        rst = e(nc.sbuf_tensor(pfx + "rst", [128, 6, NT], F32))
        nmx, den, m1, m2 = rst[:, 0, :], rst[:, 1, :], rst[:, 2, :], rst[:, 3, :]
        w1s, w2s = rst[:, 4, :], rst[:, 5, :]
        l8sb = e(nc.sbuf_tensor(pfx + "l8sb", [8, 128], F32))
        ident_f = e(nc.sbuf_tensor(pfx + "ident_f", [128, 128], F32))
        gT_sb = e(nc.sbuf_tensor(pfx + "gT_sb", [128, NCH, 8], F32))
        b1F = e(nc.sbuf_tensor(pfx + "b1F", [128, NCH, 8], F32))
        b2F = e(nc.sbuf_tensor(pfx + "b2F", [128, NCH, 8], F32))
        b3F = e(nc.sbuf_tensor(pfx + "b3F", [128, NCH, 8], F32))
        ln1 = new_ln(nc, ctx, pfx + "ln1")
        ln2 = ln1
        var_s = e(nc.sbuf_tensor(pfx + "var_s", [128, 1], F32))
        psA = e(nc.psum_tensor(pfx + "psA", [128, 512], F32))
        psB = e(nc.psum_tensor(pfx + "psB", [128, 512], F32))
        psC = e(nc.psum_tensor(pfx + "psC", [128, 512], F32))
        psD = e(nc.psum_tensor(pfx + "psD", [128, 512], F32))
        psE = e(nc.psum_tensor(pfx + "psE", [128, 512], F32))
        psF = e(nc.psum_tensor(pfx + "psF", [128, 512], F32))
        psW = e(nc.psum_tensor(pfx + "psW", [128, 1024], F32))

        t_idf = make_identity(S_, ident_f[:, :])
        t_idb = make_identity(S_, ident_bf[:, :])
        t_g = S_.dma("sync", gT_sb[:, :, :],
                     gT_h[:, :].rearrange("(c p) e -> p c e", p=128))
        t_b1 = S_.dma("sync", b1F[:, :, :],
                      b1_h[:, :].rearrange("(c p) e -> p c e", p=128))
        t_b2 = S_.dma("sync", b2F[:, :, :],
                      b2_h[:, :].rearrange("(c p) e -> p c e", p=128))
        t_b3 = S_.dma("sync", b3F[:, :, :],
                      b3_h[:, :].rearrange("(c p) e -> p c e", p=128))

        for b in range(B_CORE if nbatch is None else nbatch):
            x_b = x_hbm[b]
            y_b = y_hbm[b]
            for t in range(NT):
                rot = t % 2
                t_x = S_.dma("sync", xres[:, rot, :],
                             x_b[t * 128:(t + 1) * 128, :],
                             wait=[S_.g(("xres", rot))])
                t_ln = ln_tile(S_, ln1, t, xres[:, rot, :], sq_scr, var_s[:, :],
                               wait=[t_x])
                t_ap = S_.on("vector",
                             lambda e_, t=t, rot=rot: e_.tensor_scalar(
                                 xhat32[:, rot, :], xres[:, rot, :],
                                 ln1["r"][:, t:t + 1], ln1["nmr"][:, t:t + 1],
                                 ALU.mult, ALU.add),
                             wait=[t_ln, S_.g(("xh", rot)),
                                   S_.g(("outt", rot))])
                S_.setg(("xres", rot), t_ap)

                def ftp(e_, rot=rot):
                    ins = None
                    for c in range(NCH):
                        ins = e_.matmul(psW[:, c * 128:(c + 1) * 128],
                                        xhat32[:, rot, c * 128:(c + 1) * 128],
                                        ident_f[:, :], is_transpose=True,
                                        start=(c % 4 == 0), stop=(c % 4 == 3),
                                        skip_group_check=True)
                    return ins
                t_tp = S_.on("tensor", ftp,
                             wait=[t_ap, t_idf, S_.g("psW")], inc="pe")
                S_.setg(("xh", rot), t_tp)
                t_e1 = S_.on("vector",
                             lambda e_, t=t: e_.tensor_copy(
                                 xhatF[:, :, t * 128:(t + 1) * 128],
                                 psW[:, :].rearrange("p (c n) -> p c n", c=NCH)),
                             wait=[t_tp])
                t_e2 = S_.on("vector",
                             lambda e_: e_.tensor_copy(
                                 xf32[:, :, :],
                                 psW[:, :].rearrange("p (c n) -> p c n", c=NCH)),
                             wait=[S_.g("xf32")])
                S_.setg("psW", t_e2)
                pairs = [(gT_sb[:, kc, :], xf32[:, kc, :]) for kc in range(NCH)]
                t_rm = mm_chain(S_, psA[0:8, 0:128], pairs,
                                wait=[t_e2, t_g, S_.g("psA")])
                t_l8 = S_.on("scalar",
                             lambda e_: e_.copy(l8sb[:, :], psA[0:8, 0:128]),
                             wait=[t_rm], inc="se")
                S_.setg("psA", t_l8)
                t_lt = S_.on("tensor",
                             lambda e_: e_.matmul(psB[0:128, 0:8], l8sb[:, :],
                                                  ident_f[0:8, 0:8], start=True,
                                                  stop=True,
                                                  skip_group_check=True),
                             wait=[t_l8, t_idf, S_.g("psB")], inc="pe")
                t_lg = S_.on("vector",
                             lambda e_, t=t: e_.tensor_copy(
                                 logits[:, t, :], psB[0:128, 0:8]),
                             wait=[t_lt])
                S_.setg("psB", t_lg)
                S_.setg("xf32", t_rm)
            S_.setg("wsB", S_.tok("pe"))
            S_.on("vector", lambda e_: e_.reduce_max(nmx, logits[:, :, :],
                                                     axis=AX.X))
            S_.on("vector", lambda e_: e_.tensor_scalar(nmx, nmx, -1.0, None,
                                                        ALU.mult))
            tv = S_.tok("ve")
            for t in range(NT):
                S_.on("scalar",
                      lambda e_, t=t: e_.activation(
                          exps[:, t, :], logits[:, t, :], AF.Exp,
                          bias=nmx[:, t:t + 1],
                          accum_out=den[:, t:t + 1]),
                      wait=[tv], inc="se")
            tse = S_.tok("se")
            S_.on("vector", lambda e_: e_.reciprocal(den, den), wait=[tse])
            S_.on("vector", lambda e_: e_.reduce_max(m1, exps[:, :, :],
                                                     axis=AX.X))
            for t in range(NT):
                S_.on("vector",
                      lambda e_, t=t: e_.tensor_scalar(
                          cmb[:, t, 0:8], exps[:, t, :], m1[:, t:t + 1], None,
                          ALU.is_ge))
            S_.on("vector",
                  lambda e_: e_.scalar_tensor_tensor(
                      logits[:, :, :], cmb[:, :, 0:8], -1e30, exps[:, :, :],
                      ALU.mult, ALU.add))
            S_.on("vector", lambda e_: e_.reduce_max(m2, logits[:, :, :],
                                                     axis=AX.X))
            for t in range(NT):
                S_.on("vector",
                      lambda e_, t=t: e_.tensor_scalar(
                          cmb[:, t, 8:16], logits[:, t, :], m2[:, t:t + 1],
                          None, ALU.is_ge))
            S_.on("vector", lambda e_: e_.tensor_tensor(w1s, m1, den, ALU.mult))
            S_.on("vector", lambda e_: e_.tensor_tensor(w2s, m2, den, ALU.mult))
            for t in range(NT):
                S_.on("vector",
                      lambda e_, t=t: e_.tensor_scalar(
                          exps[:, t, :], cmb[:, t, 0:8], w1s[:, t:t + 1], None,
                          ALU.mult))
                S_.on("vector",
                      lambda e_, t=t: e_.tensor_tensor(
                          cmb[:, t, 0:8], cmb[:, t, 0:8], cmb[:, t, 8:16],
                          ALU.add))
                S_.on("vector",
                      lambda e_, t=t: e_.scalar_tensor_tensor(
                          cmb[:, t, 8:16], cmb[:, t, 8:16], w2s[:, t:t + 1],
                          exps[:, t, :], ALU.mult, ALU.add))
            t_rr = S_.tok("ve")
            for t in range(NT):
                t_mt = S_.on("tensor",
                             lambda e_, t=t: e_.matmul(
                                 psA[0:16, 0:128], cmb[:, t, :], ident_bf[:, :],
                                 start=True, stop=True, skip_group_check=True),
                             wait=[t_rr, S_.g("psA"), t_idb], inc="pe")
                t_me = S_.on("vector",
                             lambda e_, t=t: e_.tensor_copy(
                                 maskF[:, t * 128:(t + 1) * 128],
                                 psA[0:16, 0:128]),
                             wait=[t_mt])
                S_.setg("psA", t_me)
            t_mask = S_.tok("ve")
            for hh in range(2):
                w1v = sw1_h[:, :].rearrange("(c p) n -> p c n", p=128)[
                    :, :, hh * D:(hh + 1) * D]
                t_w = load_w2(S_, w1v, wstage[:, :], wstgB, swh, NCH,
                              wait=[S_.g("swh"), S_.tok("str")])
                for mcl in range(NCH):
                    for nt in range(4):
                        psu, key = (psC, "psC") if (mcl * 4 + nt) % 2 == 0 \
                            else (psD, "psD")
                        pairs = [(swh[:, kc, mcl * 128:(mcl + 1) * 128],
                                  xhatF[:, kc, nt * 512:(nt + 1) * 512])
                                 for kc in range(NCH)]
                        t_mm = mm_chain(S_, psu[:, :], pairs,
                                        wait=[t_w, S_.g(key)])
                        t_ev = S_.on("scalar",
                                     lambda e_, mcl=mcl, nt=nt, psu=psu:
                                     e_.activation(
                                         g1[:, mcl, nt * 512:(nt + 1) * 512],
                                         psu[:, :], AF.Silu),
                                     wait=[t_mm], inc="se")
                        S_.setg(key, t_ev)
                S_.setg("swh", S_.tok("pe"))
                w2v = sw2_h[:, :].rearrange("(c p) n -> p c n", p=128)[
                    :, :, hh * D:(hh + 1) * D]
                t_w = load_w2(S_, w2v, wstage[:, :], wstgB, swh, NCH,
                              wait=[S_.g("swh")])
                for mcl in range(NCH):
                    for nt in range(4):
                        psu, key = (psC, "psC") if (mcl * 4 + nt) % 2 == 0 \
                            else (psD, "psD")
                        pairs = [(swh[:, kc, mcl * 128:(mcl + 1) * 128],
                                  xhatF[:, kc, nt * 512:(nt + 1) * 512])
                                 for kc in range(NCH)]
                        t_mm = mm_chain(S_, psu[:, :], pairs,
                                        wait=[t_w, S_.g(key)])
                        t_ev = S_.on("vector",
                                     lambda e_, mcl=mcl, nt=nt, psu=psu:
                                     e_.tensor_tensor(
                                         g1[:, mcl, nt * 512:(nt + 1) * 512],
                                         g1[:, mcl, nt * 512:(nt + 1) * 512],
                                         psu[:, :], ALU.mult),
                                     wait=[t_mm])
                        S_.setg(key, t_ev)
                S_.setg("swh", S_.tok("pe"))
                w3v = sw3_h[:, :].rearrange("(c p) n -> p c n", p=128)[
                    :, hh * NCH:(hh + 1) * NCH, :]
                t_w = load_w2(S_, w3v, wstage[:, :], wstgB, swh, NCH,
                              wait=[S_.g("swh")])
                for mc in range(NCH):
                    for nt in range(4):
                        psu, key = (psC, "psC") if (mc * 4 + nt) % 2 == 0 \
                            else (psD, "psD")
                        pairs = [(swh[:, kc, mc * 128:(mc + 1) * 128],
                                  g1[:, kc, nt * 512:(nt + 1) * 512])
                                 for kc in range(NCH)]
                        t_mm = mm_chain(S_, psu[:, :], pairs,
                                        wait=[t_w, S_.g(key)])
                        if hh == 0:
                            t_ev = S_.on("vector",
                                         lambda e_, mc=mc, nt=nt, psu=psu:
                                         e_.tensor_copy(
                                             y_sb[:, mc, nt * 512:(nt + 1) * 512],
                                             psu[:, :]),
                                         wait=[t_mm])
                        else:
                            t_ev = S_.on("vector",
                                         lambda e_, mc=mc, nt=nt, psu=psu:
                                         e_.tensor_tensor(
                                             y_sb[:, mc, nt * 512:(nt + 1) * 512],
                                             y_sb[:, mc, nt * 512:(nt + 1) * 512],
                                             psu[:, :], ALU.add),
                                         wait=[t_mm])
                        S_.setg(key, t_ev)
                S_.setg("swh", S_.tok("pe"))
            t_xhat = S_.tok("ve")

            def rebuild_selb(ei):
                sl = ei % 2
                S_.setg(("selb", sl), S_.tok("pe"))
                t_s0 = S_.on("gpsimd",
                             lambda e_, sl=sl: e_.memset(selb[:, sl, :, :],
                                                         0.0),
                             wait=[S_.g(("selb", sl))])
                S_.on("gpsimd",
                      lambda e_, ei=ei, sl=sl: e_.affine_select(
                          out=selb[:, sl, 0, :], in_=selb[:, sl, 0, :],
                          compare_op=ALU.not_equal, fill=1.0, base=-ei,
                          pattern=[[0, 128]], channel_multiplier=1),
                      wait=[t_s0])
                return S_.on("gpsimd",
                             lambda e_, ei=ei, sl=sl: e_.affine_select(
                                 out=selb[:, sl, 1, :], in_=selb[:, sl, 1, :],
                                 compare_op=ALU.not_equal, fill=1.0,
                                 base=-(8 + ei), pattern=[[0, 128]],
                                 channel_multiplier=1))

            def bl_issue(ei, nt, t_sel):
                sl = ei % 2
                n0 = nt * 512
                t_mb = S_.on("tensor",
                             lambda e_, sl=sl, n0=n0: e_.matmul(
                                 psE[:, :], selb[:, sl, 0, :],
                                 maskF[:, n0:n0 + 512], start=True, stop=True,
                                 skip_group_check=True),
                             wait=[t_mask, t_sel, S_.g("psE")], inc="pe")
                t_mbe = S_.on("scalar",
                              lambda e_: e_.copy(mb_sb, psE[:, :]),
                              wait=[t_mb, S_.g("mb_sb")], inc="se")
                S_.setg("psE", t_mbe)
                t_sb = S_.on("tensor",
                             lambda e_, sl=sl, n0=n0: e_.matmul(
                                 psF[:, :], selb[:, sl, 1, :],
                                 maskF[:, n0:n0 + 512], start=True, stop=True,
                                 skip_group_check=True),
                             wait=[t_mask, t_sel, S_.g("psF")], inc="pe")
                t_sbe = S_.on("scalar",
                              lambda e_: e_.copy(sbc_sb, psF[:, :]),
                              wait=[t_sb, S_.g("sbc_sb")], inc="se")
                S_.setg("psF", t_sbe)
                t_bld = None
                for kc in range(NCH):
                    t_bld = S_.on("vector",
                                  lambda e_, kc=kc, n0=n0: e_.tensor_tensor(
                                      xs_scr[:, kc, :],
                                      xhatF[:, kc, n0:n0 + 512],
                                      sbc_sb, ALU.mult),
                                  wait=[t_sbe, t_xhat, S_.g("xs_scr")])
                S_.setg("sbc_sb", t_bld)
                return t_mbe, t_bld

            t_sel_cur = rebuild_selb(0)
            t_sel_nxt = None
            pend = None
            for ei in range(8):
                t_w1 = load_w2(S_, w1_h[ei].rearrange("(c p) n -> p c n",
                                                      p=128),
                               wstage[:, :], wstgB, wr1, NCH,
                               wait=[S_.g("wr1"), S_.g("swh")])
                t_w3 = load_w2(S_, w3_h[ei].rearrange("(c p) n -> p c n",
                                                      p=128),
                               wstage[:, :], wstgB, wr3, NCH,
                               wait=[S_.g("wr3"), S_.g("swh")])
                t_w2 = load_w2(S_, w2_h[ei].rearrange("(c p) n -> p c n",
                                                      p=128),
                               wstage[:, :], wstgB, wr2, NCH,
                               wait=[S_.g("wr2"), S_.g("swh")])
                if pend is None:
                    pend = bl_issue(ei, 0, t_sel_cur)
                for nt in range(4):
                    n0 = nt * 512
                    t_mbe, t_bld = pend
                    t_hlast = None
                    for mc in range(NCH):
                        psu, key = (psA, "psA") if mc % 2 == 0 else (psB, "psB")
                        pairs = [(wr1[:, kc, mc * 128:(mc + 1) * 128],
                                  xs_scr[:, kc, :])
                                 for kc in range(NCH)]
                        t_mm = mm_chain(S_, psu[:, :], pairs,
                                        wait=[t_w1, t_bld, S_.g(key)])
                        t_hlast = S_.on("scalar",
                                        lambda e_, mc=mc, psu=psu, ei=ei:
                                        e_.activation(
                                            h_nt[:, mc, :], psu[:, :], AF.Silu,
                                            bias=b1F[:, mc, ei:ei + 1]),
                                        wait=[t_mm, t_b1, S_.g("h_nt")],
                                        inc="se")
                        S_.setg(key, t_hlast)
                    if nt == 3:
                        S_.setg("wr1", S_.tok("pe"))
                    for mc in range(NCH):
                        psu, key = (psC, "psC") if mc % 2 == 0 else (psD, "psD")
                        pairs = [(wr3[:, kc, mc * 128:(mc + 1) * 128],
                                  xs_scr[:, kc, :])
                                 for kc in range(NCH)]
                        t_mm = mm_chain(S_, psu[:, :], pairs,
                                        wait=[t_w3, t_bld, S_.g(key)])
                        t_x3m = S_.on("vector",
                                      lambda e_, mc=mc, psu=psu, ei=ei:
                                      e_.scalar_tensor_tensor(
                                          x3m[:, mc, :], psu[:, :],
                                          b3F[:, mc, ei:ei + 1], mb_sb,
                                          ALU.add, ALU.mult),
                                      wait=[t_mm, t_b3, t_mbe,
                                            S_.g("x3m")])
                        S_.setg(key, t_x3m)
                    S_.setg("mb_sb", S_.tok("ve"))
                    S_.setg("xs_scr", S_.tok("pe"))
                    if nt == 3:
                        S_.setg("wr3", S_.tok("pe"))
                    if nt == 0 and ei < 7:
                        t_sel_nxt = rebuild_selb(ei + 1)
                    if nt < 3:
                        pend = bl_issue(ei, nt + 1, t_sel_cur)
                    elif ei < 7:
                        pend = bl_issue(ei + 1, 0, t_sel_nxt)
                        t_sel_cur = t_sel_nxt
                    else:
                        pend = None
                    for mc in range(NCH):
                        psu = psW[:, 0:512] if mc % 2 == 0 else psW[:, 512:1024]
                        key = "psW0" if mc % 2 == 0 else "psW1"
                        pairs = [(wr2[:, kc, mc * 128:(mc + 1) * 128],
                                  h_nt[:, kc, :])
                                 for kc in range(NCH)]
                        t_mm = mm_chain(S_, psu, pairs,
                                        wait=[t_w2, S_.g(key), t_hlast])
                        t_st = S_.on("vector",
                                     lambda e_, mc=mc, psu=psu, ei=ei:
                                     e_.scalar_tensor_tensor(
                                         tmp32[:, :], psu,
                                         b2F[:, mc, ei:ei + 1],
                                         x3m[:, mc, :], ALU.add, ALU.mult),
                                     wait=[t_mm, t_b2])
                        t_ya = S_.on("vector",
                                     lambda e_, mc=mc, n0=n0:
                                     e_.tensor_tensor(
                                         y_sb[:, mc, n0:n0 + 512],
                                         y_sb[:, mc, n0:n0 + 512],
                                         tmp32[:, :], ALU.add))
                        S_.setg(key, t_ya)
                    S_.setg("h_nt", S_.tok("pe"))
                    S_.setg("x3m", S_.tok("ve"))
                S_.setg("wr2", S_.tok("pe"))
            t_y = S_.tok("ve")
            for t in range(NT):
                rot = t % 2

                def ftp2(e_, t=t):
                    ins = None
                    for c in range(NCH):
                        ins = e_.matmul(psW[:, c * 128:(c + 1) * 128],
                                        y_sb[:, c, t * 128:(t + 1) * 128],
                                        ident_f[:, :], is_transpose=True,
                                        start=(c % 4 == 0), stop=(c % 4 == 3),
                                        skip_group_check=True)
                    return ins
                t_tp = S_.on("tensor", ftp2, wait=[t_y, t_idf, S_.g("psW")],
                             inc="pe")
                t_ln = ln_tile(S_, ln2, t, psW[:, :], sq_scr, var_s[:, :],
                               wait=[t_tp], from_psum=True)
                t_x = S_.dma("sync", xres[:, rot, :],
                             x_b[t * 128:(t + 1) * 128, :],
                             wait=[S_.g(("xres", rot))])
                S_.on("vector",
                      lambda e_, t=t: e_.tensor_scalar(
                          outt[:, t % 2, :], psW[:, :],
                          ln2["r"][:, t:t + 1], ln2["nmr"][:, t:t + 1],
                          ALU.mult, ALU.add),
                      wait=[t_ln, S_.g(("outt", rot))])
                t_ap = S_.on("vector",
                             lambda e_, rot=rot: e_.tensor_tensor(
                                 outt[:, rot, :], outt[:, rot, :],
                                 xres[:, rot, :], ALU.add),
                             wait=[t_x])
                S_.setg("psW", t_ap)
                S_.setg(("xres", rot), t_ap)
                t_st = S_.dma("gpsimd", y_b[t * 128:(t + 1) * 128, :],
                              outt[:, rot, :], wait=[t_ap], inc="str")
                S_.setg(("outt", rot), t_st)
            S_.setg("y_sb", S_.tok("ve"))
            S_.setg(("xh", 0), S_.tok("pe"))
            S_.setg(("xh", 1), S_.tok("pe"))

        if debug_dump:
            allw = [S_.tok(s) for s in ("ve", "se", "pe", "gp", "dma")]
            Lx = locals()
            prev = None
            for nm in debug_dump:
                if "." in nm:
                    a_, b_ = nm.split(".")
                    ap = Lx[a_][b_]
                    nm = a_ + "_" + b_
                else:
                    ap = Lx[nm]
                if not isinstance(ap, bass.AP):
                    ap = ap[tuple(slice(None) for _ in ap.shape)]
                if len(ap.shape) > 2:
                    dims = " ".join(f"a{i}" for i in range(len(ap.shape) - 1))
                    ap = ap.rearrange(f"p {dims} -> p ({dims})")
                ncols = ap.shape[1]
                nparts = ap.shape[0]
                dbg = nc.dram_tensor(pfx + "dbg_" + nm, [nparts, ncols], F32,
                                     kind="ExternalOutput")
                for c0 in range(0, ncols, 2048):
                    cw = min(2048, ncols - c0)
                    t_c = S_.on("vector",
                                lambda e_, ap=ap, c0=c0, cw=cw, nparts=nparts:
                                e_.tensor_copy(tmp32[0:nparts, 0:min(cw, 512)]
                                               if cw <= 512 else
                                               y_sb[0:nparts, 0, 0:cw],
                                               ap[:, c0:c0 + cw]),
                                wait=allw + ([prev] if prev else []))
                    prev = S_.dma("gpsimd", dbg[:, c0:c0 + cw],
                                  tmp32[0:nparts, 0:min(cw, 512)]
                                  if cw <= 512 else y_sb[0:nparts, 0, 0:cw],
                                  wait=[t_c], inc="str")


NPOS = 256
NSTEP = 16
NT4 = 32


FFN1_WN = ["wfT", "ufT", "wiT", "uiT", "wcT", "ucT", "woT", "uoT"]
FFN1_BN = ["bf", "bi", "bc", "bo", "in_c"]


def build_l1ffn(sim_debug=False, nstep=None, debug_dump=()):
    nc = bass.Bass("TRN2", target_bir_lowering=False,
                   detect_race_conditions=not sim_debug)
    io = {"x": nc.dram_tensor("x", [NSTEP, NPOS, D], F32, kind="ExternalInput"),
          "y": nc.dram_tensor("y", [NSTEP, NPOS, D], F32, kind="ExternalOutput")}
    for k in FFN1_WN:
        io[k] = nc.dram_tensor(k, [D, D], F32, kind="ExternalInput")
    for k in FFN1_BN:
        io[k] = nc.dram_tensor(k, [D], F32, kind="ExternalInput")
    with ExitStack() as ctx:
        S_ = Sched(nc, ctx)
        block = ctx.enter_context(nc.Block())
        emit_l1ffn(nc, S_, io, nstep=nstep, debug_dump=debug_dump)
        S_.on("gpsimd", lambda e_: e_.engine_nop(), wait=[S_.tok("str")])
        S_.finalize(block)
    return nc


def emit_l1ffn(nc, S_, io, pfx="", nstep=None, debug_dump=()):
    x_hbm = io["x"]
    y_hbm = io["y"]
    Wh = io
    Bh = io

    with ExitStack() as ctx:
        def e(cm):
            return ctx.enter_context(cm)
        wreg = e(nc.sbuf_tensor(pfx + "wreg", [128, 6, NCH, D], BF16))
        wv = {k: wreg[:, i, :, :] for i, k in
              enumerate(["wfT", "ufT", "wiT", "uiT", "wcT", "ucT"])}
        wv["woT"] = wv["wfT"]
        wv["uoT"] = wv["ufT"]
        wstage = e(nc.sbuf_tensor(pfx + "wstage", [128, 1024], F32))
        xres = e(nc.sbuf_tensor(pfx + "xres", [128, 2, D], F32))
        xhat_tok = e(nc.sbuf_tensor(pfx + "xhat_tok", [128, 2, D], BF16))
        xFs = e(nc.sbuf_tensor(pfx + "xFs", [128, 2, NCH, NPOS], BF16))
        c_sb = e(nc.sbuf_tensor(pfx + "c_sb", [128, NCH, NPOS], F32))
        cbf = e(nc.sbuf_tensor(pfx + "cbf", [128, NCH, NPOS], BF16))
        gf = e(nc.sbuf_tensor(pfx + "gf", [128, NCH, NPOS], F32))
        gi = e(nc.sbuf_tensor(pfx + "gi", [128, NCH, NPOS], F32))
        gc = e(nc.sbuf_tensor(pfx + "gc", [128, NCH, NPOS], F32))
        cuo = gf
        hF = gi
        ones_t = e(nc.sbuf_tensor(pfx + "ones_t", [128, NPOS], F32))
        biasF = e(nc.sbuf_tensor(pfx + "biasF", [128, 5, NCH], F32))
        bvw = {k: biasF[:, i, :] for i, k in
               enumerate(["bf", "bi", "bc", "bo", "in_c"])}
        ln1 = new_ln(nc, ctx, pfx + "ln1", ntiles=NT4)
        ln2 = new_ln(nc, ctx, pfx + "ln2", ntiles=NT4)
        var_s = e(nc.sbuf_tensor(pfx + "var_s", [128, 1], F32))
        sq_scr = e(nc.sbuf_tensor(pfx + "sq_scr", [128, D], BF16))
        ident_f = e(nc.sbuf_tensor(pfx + "ident_f", [128, 128], F32))
        ident_b = e(nc.sbuf_tensor(pfx + "ident_b", [128, 128], BF16))
        outt = e(nc.sbuf_tensor(pfx + "outt", [128, 2, D], F32))
        psA = e(nc.psum_tensor(pfx + "psA", [128, 512], F32))
        psB = e(nc.psum_tensor(pfx + "psB", [128, 512], F32))
        psW = e(nc.psum_tensor(pfx + "psW", [128, 1024], F32))
        psT = e(nc.psum_tensor(pfx + "psT", [128, 1024], BF16))

        t_idf = make_identity(S_, ident_f[:, :])
        t_idb = make_identity(S_, ident_b[:, :])
        t_one = S_.on("gpsimd", lambda e_: e_.memset(ones_t[:, :], 1.0))
        for k in ["bf", "bi", "bc", "bo", "in_c"]:
            S_.dma("sync", bvw[k],
                   Bh[k][:].rearrange("(c p) -> p c", p=128),
                   allow_slow_non_contiguous=True)
        t_bias = S_.tok("dma")
        for k in ["wfT", "ufT", "wiT", "uiT", "wcT", "ucT"]:
            load_w(S_, Wh[k][:, :].rearrange("(c p) n -> p c n", p=128),
                   wstage, wv[k], NCH, D)
        t_wload = S_.tok("ve")
        for ch in range(NCH):
            S_.on("vector",
                  lambda e_, ch=ch: e_.tensor_scalar(
                      c_sb[:, ch, :], ones_t[:, 0:NPOS],
                      bvw["in_c"][:, ch:ch + 1], None, ALU.mult),
                  wait=[t_one, t_bias])
        t_c0 = S_.on("vector", lambda e_: e_.tensor_copy(
            cbf[:, :, :], c_sb[:, :, :]))

        def prep_xhat(ts, ln, rot):
            toks = []
            for half in range(2):
                tile = ts * 2 + half
                hrot = half
                t_x = S_.dma("sync", xres[:, hrot, :],
                             x_hbm[ts][half * 128:(half + 1) * 128, :],
                             wait=[S_.g(("xres", hrot))])
                if ln is ln1:
                    t_ln = ln_tile(S_, ln, tile, xres[:, hrot, :], sq_scr,
                                   var_s[:, :], wait=[t_x])
                else:
                    t_ln = t_x
                t_ap = S_.on("vector",
                             lambda e_, tile=tile, hrot=hrot: e_.tensor_scalar(
                                 xhat_tok[:, hrot, :], xres[:, hrot, :],
                                 ln1["r"][:, tile:tile + 1],
                                 ln1["nmr"][:, tile:tile + 1],
                                 ALU.mult, ALU.add),
                             wait=[t_ln, S_.g(("xht", hrot))])
                S_.setg(("xres", hrot), t_ap)

                def ftp(e_, hrot=hrot):
                    ins = None
                    for cc in range(NCH):
                        ins = e_.matmul(psT[:, cc * 128:(cc + 1) * 128],
                                        xhat_tok[:, hrot, cc * 128:(cc + 1) * 128],
                                        ident_b[:, :], is_transpose=True,
                                        start=(cc == 0), stop=(cc == NCH - 1),
                                        skip_group_check=True)
                    return ins
                t_tp = S_.on("tensor", ftp,
                             wait=[t_ap, t_idb, S_.g("psT")], inc="pe")
                S_.setg(("xht", hrot), t_tp)
                t_ev = S_.on("vector",
                             lambda e_, rot=rot, half=half: e_.tensor_copy(
                                 xFs[:, rot, :, half * 128:(half + 1) * 128],
                                 psT[:, :].rearrange("p (c n) -> p c n",
                                                     c=NCH)),
                             wait=[t_tp])
                S_.setg("psT", t_ev)
                toks.append(t_ev)
            return toks[-1]

        GATES = [("wfT", "ufT", "bf", AF.Sigmoid, gf),
                 ("wiT", "uiT", "bi", AF.Sigmoid, gi),
                 ("wcT", "ucT", "bc", AF.Tanh, gc)]
        t_cup = t_c0
        for ts in range(NSTEP if nstep is None else nstep):
            rot = ts % 2
            t_xf = prep_xhat(ts, ln1, rot)
            for (wk, uk, bk, act, dst) in GATES:
                for mc in range(NCH):
                    psu, key = (psA, "psA") if mc % 2 == 0 else (psB, "psB")
                    pairs = ([(wv[wk][:, kc, mc * 128:(mc + 1) * 128],
                               xFs[:, rot, kc, :]) for kc in range(NCH)] +
                             [(wv[uk][:, kc, mc * 128:(mc + 1) * 128],
                               cbf[:, kc, :]) for kc in range(NCH)])
                    t_mm = mm_chain(S_, psu[:, 0:NPOS], pairs,
                                    wait=[t_wload, t_xf, t_cup, S_.g(key)])
                    t_g = S_.on("scalar",
                                lambda e_, mc=mc, psu=psu, dst=dst, act=act,
                                bk=bk: e_.activation(
                                    dst[:, mc, :], psu[:, 0:NPOS], act,
                                    bias=bvw[bk][:, mc:mc + 1]),
                                wait=[t_mm, t_bias], inc="se")
                    S_.setg(key, t_g)
            t_gates = S_.tok("se")
            S_.on("vector", lambda e_: e_.tensor_tensor(
                gf[:, :, :], gf[:, :, :], c_sb[:, :, :], ALU.mult),
                wait=[t_gates])
            S_.on("vector", lambda e_: e_.tensor_tensor(
                gi[:, :, :], gi[:, :, :], gc[:, :, :], ALU.mult))
            S_.on("vector", lambda e_: e_.tensor_tensor(
                c_sb[:, :, :], gf[:, :, :], gi[:, :, :], ALU.add))
            t_cup = S_.on("vector", lambda e_: e_.tensor_copy(
                cbf[:, :, :], c_sb[:, :, :]))

        t_pe_done = S_.tok("pe")
        load_w(S_, Wh["woT"][:, :].rearrange("(c p) n -> p c n", p=128),
               wstage, wv["woT"], NCH, D, wait=[t_pe_done])
        load_w(S_, Wh["uoT"][:, :].rearrange("(c p) n -> p c n", p=128),
               wstage, wv["uoT"], NCH, D, wait=[t_pe_done])
        t_wo = S_.tok("ve")
        for mc in range(NCH):
            psu, key = (psA, "psA") if mc % 2 == 0 else (psB, "psB")
            pairs = [(wv["uoT"][:, kc, mc * 128:(mc + 1) * 128],
                      cbf[:, kc, :]) for kc in range(NCH)]
            t_mm = mm_chain(S_, psu[:, 0:NPOS], pairs,
                            wait=[t_wo, t_cup, S_.g(key)])
            t_cu = S_.on("vector",
                         lambda e_, mc=mc, psu=psu: e_.tensor_copy(
                             cuo[:, mc, :], psu[:, 0:NPOS]),
                         wait=[t_mm])
            S_.setg(key, t_cu)
        t_cuo = S_.tok("ve")
        for ts in range(NSTEP if nstep is None else nstep):
            rot = ts % 2
            t_xf = prep_xhat(ts, ln2, rot)
            for mc in range(NCH):
                psu, key = (psA, "psA") if mc % 2 == 0 else (psB, "psB")
                pairs = [(wv["woT"][:, kc, mc * 128:(mc + 1) * 128],
                          xFs[:, rot, kc, :]) for kc in range(NCH)]
                t_mm = mm_chain(S_, psu[:, 0:NPOS], pairs,
                                wait=[t_wo, t_xf, t_cuo, S_.g(key)])
                t_add = S_.on("vector",
                              lambda e_, mc=mc, psu=psu: e_.tensor_tensor(
                                  psu[:, 0:NPOS], psu[:, 0:NPOS],
                                  cuo[:, mc, :], ALU.add),
                              wait=[t_mm])
                t_h = S_.on("scalar",
                            lambda e_, mc=mc, psu=psu: e_.activation(
                                hF[:, mc, :], psu[:, 0:NPOS], AF.Tanh,
                                bias=bvw["bo"][:, mc:mc + 1]),
                            wait=[t_bias, t_add, S_.g("hF")], inc="se")
                S_.setg(key, t_h)
            t_hd = S_.tok("se")
            for half in range(2):
                tile = ts * 2 + half
                hrot = half

                def ftp2(e_, half=half):
                    ins = None
                    for cc in range(NCH):
                        ins = e_.matmul(psW[:, cc * 128:(cc + 1) * 128],
                                        hF[:, cc, half * 128:(half + 1) * 128],
                                        ident_f[:, :], is_transpose=True,
                                        start=(cc % 4 == 0),
                                        stop=(cc % 4 == 3),
                                        skip_group_check=True)
                    return ins
                t_tp = S_.on("tensor", ftp2,
                             wait=[t_hd, t_idf, S_.g("psW")], inc="pe")
                t_ln = ln_tile(S_, ln2, tile, psW[:, :], sq_scr, var_s[:, :],
                               wait=[t_tp], from_psum=True)
                t_x = S_.dma("sync", xres[:, hrot, :],
                             x_hbm[ts][half * 128:(half + 1) * 128, :],
                             wait=[S_.g(("xres", hrot))])
                S_.on("vector",
                      lambda e_, tile=tile, hrot=hrot: e_.tensor_scalar(
                          outt[:, hrot, :], psW[:, :],
                          ln2["r"][:, tile:tile + 1],
                          ln2["nmr"][:, tile:tile + 1],
                          ALU.mult, ALU.add),
                      wait=[t_ln, S_.g(("outt", hrot))])
                t_ap = S_.on("vector",
                             lambda e_, hrot=hrot: e_.tensor_tensor(
                                 outt[:, hrot, :], outt[:, hrot, :],
                                 xres[:, hrot, :], ALU.add),
                             wait=[t_x])
                S_.setg("psW", t_ap)
                S_.setg(("xres", hrot), t_ap)
                t_st = S_.dma("gpsimd", y_hbm[ts][half * 128:(half + 1) * 128, :],
                              outt[:, hrot, :], wait=[t_ap], inc="str")
                S_.setg(("outt", hrot), t_st)
            S_.setg("hF", S_.tok("pe"))


def build_fused(sim_debug=False):
    nc = bass.Bass("TRN2", target_bir_lowering=False,
                   detect_race_conditions=not sim_debug)
    io1 = {"x": nc.dram_tensor("x", [B_CORE, S, D], F32, kind="ExternalInput")}
    for k, sh in ATTN_W[True]:
        io1[k] = nc.dram_tensor("a1" + k, sh, F32, kind="ExternalInput")
    io2 = {}
    for k, sh in FFN0_W:
        io2[k] = nc.dram_tensor("f2" + k, sh, F32, kind="ExternalInput")
    io3 = {}
    for k, sh in ATTN_W[False]:
        io3[k] = nc.dram_tensor("a3" + k, sh, F32, kind="ExternalInput")
    io4 = {"y": nc.dram_tensor("y", [NSTEP, NPOS, D], F32,
                               kind="ExternalOutput")}
    for k in FFN1_WN:
        io4[k] = nc.dram_tensor("f4" + k, [D, D], F32, kind="ExternalInput")
    for k in FFN1_BN:
        io4[k] = nc.dram_tensor("f4" + k, [D], F32, kind="ExternalInput")
    x1_i = nc.dram_tensor("x1_i", [B_CORE, S, D], F32)
    x2_i = nc.dram_tensor("x2_i", [B_CORE, S, D], F32)
    cc_in = nc.dram_tensor("cc_in", [8, B_CORE * NPOS * D], F32)
    cc_out = nc.dram_tensor("cc_out", [8, B_CORE * NPOS * D], F32)
    io1["y"] = x1_i
    io2["x"] = x1_i
    io2["y"] = x2_i
    io3["x"] = x2_i
    io3["y"] = cc_in
    io4["x"] = cc_out.ap().rearrange("s (b p n) -> (s b) p n",
                                     b=B_CORE, p=NPOS)

    from concourse.aot_env import allow_ici_all_to_all
    with ExitStack() as ctx, allow_ici_all_to_all():
        S_ = Sched(nc, ctx)
        block = ctx.enter_context(nc.Block())
        S_.set_prefix("a1_")
        emit_attn(nc, S_, io1, mla=True, pfx="a1_")
        S_.barrier("f2_")
        emit_l0ffn(nc, S_, io2, pfx="f2_")
        S_.barrier("a3_")
        bnc = cc_in.ap().rearrange("j (b p n) -> j b p n", b=B_CORE, p=NPOS)

        def yv(bb, tt):
            return bnc[tt // 2, bb, (tt % 2) * 128:(tt % 2) * 128 + 128, :]
        emit_attn(nc, S_, io3, mla=False, pfx="a3_", y_view=yv)
        S_.barrier("cc_")
        S_.on("gpsimd",
              lambda e_: e_.collective_compute(
                  "AllToAll", ALU.bypass,
                  replica_groups=[list(range(8))],
                  ins=[cc_in.ap().opt()], outs=[cc_out.ap().opt()]),
              inc="cc", is_dma=True)
        S_.barrier("f4_")
        emit_l1ffn(nc, S_, io4, pfx="f4_")
        S_.barrier("zz_")
        S_.finalize(block)
    return nc


def kernel(**inp):
    from concourse.aot_env import allow_ici_all_to_all
    ncores = 8
    f = lambda a: np.ascontiguousarray(np.asarray(a, dtype=np.float32))
    tr = lambda a: np.ascontiguousarray(np.asarray(a, dtype=np.float32).T)
    x = f(inp["L0_attn_x"])
    w = {"a1wq": f(inp["L0_attn_W_q"]), "a1wk": f(inp["L0_attn_W_k"]),
         "a1wv": f(inp["L0_attn_W_v"]), "a1wo": f(inp["L0_attn_W_o"]),
         "a1wdn": f(inp["L0_attn_W_down_kv"]),
         "a1wuk": f(inp["L0_attn_W_up_k"]), "a1wuv": f(inp["L0_attn_W_up_v"]),
         "f2gT": tr(inp["L0_ffn_gate_w"]),
         "f2w1T": np.ascontiguousarray(
             np.asarray(inp["L0_ffn_w1"], np.float32).transpose(0, 2, 1)),
         "f2w2T": np.ascontiguousarray(
             np.asarray(inp["L0_ffn_w2"], np.float32).transpose(0, 2, 1)),
         "f2w3T": np.ascontiguousarray(
             np.asarray(inp["L0_ffn_w3"], np.float32).transpose(0, 2, 1)),
         "f2b1T": tr(inp["L0_ffn_b1"]), "f2b2T": tr(inp["L0_ffn_b2"]),
         "f2b3T": tr(inp["L0_ffn_b3"]),
         "f2sw1T": tr(inp["L0_ffn_sw1"]), "f2sw2T": tr(inp["L0_ffn_sw2"]),
         "f2sw3T": tr(inp["L0_ffn_sw3"]),
         "a3wqkv": tr(inp["L1_attn_W_qkv"]), "a3wo": tr(inp["L1_attn_W_o"]),
         "f4wfT": tr(inp["L1_ffn_wf"]), "f4ufT": tr(inp["L1_ffn_uf"]),
         "f4wiT": tr(inp["L1_ffn_wi"]), "f4uiT": tr(inp["L1_ffn_ui"]),
         "f4wcT": tr(inp["L1_ffn_wc"]), "f4ucT": tr(inp["L1_ffn_uc"]),
         "f4woT": tr(inp["L1_ffn_wo"]), "f4uoT": tr(inp["L1_ffn_uo"]),
         "f4bf": f(inp["L1_ffn_bf"]), "f4bi": f(inp["L1_ffn_bi"]),
         "f4bc": f(inp["L1_ffn_bc"]), "f4bo": f(inp["L1_ffn_bo"]),
         "f4in_c": f(inp["L1_ffn_in_c"])}
    nc = build_fused()
    in_maps = [{"x": x[2 * c:2 * c + 2], **w} for c in range(ncores)]
    with allow_ici_all_to_all():
        res = run_bass_kernel_spmd(nc, in_maps, core_ids=list(range(ncores)))
    out = np.concatenate([res.results[c]["y"] for c in range(ncores)], axis=1)
    return out


# version 11
# speedup vs baseline: 1.0354x; 1.0045x over previous
import math
from contextlib import ExitStack

import numpy as np

import concourse.bass as bass
import concourse.mybir as mybir
from concourse.bass_utils import run_bass_kernel_spmd

F32 = mybir.dt.float32
BF16 = mybir.dt.bfloat16
AF = mybir.ActivationFunctionType
ALU = mybir.AluOpType
AX = mybir.AxisListType

D = 1024
S = 2048
B_CORE = 2
NCH = 8
NT = S // 128
H = 16
HD = 64
DL = 256
LAM = 0.01
EPS = 1e-5


class Sched:
    ENGINES = ("sync", "tensor", "vector", "scalar", "gpsimd")

    LOCAL_SEMS = ("ve", "se", "gp", "pe", "dma", "dmb", "str")

    def __init__(self, nc, ctx):
        self.nc = nc
        self.ctx = ctx
        self.q = {e: [] for e in self.ENGINES}
        self._sems = {}
        self.guard = {}
        self.pfx = ""

    def set_prefix(self, p):
        self.pfx = p

    def _resolve(self, name):
        return (self.pfx + name) if name in self.LOCAL_SEMS else name

    def _sem(self, name):
        if name not in self._sems:
            self._sems[name] = [self.ctx.enter_context(self.nc.semaphore(name)), 0]
        return self._sems[name]

    def barrier(self, new_pfx=None):
        if new_pfx is not None:
            self.pfx = new_pfx
        toks = [(nm, ent[1]) for nm, ent in self._sems.items()]
        toks = [t for t in toks if t[1] > 0]
        for eng in self.ENGINES:
            if eng in ("sync", "tensor"):
                self.on(eng, lambda e: e.nop(), wait=toks,
                        inc=("pe" if eng == "tensor" else None), is_dma=True)
            else:
                self.on(eng,
                        lambda e: (e.engine_nop() if hasattr(e, "engine_nop")
                                   else e.nop()),
                        wait=toks)
        self.guard.clear()

    CHAINED = {"vector": "ve", "scalar": "se", "gpsimd": "gp"}

    def on(self, eng, fn, wait=(), inc=None, inc_by=1, is_dma=False):
        ch = self.CHAINED.get(eng)
        wait = list(wait)
        if inc is not None:
            inc = self._resolve(inc)
        if ch is not None and not is_dma:
            ch = self._resolve(ch)
            wait.append(self.tok(ch))
            if inc is None:
                inc, inc_by = ch, 1
            elif inc != ch:
                raise ValueError(f"{eng} ops must inc {ch}")
        tok = None
        if inc is not None:
            entry = self._sem(inc)
            entry[1] += inc_by
            tok = (inc, entry[1])
        wmax = {}
        for w in wait:
            if w is None:
                continue
            s, v = w
            if v > wmax.get(s, -1):
                wmax[s] = v
        whandles = [(self._sems[s][0], v) for s, v in wmax.items()]
        ihandle = self._sems[inc][0] if inc is not None else None

        def thunk(e, fn=fn, whandles=whandles, ihandle=ihandle, inc_by=inc_by):
            for (h, v) in whandles:
                e.wait_ge(h, v)
            ins = fn(e)
            if ihandle is not None:
                ins.then_inc(ihandle, inc_by)
        self.q[eng].append(thunk)
        return tok

    def dma(self, eng, out, in_, wait=(), inc="dma", **kw):
        return self.on(eng, lambda e: e.dma_start(out=out, in_=in_, **kw),
                       wait=wait, inc=inc, inc_by=16, is_dma=True)

    def tok(self, name):
        name = self._resolve(name)
        entry = self._sem(name)
        return (name, entry[1])

    def g(self, key):
        return self.guard.get(key)

    def setg(self, key, tok):
        self.guard[key] = tok

    def finalize(self, block):
        for name in self.ENGINES:
            thunks = self.q[name]
            if not thunks:
                continue

            def runner(e, ts=tuple(thunks)):
                for t in ts:
                    t(e)
            getattr(block, name)(runner)


def mm_chain(S_, psum_ap, pairs, wait=(), inc="pe", start=True, stop=True):
    n = len(pairs)

    def fn(e):
        ins = None
        for i, (l, r) in enumerate(pairs):
            ins = e.matmul(psum_ap, l, r,
                           start=(start and i == 0),
                           stop=(stop and i == n - 1),
                           skip_group_check=True)
        return ins
    return S_.on("tensor", fn, wait=wait, inc=inc)


def make_identity(S_, ident_ap):
    t0 = S_.on("gpsimd", lambda e: e.memset(ident_ap, 0.0), inc="gp")
    return S_.on("gpsimd",
                 lambda e: e.affine_select(out=ident_ap, in_=ident_ap,
                                           compare_op=ALU.not_equal, fill=1.0,
                                           base=0, pattern=[[-1, 128]],
                                           channel_multiplier=1),
                 wait=[t0], inc="gp")


def load_w_half(S_, w_dram, wstage, wbf, chunks, dout, wait=(), into_off=0):
    rounds = [(i, min(2, chunks - i)) for i in range(0, chunks, 2)]
    wv = w_dram[:, :].rearrange("(c p) n -> p c n", p=128)
    t_c = None
    for (c0, cs) in rounds:
        t_d = S_.dma("sync", wstage[:, 0:cs, 0:dout], wv[:, c0:c0 + cs, :],
                     wait=list(wait) + [S_.g("wstage")])
        t_c = S_.on("vector",
                    lambda e, c0=c0, cs=cs: e.tensor_copy(
                        wbf[:, into_off + c0:into_off + c0 + cs, 0:dout],
                        wstage[:, 0:cs, 0:dout]),
                    wait=[t_d], inc="ve")
        S_.setg("wstage", t_c)
    return t_c


def ln_tile(S_, st, t, x_ap, sq_scr, var_col, wait, from_psum=False):
    if not isinstance(sq_scr, bass.AP):
        sq_scr = sq_scr[:, :]
    if from_psum:
        t1 = S_.on("scalar",
                   lambda e: e.activation(sq_scr, x_ap, AF.Copy,
                                          accum_out=st["sum"][:, t:t + 1]),
                   wait=wait, inc="se")
    else:
        t1 = S_.on("vector",
                   lambda e: e.reduce_sum(st["sum"][:, t:t + 1], x_ap,
                                          axis=AX.X),
                   wait=wait, inc="ve")
    t2 = S_.on("scalar",
               lambda e: e.activation(sq_scr, x_ap, AF.Square,
                                      accum_out=st["sumq"][:, t:t + 1]),
               wait=wait, inc="se")

    S_.on("vector",
          lambda e: e.tensor_scalar(st["negm"][:, t:t + 1], st["sum"][:, t:t + 1],
                                    -1.0 / D, None, ALU.mult),
          wait=[t1])
    S_.on("vector",
          lambda e: e.tensor_scalar(var_col, st["negm"][:, t:t + 1],
                                    st["negm"][:, t:t + 1], -EPS,
                                    ALU.mult, ALU.add))
    t3 = S_.on("vector",
               lambda e: e.scalar_tensor_tensor(var_col, st["sumq"][:, t:t + 1],
                                                1.0 / D, var_col, ALU.mult,
                                                ALU.subtract),
               wait=[t2])
    t4 = S_.on("scalar",
               lambda e: e.activation(var_col, var_col, AF.Sqrt),
               wait=[t3], inc="se")
    S_.on("vector", lambda e: e.reciprocal(st["r"][:, t:t + 1], var_col),
          wait=[t4])
    return S_.on("vector",
                 lambda e: e.tensor_tensor(st["nmr"][:, t:t + 1],
                                           st["negm"][:, t:t + 1],
                                           st["r"][:, t:t + 1], ALU.mult))


def new_ln(nc, ctx, name, ntiles=NT):
    sb = lambda n, sh: ctx.enter_context(nc.sbuf_tensor(name + n, sh, F32))
    return {"sum": sb("_s", [128, ntiles]), "sumq": sb("_q", [128, ntiles]),
            "negm": sb("_m", [128, ntiles]), "r": sb("_r", [128, ntiles]),
            "nmr": sb("_n", [128, ntiles])}


ATTN_W = {True: [("wq", [D, D]), ("wk", [D, D]), ("wv", [D, D]),
                 ("wo", [D, D]), ("wdn", [D, DL]),
                 ("wuk", [DL, D]), ("wuv", [DL, D])],
          False: [("wqkv", [D, D]), ("wo", [D, D])]}


def build_attn(mla, sim_debug=False, phases=5, nbatch=None, debug_dump=()):
    nc = bass.Bass("TRN2", target_bir_lowering=False,
                   detect_race_conditions=not sim_debug)
    io = {"x": nc.dram_tensor("x", [B_CORE, S, D], F32, kind="ExternalInput"),
          "y": nc.dram_tensor("y", [B_CORE, S, D], F32, kind="ExternalOutput")}
    for k, sh in ATTN_W[mla]:
        io[k] = nc.dram_tensor(k, sh, F32, kind="ExternalInput")
    with ExitStack() as ctx:
        S_ = Sched(nc, ctx)
        block = ctx.enter_context(nc.Block())
        emit_attn(nc, S_, io, mla, phases=phases, nbatch=nbatch,
                  debug_dump=debug_dump)
        S_.on("gpsimd", lambda e_: e_.engine_nop(), wait=[S_.tok("str")])
        S_.finalize(block)
    return nc


def emit_attn(nc, S_, io, mla, pfx="", phases=5, nbatch=None, debug_dump=(),
              y_view=None):
    x_hbm = io["x"]
    y_hbm = io["y"]
    W = io

    with ExitStack() as ctx:
        def e(cm):
            return ctx.enter_context(cm)
        xres = e(nc.sbuf_tensor(pfx + "xres", [128, 2, D], F32))
        sq_scr = e(nc.sbuf_tensor(pfx + "sq_scr", [128, D], BF16))
        xhat_tok = e(nc.sbuf_tensor(pfx + "xhat_tok", [128, 2, D], BF16))
        xhatF = e(nc.sbuf_tensor(pfx + "xhatF", [128, NCH, S], BF16))
        qF = e(nc.sbuf_tensor(pfx + "qF", [128, NCH, S], BF16))
        poolA = e(nc.sbuf_tensor(pfx + "poolA", [128, 20736], BF16))
        latF = poolA[:, 0:4096].rearrange("p (c n) -> p c n", c=2)
        vTo = poolA[:, 4096:20736].rearrange("p (t h c) -> p t h c", t=NT, h=H)
        poolB = e(nc.sbuf_tensor(pfx + "poolB", [128, NCH, S], BF16))
        kvF = poolB
        OF = poolB
        wstage = e(nc.sbuf_tensor(pfx + "wstage", [128, 2, D], F32))
        wbf = e(nc.sbuf_tensor(pfx + "wbf", [128, NCH, D], BF16))
        ident_bf = e(nc.sbuf_tensor(pfx + "ident_bf", [128, 128], BF16))
        ident_f = e(nc.sbuf_tensor(pfx + "ident_f", [128, 128], F32))
        ones1 = e(nc.sbuf_tensor(pfx + "ones1", [1, 64], F32))
        ln1 = new_ln(nc, ctx, pfx + "ln1")
        ln2 = new_ln(nc, ctx, pfx + "ln2")
        var_s = e(nc.sbuf_tensor(pfx + "var_s", [128, 1], F32))
        decay = e(nc.sbuf_tensor(pfx + "decay", [128, 3968], F32)) if mla else None
        expS = e(nc.sbuf_tensor(pfx + "expS", [128, 2, 1024], BF16))
        rbc = e(nc.sbuf_tensor(pfx + "rbc", [64, 1024], F32))
        aos = e(nc.sbuf_tensor(pfx + "aos", [128, 2, D], F32))
        outt = aos
        psT = e(nc.psum_tensor(pfx + "psT", [128, 1024], BF16))
        psA = e(nc.psum_tensor(pfx + "psA", [128, 512], F32))
        psB = e(nc.psum_tensor(pfx + "psB", [128, 512], F32))
        psS = e(nc.psum_tensor(pfx + "psS", [128, 1024], F32))
        psAVe = e(nc.psum_tensor(pfx + "psAVe", [65, 512], F32))
        psAVo = e(nc.psum_tensor(pfx + "psAVo", [65, 512], F32))
        psBC = e(nc.psum_tensor(pfx + "psBC", [64, 512], F32))

        t_id = make_identity(S_, ident_bf[:, :])
        t_idf = make_identity(S_, ident_f[:, :])
        t_on = S_.on("gpsimd", lambda e_: e_.memset(ones1[:, :], 1.0), inc="gp")
        t_dec = None
        if mla:
            t_i = S_.on("gpsimd",
                        lambda e_: e_.iota(decay[:, :], [[1, 3968]], base=-1920,
                                           channel_multiplier=-1,
                                           allow_small_or_imprecise_dtypes=True),
                        inc="gp")
            t_ab = S_.on("scalar",
                         lambda e_: e_.activation(decay[:, :], decay[:, :], AF.Abs),
                         wait=[t_i], inc="se")
            t_dec = S_.on("scalar",
                          lambda e_: e_.activation(decay[:, :], decay[:, :],
                                                   AF.Exp, scale=-LAM),
                          wait=[t_ab], inc="se")

        for b in range(B_CORE if nbatch is None else nbatch):
            emit_attn_batch(S_, mla, b, x_hbm, y_hbm, W, locals(), phases,
                            y_view)

        if phases < 4.8 or (nbatch is not None and nbatch < B_CORE):
            t_z = S_.on("vector",
                        lambda e_: e_.tensor_copy(aos[:, 0, :], xres[:, 0, :]),
                        wait=[S_.tok("ve"), S_.tok("se"), S_.tok("pe"),
                              S_.tok("gp")], inc="ve")
            S_.dma("gpsimd", y_hbm[0][0:128, :], aos[:, 0, :], wait=[t_z],
                   inc="str")
        if debug_dump:
            allw = [S_.tok(s) for s in ("ve", "se", "pe", "gp", "dma")]
            Lx = locals()
            prev = None
            for nm in debug_dump:
                if "." in nm:
                    a_, b_ = nm.split(".")
                    ap = Lx[a_][b_]
                    nm = a_ + "_" + b_
                else:
                    ap = Lx[nm]
                if not isinstance(ap, bass.AP):
                    ap = ap[tuple(slice(None) for _ in ap.shape)]
                if len(ap.shape) > 2:
                    dims = " ".join(f"a{i}" for i in range(len(ap.shape) - 1))
                    ap = ap.rearrange(f"p {dims} -> p ({dims})")
                ncols = ap.shape[1]
                dbg = nc.dram_tensor(pfx + "dbg_" + nm, [128, ncols], F32,
                                     kind="ExternalOutput")
                for c0 in range(0, ncols, 2048):
                    cw = min(2048, ncols - c0)
                    t_c = S_.on("vector",
                                lambda e, ap=ap, c0=c0, cw=cw: e.tensor_copy(
                                    aos[:, 0, 0:cw] if cw < 2048 else
                                    aos[:, :, :].rearrange("p a b -> p (a b)")[:, 0:cw],
                                    ap[:, c0:c0 + cw]),
                                wait=allw + ([prev] if prev else []), inc="ve")
                    prev = S_.dma("gpsimd", dbg[:, c0:c0 + cw],
                                  aos[:, 0, 0:cw] if cw < 2048 else
                                  aos[:, :, :].rearrange("p a b -> p (a b)")[:, 0:cw],
                                  wait=[t_c], inc="str")


def emit_attn_batch(S_, mla, b, x_hbm, y_hbm, W, L, phases=5,
                    y_view=None):
    xres, sq_scr, xhat_tok = L["xres"], L["sq_scr"], L["xhat_tok"]
    xhatF, qF, latF, vTo = L["xhatF"], L["qF"], L["latF"], L["vTo"]
    kvF, OF = L["kvF"], L["OF"]
    wstage, wbf = L["wstage"], L["wbf"]
    ident_bf, ident_f, ones1 = L["ident_bf"], L["ident_f"], L["ones1"]
    ln1, ln2, var_s = L["ln1"], L["ln2"], L["var_s"]
    decay, expS, rbc = L["decay"], L["expS"], L["rbc"]
    aos, outt = L["aos"], L["outt"]
    psT, psA, psB, psS = L["psT"], L["psA"], L["psB"], L["psS"]
    psAVe, psAVo, psBC = L["psAVe"], L["psAVo"], L["psBC"]
    t_id, t_idf, t_on, t_dec = L["t_id"], L["t_idf"], L["t_on"], L["t_dec"]
    kRF = xhatF if mla else qF
    x_b = x_hbm[b]
    y_b = y_hbm[b]
    if y_view is None:
        y_view = lambda bb, tt: y_hbm[bb][tt * 128:(tt + 1) * 128, :]
    SCALE = (1.0 / 32.0) if mla else (1.0 / 8.0)

    for t in range(NT):
        rot = t % 2
        t_x = S_.dma("sync", xres[:, rot, :], x_b[t * 128:(t + 1) * 128, :],
                     wait=[S_.g(("xres", rot))])
        t_ln = ln_tile(S_, ln1, t, xres[:, rot, :], sq_scr[:, :], var_s[:, :],
                       wait=[t_x])
        t_ap = S_.on("vector",
                     lambda e, t=t, rot=rot: e.tensor_scalar(
                         xhat_tok[:, rot, :], xres[:, rot, :],
                         ln1["r"][:, t:t + 1], ln1["nmr"][:, t:t + 1],
                         ALU.mult, ALU.add),
                     wait=[t_ln, S_.g(("xh", rot))], inc="ve")
        S_.setg(("xres", rot), t_ap)

        def ftp(e, rot=rot):
            ins = None
            for c in range(NCH):
                ins = e.matmul(psT[:, c * 128:(c + 1) * 128],
                               xhat_tok[:, rot, c * 128:(c + 1) * 128],
                               ident_bf[:, :], is_transpose=True,
                               start=(c == 0), stop=(c == NCH - 1),
                               skip_group_check=True)
            return ins
        t_tp = S_.on("tensor", ftp, wait=[t_ap, t_id, S_.g("psT")], inc="pe")
        S_.setg(("xh", rot), t_tp)
        t_ev = S_.on("vector",
                     lambda e, t=t: e.tensor_copy(
                         xhatF[:, :, t * 128:(t + 1) * 128], psT[:, :].rearrange(
                             "p (c n) -> p c n", c=NCH)),
                     wait=[t_tp], inc="ve")
        S_.setg("psT", t_ev)
    t_xhatF = S_.tok("ve")
    if phases < 2:
        return

    def proj(dst, n_dout, wt_tok, src, src_nc, extra=(), accum=False):
        toks = []
        for mc in range(n_dout // 128):
            for nt4 in range(S // 512):
                psu, key = (psA, "psA") if (mc * 4 + nt4) % 2 == 0 else (psB, "psB")
                pairs = [(wbf[:, kc, mc * 128:(mc + 1) * 128],
                          src[:, kc, nt4 * 512:(nt4 + 1) * 512])
                         for kc in range(src_nc)]
                t_mm = mm_chain(S_, psu[:, :], pairs,
                                wait=[wt_tok, S_.g(key)] + list(extra))
                if accum:
                    t_e = S_.on("vector",
                                lambda e, dst=dst, mc=mc, nt4=nt4, psu=psu:
                                e.tensor_tensor(
                                    dst[:, mc, nt4 * 512:(nt4 + 1) * 512],
                                    dst[:, mc, nt4 * 512:(nt4 + 1) * 512],
                                    psu[:, :], ALU.add),
                                wait=[t_mm], inc="ve")
                else:
                    t_e = S_.on("vector",
                                lambda e, dst=dst, mc=mc, nt4=nt4, psu=psu:
                                e.tensor_copy(
                                    dst[:, mc, nt4 * 512:(nt4 + 1) * 512],
                                    psu[:, :]),
                                wait=[t_mm], inc="ve")
                S_.setg(key, t_e)
                toks.append(t_e)
        return toks

    if mla:
        t_w = load_w_half(S_, W["wq"], wstage, wbf, NCH, D,
                          wait=[S_.g("wbf")])
        tq = proj(qF, D, t_w, xhatF, NCH, extra=[t_xhatF, S_.g("qF")])
        S_.setg("wbf", S_.tok("pe"))
        t_w = load_w_half(S_, W["wk"], wstage, wbf, NCH, D, wait=[S_.g("wbf")])
        tk_ = proj(kvF, D, t_w, xhatF, NCH, extra=[S_.g("kvF")])
        S_.setg("wbf", S_.tok("pe"))
        t_w = load_w_half(S_, W["wv"], wstage, wbf, NCH, D, wait=[S_.g("wbf")])
        tv = proj(kvF, D, t_w, xhatF, NCH, extra=[tk_[-1]], accum=True)
        S_.setg("wbf", S_.tok("pe"))
        t_w = load_w_half(S_, W["wdn"], wstage, wbf, NCH, DL, wait=[S_.g("wbf")])
        tl = proj(latF, DL, t_w, kvF, NCH, extra=[tv[-1]])
        S_.setg("wbf", S_.tok("pe"))
        t_w = load_w_half(S_, W["wuk"], wstage, wbf, 2, D, wait=[S_.g("wbf")])
        tkr = proj(kRF, D, t_w, latF, 2, extra=[tl[-1]])
        S_.setg("wbf", S_.tok("pe"))
        t_w = load_w_half(S_, W["wuv"], wstage, wbf, 2, D, wait=[S_.g("wbf")])
        for tt in range(NT):
            for nh in range(2):
                psu, key = (psA, "psA") if (tt * 2 + nh) % 2 == 0 else (psB, "psB")
                pairs = [(latF[:, kc, tt * 128:(tt + 1) * 128],
                          wbf[:, kc, nh * 512:(nh + 1) * 512]) for kc in range(2)]
                t_mm = mm_chain(S_, psu[:, :], pairs,
                                wait=[t_w, tl[-1], S_.g(key)])
                t_e = S_.on("vector",
                            lambda e, tt=tt, nh=nh, psu=psu: e.tensor_copy(
                                vTo[:, tt, nh * 8:(nh + 1) * 8, 0:64],
                                psu[:, :].rearrange("p (h c) -> p h c", h=8)),
                            wait=[t_mm], inc="ve")
                S_.setg(key, t_e)
        t_keys_done = tkr[-1]
        t_qdone = tq[-1]
    else:
        t_w = load_w_half(S_, W["wqkv"], wstage, wbf, NCH, D, wait=[S_.g("wbf")])
        tq = proj(qF, D, t_w, xhatF, NCH, extra=[t_xhatF, S_.g("qF")])
        S_.setg("wbf", S_.tok("pe"))
        for tt in range(NT):
            def ftv(e, tt=tt):
                ins = None
                for c in range(NCH):
                    ins = e.matmul(psT[:, c * 128:(c + 1) * 128],
                                   qF[:, c, tt * 128:(tt + 1) * 128],
                                   ident_bf[:, :], is_transpose=True,
                                   start=(c == 0), stop=(c == NCH - 1),
                                   skip_group_check=True)
                return ins
            t_tp = S_.on("tensor", ftv,
                         wait=[tq[-1], t_id, S_.g("psT")], inc="pe")
            t_e = S_.on("vector",
                        lambda e, tt=tt: e.tensor_copy(
                            vTo[:, tt, :, 0:64],
                            psT[:, :].rearrange("p (h c) -> p h c", h=H)),
                        wait=[t_tp], inc="ve")
            S_.setg("psT", t_e)
        t_keys_done = tq[-1]
        t_qdone = tq[-1]
    t_vto = S_.tok("ve")
    t_ones = S_.on("gpsimd", lambda e: e.memset(vTo[:, :, :, 64:65], 1.0),
                   wait=[t_vto], inc="gp")

    if phases < 3:
        return
    sbank = {(0, 0): (psS[:, 0:512], "psS0"),
             (0, 1): (psS[:, 512:1024], "psS1"),
             (1, 0): (psA[:, :], "psA"), (1, 1): (psB[:, :], "psB")}
    for pr in range(NCH):
        for qc in range(4):
            q0 = qc * 512
            t_es = {}

            def emit_score_chain(half, tki, pr=pr, q0=q0, t_es=t_es):
                k0 = tki * 128
                rot = tki % 2
                bank, bkey = sbank[(half, rot)]
                p0 = half * 64
                t_s = S_.on("tensor",
                            lambda e, bank=bank, p0=p0, k0=k0: e.matmul(
                                bank, kRF[p0:p0 + 64, pr, k0:k0 + 128],
                                qF[p0:p0 + 64, pr, q0:q0 + 512],
                                start=True, stop=True, skip_group_check=True),
                            wait=[t_keys_done, t_qdone, S_.g(bkey)], inc="pe")
                if mla:
                    dv = decay[:, q0 - k0 + 1920:q0 - k0 + 1920 + 512]
                    t_d = S_.on("vector",
                                lambda e, bank=bank, dv=dv: e.tensor_tensor(
                                    bank, bank, dv, ALU.mult),
                                wait=[t_s, t_dec], inc="ve")
                else:
                    t_d = t_s
                t_e = S_.on("scalar",
                            lambda e, bank=bank, rot=rot, half=half:
                            e.activation(
                                expS[:, rot, half * 512:half * 512 + 512],
                                bank, AF.Exp, scale=SCALE),
                            wait=[t_d, S_.g(("expS", half, rot))], inc="se")
                S_.setg(bkey, t_e)
                t_es[(half, tki)] = t_e

            def emit_av(half, tki, pr=pr, t_es=t_es):
                rot = tki % 2
                psu = psAVe if half == 0 else psAVo
                wait_av = [t_es[(half, tki)], t_ones]
                if tki == 0:
                    wait_av += [S_.g("psAV")]
                t_av = S_.on("tensor",
                             lambda e, psu=psu, tki=tki, rot=rot, half=half:
                             e.matmul(psu[:, :], vTo[:, tki, 2 * pr + half, :],
                                      expS[:, rot, half * 512:half * 512 + 512],
                                      start=(tki == 0), stop=(tki == NT - 1),
                                      skip_group_check=True),
                             wait=wait_av, inc="pe")
                S_.setg(("expS", half, rot), t_av)
                return t_av

            for tki in range(NT):
                emit_score_chain(0, tki)
                emit_score_chain(1, tki)
                if tki > 0:
                    emit_av(0, tki - 1)
                    emit_av(1, tki - 1)
            emit_av(0, NT - 1)
            t_av = emit_av(1, NT - 1)
            t_rc = S_.on("vector",
                         lambda e: (e.reciprocal(rbc[0:1, 0:512],
                                                 psAVe[64:65, :]),
                                    e.reciprocal(rbc[0:1, 512:1024],
                                                 psAVo[64:65, :]))[-1],
                         wait=[t_av], inc="ve")
            t_b1 = S_.on("tensor",
                         lambda e: e.matmul(psBC[:, :], ones1[:, :],
                                            rbc[0:1, 0:512], start=True,
                                            stop=True, skip_group_check=True),
                         wait=[t_rc, t_on, S_.g("psBC")], inc="pe")
            t_c1 = S_.on("scalar",
                         lambda e: e.copy(rbc[:, 0:512], psBC[:, :]),
                         wait=[t_b1, S_.g("rbc")], inc="se")
            t_b2 = S_.on("tensor",
                         lambda e: e.matmul(psBC[:, :], ones1[:, :],
                                            rbc[0:1, 512:1024], start=True,
                                            stop=True, skip_group_check=True),
                         wait=[t_c1], inc="pe")
            t_c2 = S_.on("scalar",
                         lambda e: e.copy(rbc[:, 512:1024], psBC[:, :]),
                         wait=[t_b2], inc="se")
            S_.setg("psBC", t_c2)

            def fnrm(e, pr=pr, qc=qc):
                e.tensor_tensor(OF[0:64, pr, qc * 512:(qc + 1) * 512],
                                psAVe[0:64, :], rbc[:, 0:512], ALU.mult)
                return e.tensor_tensor(OF[64:128, pr, qc * 512:(qc + 1) * 512],
                                       psAVo[0:64, :], rbc[:, 512:1024],
                                       ALU.mult)
            t_nm = S_.on("vector", fnrm, wait=[t_c1, t_c2], inc="ve")
            S_.setg("psAV", t_nm)
            S_.setg("rbc", t_nm)
    t_attn = S_.tok("ve")

    if phases < 4:
        return
    S_.setg("wbf", S_.tok("pe"))
    t_wo = load_w_half(S_, W["wo"], wstage, wbf, NCH, D, wait=[S_.g("wbf")])
    for t in range(NT):
        rot = t % 2
        evs = []
        for half in range(2):
            psu, key = (psA, "psA") if half == 0 else (psB, "psB")

            def fwo(e, half=half, t=t, psu=psu):
                ins = None
                for mci in range(4):
                    mc = half * 4 + mci
                    for kc in range(NCH):
                        ins = e.matmul(psu[:, mci * 128:(mci + 1) * 128],
                                       wbf[:, kc, mc * 128:(mc + 1) * 128],
                                       OF[:, kc, t * 128:(t + 1) * 128],
                                       start=(kc == 0 and mci == 0),
                                       stop=(kc == NCH - 1 and mci == 3),
                                       skip_group_check=True)
                return ins
            t_mm = S_.on("tensor", fwo,
                         wait=[t_wo, t_attn, S_.g(key)], inc="pe")
            t_ev = S_.on("vector",
                         lambda e, half=half, rot=rot, psu=psu: e.tensor_copy(
                             aos[:, rot, half * 512:(half + 1) * 512],
                             psu[:, :]),
                         wait=[t_mm, S_.g(("aos", rot))], inc="ve")
            S_.setg(key, t_ev)
            evs.append(t_ev)
        if phases < 4.2:
            S_.setg(("aos", rot), evs[-1])
            continue

        def ftp2(e, rot=rot):
            ins = None
            for c in range(NCH):
                ins = e.matmul(psS[:, c * 128:(c + 1) * 128],
                               aos[:, rot, c * 128:(c + 1) * 128],
                               ident_f[:, :], is_transpose=True,
                               start=(c % 4 == 0), stop=(c % 4 == 3),
                               skip_group_check=True)
            return ins
        t_tp = S_.on("tensor", ftp2, wait=evs + [t_idf, S_.g("psS")], inc="pe")
        S_.setg(("aos", rot), t_tp)
        if phases < 4.4:
            S_.setg("psS", t_tp)
            continue
        t_ln = ln_tile(S_, ln2, t, psS[:, :], sq_scr[:, :], var_s[:, :],
                       wait=[t_tp], from_psum=True)
        if phases < 4.6:
            S_.setg("psS", t_ln)
            continue
        t_x = S_.dma("sync", xres[:, rot, :], x_b[t * 128:(t + 1) * 128, :],
                     wait=[S_.g(("xres", rot))])

        S_.on("vector",
              lambda e, t=t, rot=rot: e.tensor_scalar(
                  outt[:, rot, :], psS[:, :],
                  ln2["r"][:, t:t + 1], ln2["nmr"][:, t:t + 1],
                  ALU.mult, ALU.add),
              wait=[t_ln, S_.g(("outt", rot))])
        t_ap = S_.on("vector",
                     lambda e, rot=rot: e.tensor_tensor(
                         outt[:, rot, :], outt[:, rot, :],
                         xres[:, rot, :], ALU.add),
                     wait=[t_x])
        S_.setg("psS", t_ap)
        S_.setg(("xres", rot), t_ap)
        if phases < 4.8:
            S_.setg(("outt", rot), t_ap)
            S_.setg(("aos", rot), t_ap)
            continue
        t_st = S_.dma("gpsimd", y_view(b, t), outt[:, rot, :],
                      wait=[t_ap], inc="str")
        S_.setg(("outt", rot), t_st)
        S_.setg(("aos", rot), t_st)
    S_.setg("qF", S_.tok("pe"))
    S_.setg("kvF", S_.tok("pe"))


def load_w(S_, wv, wstage, wdst, chunks, dout, wait=(), stage_cols=1024):
    per = max(1, stage_cols // dout)
    t_c = None
    c0 = 0
    while c0 < chunks:
        cs = min(per, chunks - c0)
        sview = wstage[:, 0:cs * dout].rearrange("p (c n) -> p c n", c=cs)
        t_d = S_.dma("sync", sview, wv[:, c0:c0 + cs, :],
                     wait=list(wait) + [S_.g("wstage")])
        t_c = S_.on("vector",
                    lambda e, c0=c0, cs=cs, sview=sview: e.tensor_copy(
                        wdst[:, c0:c0 + cs, :], sview),
                    wait=[t_d])
        S_.setg("wstage", t_c)
        c0 += cs
    return t_c


def load_w2(S_, wv, stA, stB, wdst, chunks, wait=()):
    t_c = None
    for c in range(chunks):
        ap, key, q, sem = (stA, "wsA", "sync", "dma") if c % 2 == 0 else \
            (stB, "wsB", "gpsimd", "dmb")
        t_d = S_.dma(q, ap, wv[:, c, :], wait=list(wait) + [S_.g(key)],
                     inc=sem)
        t_c = S_.on("scalar",
                    lambda e, c=c, ap=ap: e.copy(wdst[:, c, :], ap),
                    wait=[t_d], inc="se")
        S_.setg(key, t_c)
    return t_c


FFN0_W = [("gT", [D, 8]), ("w1T", [8, D, D]), ("w2T", [8, D, D]),
          ("w3T", [8, D, D]), ("b1T", [D, 8]), ("b2T", [D, 8]),
          ("b3T", [D, 8]), ("sw1T", [D, 2 * D]), ("sw2T", [D, 2 * D]),
          ("sw3T", [2 * D, D])]


def build_l0ffn(sim_debug=False, nbatch=None, debug_dump=()):
    nc = bass.Bass("TRN2", target_bir_lowering=False,
                   detect_race_conditions=not sim_debug)
    io = {"x": nc.dram_tensor("x", [B_CORE, S, D], F32, kind="ExternalInput"),
          "y": nc.dram_tensor("y", [B_CORE, S, D], F32, kind="ExternalOutput")}
    for k, sh in FFN0_W:
        io[k] = nc.dram_tensor(k, sh, F32, kind="ExternalInput")
    with ExitStack() as ctx:
        S_ = Sched(nc, ctx)
        block = ctx.enter_context(nc.Block())
        emit_l0ffn(nc, S_, io, nbatch=nbatch, debug_dump=debug_dump)
        S_.on("gpsimd", lambda e_: e_.engine_nop(), wait=[S_.tok("str")])
        S_.finalize(block)
    return nc


def emit_l0ffn(nc, S_, io, pfx="", nbatch=None, debug_dump=()):
    x_hbm = io["x"]
    y_hbm = io["y"]
    gT_h, w1_h, w2_h, w3_h = io["gT"], io["w1T"], io["w2T"], io["w3T"]
    b1_h, b2_h, b3_h = io["b1T"], io["b2T"], io["b3T"]
    sw1_h, sw2_h, sw3_h = io["sw1T"], io["sw2T"], io["sw3T"]

    with ExitStack() as ctx:
        def e(cm):
            return ctx.enter_context(cm)
        AR = 16384 + 24576 + 4096 + 4096 + 2048 + 512 + 1024 + 4096 + 512 + 512
        arena = e(nc.sbuf_tensor(pfx + "arena", [128, AR], BF16))
        off = [0]

        def carve(n):
            o = off[0]
            off[0] += n
            return o
        o_xsF = carve(16384)
        xhatF = arena[:, o_xsF:o_xsF + 16384].rearrange("p (c n) -> p c n", c=NCH)
        o_w = carve(24576)
        wr1 = arena[:, o_w:o_w + 8192].rearrange("p (c n) -> p c n", c=NCH)
        wr3 = arena[:, o_w + 8192:o_w + 16384].rearrange("p (c n) -> p c n", c=NCH)
        wr2 = arena[:, o_w + 16384:o_w + 24576].rearrange("p (c n) -> p c n", c=NCH)
        swh = wr1
        g1 = arena[:, o_w + 8192:o_w + 24576].rearrange("p (c n) -> p c n", c=NCH)
        o_h = carve(4096)
        h_nt = arena[:, o_h:o_h + 4096].rearrange("p (c n) -> p c n", c=NCH)
        o_x3 = carve(4096)
        x3m = arena[:, o_x3:o_x3 + 4096].rearrange("p (c n) -> p c n", c=NCH)
        o_mf = carve(2048)
        maskF = arena[0:16, o_mf:o_mf + 2048]
        o_sel = carve(512)
        selb = arena[0:16, o_sel:o_sel + 512].rearrange(
            "p (s r n) -> p s r n", s=2, r=2)
        o_sq = carve(1024)
        sq_scr = arena[:, o_sq:o_sq + 1024]
        o_xscr = carve(4096)
        xs_scr = arena[:, o_xscr:o_xscr + 4096].rearrange("p (c n) -> p c n",
                                                          c=NCH)
        o_mb = carve(512)
        mb_sb = arena[:, o_mb:o_mb + 512]
        o_sbc = carve(512)
        sbc_sb = arena[:, o_sbc:o_sbc + 512]
        y_sb = e(nc.sbuf_tensor(pfx + "y_sb", [128, NCH, S], F32))
        xres = e(nc.sbuf_tensor(pfx + "xres", [128, 2, D], F32))
        scr32 = e(nc.sbuf_tensor(pfx + "scr32", [128, 3072], F32))
        xhat32 = scr32[:, 0:2048].rearrange("p (r n) -> p r n", r=2)
        xf32 = scr32[:, 2048:3072].rearrange("p (c n) -> p c n", c=NCH)
        outt = scr32[:, 0:2048].rearrange("p (r n) -> p r n", r=2)
        wstage = e(nc.sbuf_tensor(pfx + "wstage", [128, 1024], F32))
        wstgB = scr32[:, 0:1024]
        tmp32 = e(nc.sbuf_tensor(pfx + "tmp32", [128, 512], F32))
        logits = e(nc.sbuf_tensor(pfx + "logits", [128, NT, 8], F32))
        exps = e(nc.sbuf_tensor(pfx + "exps", [128, NT, 8], F32))
        cmb = e(nc.sbuf_tensor(pfx + "cmb", [128, NT, 16], BF16))
        ident_bf = e(nc.sbuf_tensor(pfx + "ident_bf", [128, 128], BF16))
        rst = e(nc.sbuf_tensor(pfx + "rst", [128, 6, NT], F32))
        nmx, den, m1, m2 = rst[:, 0, :], rst[:, 1, :], rst[:, 2, :], rst[:, 3, :]
        w1s, w2s = rst[:, 4, :], rst[:, 5, :]
        l8sb = e(nc.sbuf_tensor(pfx + "l8sb", [8, 128], F32))
        ident_f = e(nc.sbuf_tensor(pfx + "ident_f", [128, 128], F32))
        gT_sb = e(nc.sbuf_tensor(pfx + "gT_sb", [128, NCH, 8], F32))
        b1F = e(nc.sbuf_tensor(pfx + "b1F", [128, NCH, 8], F32))
        b2F = e(nc.sbuf_tensor(pfx + "b2F", [128, NCH, 8], F32))
        b3F = e(nc.sbuf_tensor(pfx + "b3F", [128, NCH, 8], F32))
        ln1 = new_ln(nc, ctx, pfx + "ln1")
        ln2 = ln1
        var_s = e(nc.sbuf_tensor(pfx + "var_s", [128, 1], F32))
        psA = e(nc.psum_tensor(pfx + "psA", [128, 512], F32))
        psB = e(nc.psum_tensor(pfx + "psB", [128, 512], F32))
        psC = e(nc.psum_tensor(pfx + "psC", [128, 512], F32))
        psD = e(nc.psum_tensor(pfx + "psD", [128, 512], F32))
        psE = e(nc.psum_tensor(pfx + "psE", [128, 512], F32))
        psF = e(nc.psum_tensor(pfx + "psF", [128, 512], F32))
        psW = e(nc.psum_tensor(pfx + "psW", [128, 1024], F32))

        t_idf = make_identity(S_, ident_f[:, :])
        t_idb = make_identity(S_, ident_bf[:, :])
        t_g = S_.dma("sync", gT_sb[:, :, :],
                     gT_h[:, :].rearrange("(c p) e -> p c e", p=128))
        t_b1 = S_.dma("sync", b1F[:, :, :],
                      b1_h[:, :].rearrange("(c p) e -> p c e", p=128))
        t_b2 = S_.dma("sync", b2F[:, :, :],
                      b2_h[:, :].rearrange("(c p) e -> p c e", p=128))
        t_b3 = S_.dma("sync", b3F[:, :, :],
                      b3_h[:, :].rearrange("(c p) e -> p c e", p=128))

        for b in range(B_CORE if nbatch is None else nbatch):
            x_b = x_hbm[b]
            y_b = y_hbm[b]
            for t in range(NT):
                rot = t % 2
                t_x = S_.dma("sync", xres[:, rot, :],
                             x_b[t * 128:(t + 1) * 128, :],
                             wait=[S_.g(("xres", rot))])
                t_ln = ln_tile(S_, ln1, t, xres[:, rot, :], sq_scr, var_s[:, :],
                               wait=[t_x])
                t_ap = S_.on("vector",
                             lambda e_, t=t, rot=rot: e_.tensor_scalar(
                                 xhat32[:, rot, :], xres[:, rot, :],
                                 ln1["r"][:, t:t + 1], ln1["nmr"][:, t:t + 1],
                                 ALU.mult, ALU.add),
                             wait=[t_ln, S_.g(("xh", rot)),
                                   S_.g(("outt", rot))])
                S_.setg(("xres", rot), t_ap)

                def ftp(e_, rot=rot):
                    ins = None
                    for c in range(NCH):
                        ins = e_.matmul(psW[:, c * 128:(c + 1) * 128],
                                        xhat32[:, rot, c * 128:(c + 1) * 128],
                                        ident_f[:, :], is_transpose=True,
                                        start=(c % 4 == 0), stop=(c % 4 == 3),
                                        skip_group_check=True)
                    return ins
                t_tp = S_.on("tensor", ftp,
                             wait=[t_ap, t_idf, S_.g("psW")], inc="pe")
                S_.setg(("xh", rot), t_tp)
                t_e1 = S_.on("vector",
                             lambda e_, t=t: e_.tensor_copy(
                                 xhatF[:, :, t * 128:(t + 1) * 128],
                                 psW[:, :].rearrange("p (c n) -> p c n", c=NCH)),
                             wait=[t_tp])
                t_e2 = S_.on("vector",
                             lambda e_: e_.tensor_copy(
                                 xf32[:, :, :],
                                 psW[:, :].rearrange("p (c n) -> p c n", c=NCH)),
                             wait=[S_.g("xf32")])
                S_.setg("psW", t_e2)
                pairs = [(gT_sb[:, kc, :], xf32[:, kc, :]) for kc in range(NCH)]
                t_rm = mm_chain(S_, psA[0:8, 0:128], pairs,
                                wait=[t_e2, t_g, S_.g("psA")])
                t_l8 = S_.on("scalar",
                             lambda e_: e_.copy(l8sb[:, :], psA[0:8, 0:128]),
                             wait=[t_rm], inc="se")
                S_.setg("psA", t_l8)
                t_lt = S_.on("tensor",
                             lambda e_: e_.matmul(psB[0:128, 0:8], l8sb[:, :],
                                                  ident_f[0:8, 0:8], start=True,
                                                  stop=True,
                                                  skip_group_check=True),
                             wait=[t_l8, t_idf, S_.g("psB")], inc="pe")
                t_lg = S_.on("vector",
                             lambda e_, t=t: e_.tensor_copy(
                                 logits[:, t, :], psB[0:128, 0:8]),
                             wait=[t_lt])
                S_.setg("psB", t_lg)
                S_.setg("xf32", t_rm)
            S_.setg("wsB", S_.tok("pe"))
            S_.on("vector", lambda e_: e_.reduce_max(nmx, logits[:, :, :],
                                                     axis=AX.X))
            S_.on("vector", lambda e_: e_.tensor_scalar(nmx, nmx, -1.0, None,
                                                        ALU.mult))
            tv = S_.tok("ve")
            for t in range(NT):
                S_.on("scalar",
                      lambda e_, t=t: e_.activation(
                          exps[:, t, :], logits[:, t, :], AF.Exp,
                          bias=nmx[:, t:t + 1],
                          accum_out=den[:, t:t + 1]),
                      wait=[tv], inc="se")
            tse = S_.tok("se")
            S_.on("vector", lambda e_: e_.reciprocal(den, den), wait=[tse])
            S_.on("vector", lambda e_: e_.reduce_max(m1, exps[:, :, :],
                                                     axis=AX.X))
            for t in range(NT):
                S_.on("vector",
                      lambda e_, t=t: e_.tensor_scalar(
                          cmb[:, t, 0:8], exps[:, t, :], m1[:, t:t + 1], None,
                          ALU.is_ge))
            S_.on("vector",
                  lambda e_: e_.scalar_tensor_tensor(
                      logits[:, :, :], cmb[:, :, 0:8], -1e30, exps[:, :, :],
                      ALU.mult, ALU.add))
            S_.on("vector", lambda e_: e_.reduce_max(m2, logits[:, :, :],
                                                     axis=AX.X))
            for t in range(NT):
                S_.on("vector",
                      lambda e_, t=t: e_.tensor_scalar(
                          cmb[:, t, 8:16], logits[:, t, :], m2[:, t:t + 1],
                          None, ALU.is_ge))
            S_.on("vector", lambda e_: e_.tensor_tensor(w1s, m1, den, ALU.mult))
            S_.on("vector", lambda e_: e_.tensor_tensor(w2s, m2, den, ALU.mult))
            for t in range(NT):
                S_.on("vector",
                      lambda e_, t=t: e_.tensor_scalar(
                          exps[:, t, :], cmb[:, t, 0:8], w1s[:, t:t + 1], None,
                          ALU.mult))
                S_.on("vector",
                      lambda e_, t=t: e_.tensor_tensor(
                          cmb[:, t, 0:8], cmb[:, t, 0:8], cmb[:, t, 8:16],
                          ALU.add))
                S_.on("vector",
                      lambda e_, t=t: e_.scalar_tensor_tensor(
                          cmb[:, t, 8:16], cmb[:, t, 8:16], w2s[:, t:t + 1],
                          exps[:, t, :], ALU.mult, ALU.add))
            t_rr = S_.tok("ve")
            for t in range(NT):
                t_mt = S_.on("tensor",
                             lambda e_, t=t: e_.matmul(
                                 psA[0:16, 0:128], cmb[:, t, :], ident_bf[:, :],
                                 start=True, stop=True, skip_group_check=True),
                             wait=[t_rr, S_.g("psA"), t_idb], inc="pe")
                t_me = S_.on("vector",
                             lambda e_, t=t: e_.tensor_copy(
                                 maskF[:, t * 128:(t + 1) * 128],
                                 psA[0:16, 0:128]),
                             wait=[t_mt])
                S_.setg("psA", t_me)
            t_mask = S_.tok("ve")
            for hh in range(2):
                w1v = sw1_h[:, :].rearrange("(c p) n -> p c n", p=128)[
                    :, :, hh * D:(hh + 1) * D]
                t_w = load_w2(S_, w1v, wstage[:, :], wstgB, swh, NCH,
                              wait=[S_.g("swh"), S_.tok("str")])
                for mcl in range(NCH):
                    for nt in range(4):
                        psu, key = (psC, "psC") if (mcl * 4 + nt) % 2 == 0 \
                            else (psD, "psD")
                        pairs = [(swh[:, kc, mcl * 128:(mcl + 1) * 128],
                                  xhatF[:, kc, nt * 512:(nt + 1) * 512])
                                 for kc in range(NCH)]
                        t_mm = mm_chain(S_, psu[:, :], pairs,
                                        wait=[t_w, S_.g(key)])
                        t_ev = S_.on("scalar",
                                     lambda e_, mcl=mcl, nt=nt, psu=psu:
                                     e_.activation(
                                         g1[:, mcl, nt * 512:(nt + 1) * 512],
                                         psu[:, :], AF.Silu),
                                     wait=[t_mm], inc="se")
                        S_.setg(key, t_ev)
                S_.setg("swh", S_.tok("pe"))
                w2v = sw2_h[:, :].rearrange("(c p) n -> p c n", p=128)[
                    :, :, hh * D:(hh + 1) * D]
                t_w = load_w2(S_, w2v, wstage[:, :], wstgB, swh, NCH,
                              wait=[S_.g("swh")])
                for mcl in range(NCH):
                    for nt in range(4):
                        psu, key = (psC, "psC") if (mcl * 4 + nt) % 2 == 0 \
                            else (psD, "psD")
                        pairs = [(swh[:, kc, mcl * 128:(mcl + 1) * 128],
                                  xhatF[:, kc, nt * 512:(nt + 1) * 512])
                                 for kc in range(NCH)]
                        t_mm = mm_chain(S_, psu[:, :], pairs,
                                        wait=[t_w, S_.g(key)])
                        t_ev = S_.on("vector",
                                     lambda e_, mcl=mcl, nt=nt, psu=psu:
                                     e_.tensor_tensor(
                                         g1[:, mcl, nt * 512:(nt + 1) * 512],
                                         g1[:, mcl, nt * 512:(nt + 1) * 512],
                                         psu[:, :], ALU.mult),
                                     wait=[t_mm])
                        S_.setg(key, t_ev)
                S_.setg("swh", S_.tok("pe"))
                w3v = sw3_h[:, :].rearrange("(c p) n -> p c n", p=128)[
                    :, hh * NCH:(hh + 1) * NCH, :]
                t_w = load_w2(S_, w3v, wstage[:, :], wstgB, swh, NCH,
                              wait=[S_.g("swh")])
                for mc in range(NCH):
                    for nt in range(4):
                        psu, key = (psC, "psC") if (mc * 4 + nt) % 2 == 0 \
                            else (psD, "psD")
                        pairs = [(swh[:, kc, mc * 128:(mc + 1) * 128],
                                  g1[:, kc, nt * 512:(nt + 1) * 512])
                                 for kc in range(NCH)]
                        t_mm = mm_chain(S_, psu[:, :], pairs,
                                        wait=[t_w, S_.g(key)])
                        if hh == 0:
                            t_ev = S_.on("vector",
                                         lambda e_, mc=mc, nt=nt, psu=psu:
                                         e_.tensor_copy(
                                             y_sb[:, mc, nt * 512:(nt + 1) * 512],
                                             psu[:, :]),
                                         wait=[t_mm])
                        else:
                            t_ev = S_.on("vector",
                                         lambda e_, mc=mc, nt=nt, psu=psu:
                                         e_.tensor_tensor(
                                             y_sb[:, mc, nt * 512:(nt + 1) * 512],
                                             y_sb[:, mc, nt * 512:(nt + 1) * 512],
                                             psu[:, :], ALU.add),
                                         wait=[t_mm])
                        S_.setg(key, t_ev)
                S_.setg("swh", S_.tok("pe"))
            t_xhat = S_.tok("ve")

            def rebuild_selb(ei):
                sl = ei % 2
                S_.setg(("selb", sl), S_.tok("pe"))
                t_s0 = S_.on("gpsimd",
                             lambda e_, sl=sl: e_.memset(selb[:, sl, :, :],
                                                         0.0),
                             wait=[S_.g(("selb", sl))])
                S_.on("gpsimd",
                      lambda e_, ei=ei, sl=sl: e_.affine_select(
                          out=selb[:, sl, 0, :], in_=selb[:, sl, 0, :],
                          compare_op=ALU.not_equal, fill=1.0, base=-ei,
                          pattern=[[0, 128]], channel_multiplier=1),
                      wait=[t_s0])
                return S_.on("gpsimd",
                             lambda e_, ei=ei, sl=sl: e_.affine_select(
                                 out=selb[:, sl, 1, :], in_=selb[:, sl, 1, :],
                                 compare_op=ALU.not_equal, fill=1.0,
                                 base=-(8 + ei), pattern=[[0, 128]],
                                 channel_multiplier=1))

            def bl_issue(ei, nt, t_sel):
                sl = ei % 2
                n0 = nt * 512
                t_mb = S_.on("tensor",
                             lambda e_, sl=sl, n0=n0: e_.matmul(
                                 psE[:, :], selb[:, sl, 0, :],
                                 maskF[:, n0:n0 + 512], start=True, stop=True,
                                 skip_group_check=True),
                             wait=[t_mask, t_sel, S_.g("psE")], inc="pe")
                t_mbe = S_.on("scalar",
                              lambda e_: e_.copy(mb_sb, psE[:, :]),
                              wait=[t_mb, S_.g("mb_sb")], inc="se")
                S_.setg("psE", t_mbe)
                t_sb = S_.on("tensor",
                             lambda e_, sl=sl, n0=n0: e_.matmul(
                                 psF[:, :], selb[:, sl, 1, :],
                                 maskF[:, n0:n0 + 512], start=True, stop=True,
                                 skip_group_check=True),
                             wait=[t_mask, t_sel, S_.g("psF")], inc="pe")
                t_sbe = S_.on("scalar",
                              lambda e_: e_.copy(sbc_sb, psF[:, :]),
                              wait=[t_sb, S_.g("sbc_sb")], inc="se")
                S_.setg("psF", t_sbe)
                t_bld = None
                for kc in range(NCH):
                    t_bld = S_.on("gpsimd",
                                  lambda e_, kc=kc, n0=n0: e_.tensor_tensor(
                                      xs_scr[:, kc, :],
                                      xhatF[:, kc, n0:n0 + 512],
                                      sbc_sb, ALU.mult),
                                  wait=[t_sbe, t_xhat, S_.g("xs_scr")])
                S_.setg("sbc_sb", t_bld)
                return t_mbe, t_bld

            def loads_w13(ei):
                t_w1 = load_w2(S_, w1_h[ei].rearrange("(c p) n -> p c n",
                                                      p=128),
                               wstage[:, :], wstgB, wr1, NCH,
                               wait=[S_.g("wr1"), S_.g("swh")])
                t_w3 = load_w2(S_, w3_h[ei].rearrange("(c p) n -> p c n",
                                                      p=128),
                               wstage[:, :], wstgB, wr3, NCH,
                               wait=[S_.g("wr3"), S_.g("swh")])
                return t_w1, t_w3

            t_sel_cur = rebuild_selb(0)
            t_sel_nxt = None
            pend = None
            tw13 = loads_w13(0)
            tw13_nxt = None
            for ei in range(8):
                t_w1, t_w3 = tw13
                t_w2 = load_w2(S_, w2_h[ei].rearrange("(c p) n -> p c n",
                                                      p=128),
                               wstage[:, :], wstgB, wr2, NCH,
                               wait=[S_.g("wr2"), S_.g("swh")])
                if pend is None:
                    pend = bl_issue(ei, 0, t_sel_cur)
                for nt in range(4):
                    n0 = nt * 512
                    t_mbe, t_bld = pend
                    t_hlast = None
                    for mc in range(NCH):
                        psu, key = (psA, "psA") if mc % 2 == 0 else (psB, "psB")
                        pairs = [(wr1[:, kc, mc * 128:(mc + 1) * 128],
                                  xs_scr[:, kc, :])
                                 for kc in range(NCH)]
                        t_mm = mm_chain(S_, psu[:, :], pairs,
                                        wait=[t_w1, t_bld, S_.g(key)])
                        t_hlast = S_.on("scalar",
                                        lambda e_, mc=mc, psu=psu, ei=ei:
                                        e_.activation(
                                            h_nt[:, mc, :], psu[:, :], AF.Silu,
                                            bias=b1F[:, mc, ei:ei + 1]),
                                        wait=[t_mm, t_b1, S_.g("h_nt")],
                                        inc="se")
                        S_.setg(key, t_hlast)
                    if nt == 3:
                        S_.setg("wr1", S_.tok("pe"))
                    for mc in range(NCH):
                        psu, key = (psC, "psC") if mc % 2 == 0 else (psD, "psD")
                        pairs = [(wr3[:, kc, mc * 128:(mc + 1) * 128],
                                  xs_scr[:, kc, :])
                                 for kc in range(NCH)]
                        t_mm = mm_chain(S_, psu[:, :], pairs,
                                        wait=[t_w3, t_bld, S_.g(key)])
                        t_x3m = S_.on("vector",
                                      lambda e_, mc=mc, psu=psu, ei=ei:
                                      e_.scalar_tensor_tensor(
                                          x3m[:, mc, :], psu[:, :],
                                          b3F[:, mc, ei:ei + 1], mb_sb,
                                          ALU.add, ALU.mult),
                                      wait=[t_mm, t_b3, t_mbe,
                                            S_.g("x3m")])
                        S_.setg(key, t_x3m)
                    S_.setg("mb_sb", S_.tok("ve"))
                    S_.setg("xs_scr", S_.tok("pe"))
                    if nt == 3:
                        S_.setg("wr3", S_.tok("pe"))
                    if nt == 0 and ei < 7:
                        t_sel_nxt = rebuild_selb(ei + 1)
                    if nt < 3:
                        pend = bl_issue(ei, nt + 1, t_sel_cur)
                    elif ei < 7:
                        pend = bl_issue(ei + 1, 0, t_sel_nxt)
                        t_sel_cur = t_sel_nxt
                        tw13_nxt = loads_w13(ei + 1)
                    else:
                        pend = None
                    for mc in range(NCH):
                        psu = psW[:, 0:512] if mc % 2 == 0 else psW[:, 512:1024]
                        key = "psW0" if mc % 2 == 0 else "psW1"
                        pairs = [(wr2[:, kc, mc * 128:(mc + 1) * 128],
                                  h_nt[:, kc, :])
                                 for kc in range(NCH)]
                        t_mm = mm_chain(S_, psu, pairs,
                                        wait=[t_w2, S_.g(key), t_hlast])
                        t_st = S_.on("vector",
                                     lambda e_, mc=mc, psu=psu, ei=ei:
                                     e_.scalar_tensor_tensor(
                                         tmp32[:, :], psu,
                                         b2F[:, mc, ei:ei + 1],
                                         x3m[:, mc, :], ALU.add, ALU.mult),
                                     wait=[t_mm, t_b2])
                        t_ya = S_.on("vector",
                                     lambda e_, mc=mc, n0=n0:
                                     e_.tensor_tensor(
                                         y_sb[:, mc, n0:n0 + 512],
                                         y_sb[:, mc, n0:n0 + 512],
                                         tmp32[:, :], ALU.add))
                        S_.setg(key, t_ya)
                    S_.setg("h_nt", S_.tok("pe"))
                    S_.setg("x3m", S_.tok("ve"))
                S_.setg("wr2", S_.tok("pe"))
                tw13 = tw13_nxt
            t_y = S_.tok("ve")
            for t in range(NT):
                rot = t % 2

                def ftp2(e_, t=t):
                    ins = None
                    for c in range(NCH):
                        ins = e_.matmul(psW[:, c * 128:(c + 1) * 128],
                                        y_sb[:, c, t * 128:(t + 1) * 128],
                                        ident_f[:, :], is_transpose=True,
                                        start=(c % 4 == 0), stop=(c % 4 == 3),
                                        skip_group_check=True)
                    return ins
                t_tp = S_.on("tensor", ftp2, wait=[t_y, t_idf, S_.g("psW")],
                             inc="pe")
                t_ln = ln_tile(S_, ln2, t, psW[:, :], sq_scr, var_s[:, :],
                               wait=[t_tp], from_psum=True)
                t_x = S_.dma("sync", xres[:, rot, :],
                             x_b[t * 128:(t + 1) * 128, :],
                             wait=[S_.g(("xres", rot))])
                S_.on("vector",
                      lambda e_, t=t: e_.tensor_scalar(
                          outt[:, t % 2, :], psW[:, :],
                          ln2["r"][:, t:t + 1], ln2["nmr"][:, t:t + 1],
                          ALU.mult, ALU.add),
                      wait=[t_ln, S_.g(("outt", rot))])
                t_ap = S_.on("vector",
                             lambda e_, rot=rot: e_.tensor_tensor(
                                 outt[:, rot, :], outt[:, rot, :],
                                 xres[:, rot, :], ALU.add),
                             wait=[t_x])
                S_.setg("psW", t_ap)
                S_.setg(("xres", rot), t_ap)
                t_st = S_.dma("gpsimd", y_b[t * 128:(t + 1) * 128, :],
                              outt[:, rot, :], wait=[t_ap], inc="str")
                S_.setg(("outt", rot), t_st)
            S_.setg("y_sb", S_.tok("ve"))
            S_.setg(("xh", 0), S_.tok("pe"))
            S_.setg(("xh", 1), S_.tok("pe"))

        if debug_dump:
            allw = [S_.tok(s) for s in ("ve", "se", "pe", "gp", "dma")]
            Lx = locals()
            prev = None
            for nm in debug_dump:
                if "." in nm:
                    a_, b_ = nm.split(".")
                    ap = Lx[a_][b_]
                    nm = a_ + "_" + b_
                else:
                    ap = Lx[nm]
                if not isinstance(ap, bass.AP):
                    ap = ap[tuple(slice(None) for _ in ap.shape)]
                if len(ap.shape) > 2:
                    dims = " ".join(f"a{i}" for i in range(len(ap.shape) - 1))
                    ap = ap.rearrange(f"p {dims} -> p ({dims})")
                ncols = ap.shape[1]
                nparts = ap.shape[0]
                dbg = nc.dram_tensor(pfx + "dbg_" + nm, [nparts, ncols], F32,
                                     kind="ExternalOutput")
                for c0 in range(0, ncols, 2048):
                    cw = min(2048, ncols - c0)
                    t_c = S_.on("vector",
                                lambda e_, ap=ap, c0=c0, cw=cw, nparts=nparts:
                                e_.tensor_copy(tmp32[0:nparts, 0:min(cw, 512)]
                                               if cw <= 512 else
                                               y_sb[0:nparts, 0, 0:cw],
                                               ap[:, c0:c0 + cw]),
                                wait=allw + ([prev] if prev else []))
                    prev = S_.dma("gpsimd", dbg[:, c0:c0 + cw],
                                  tmp32[0:nparts, 0:min(cw, 512)]
                                  if cw <= 512 else y_sb[0:nparts, 0, 0:cw],
                                  wait=[t_c], inc="str")


NPOS = 256
NSTEP = 16
NT4 = 32


FFN1_WN = ["wfT", "ufT", "wiT", "uiT", "wcT", "ucT", "woT", "uoT"]
FFN1_BN = ["bf", "bi", "bc", "bo", "in_c"]


def build_l1ffn(sim_debug=False, nstep=None, debug_dump=()):
    nc = bass.Bass("TRN2", target_bir_lowering=False,
                   detect_race_conditions=not sim_debug)
    io = {"x": nc.dram_tensor("x", [NSTEP, NPOS, D], F32, kind="ExternalInput"),
          "y": nc.dram_tensor("y", [NSTEP, NPOS, D], F32, kind="ExternalOutput")}
    for k in FFN1_WN:
        io[k] = nc.dram_tensor(k, [D, D], F32, kind="ExternalInput")
    for k in FFN1_BN:
        io[k] = nc.dram_tensor(k, [D], F32, kind="ExternalInput")
    with ExitStack() as ctx:
        S_ = Sched(nc, ctx)
        block = ctx.enter_context(nc.Block())
        emit_l1ffn(nc, S_, io, nstep=nstep, debug_dump=debug_dump)
        S_.on("gpsimd", lambda e_: e_.engine_nop(), wait=[S_.tok("str")])
        S_.finalize(block)
    return nc


def emit_l1ffn(nc, S_, io, pfx="", nstep=None, debug_dump=()):
    x_hbm = io["x"]
    y_hbm = io["y"]
    Wh = io
    Bh = io

    with ExitStack() as ctx:
        def e(cm):
            return ctx.enter_context(cm)
        wreg = e(nc.sbuf_tensor(pfx + "wreg", [128, 6, NCH, D], BF16))
        wv = {k: wreg[:, i, :, :] for i, k in
              enumerate(["wfT", "ufT", "wiT", "uiT", "wcT", "ucT"])}
        wv["woT"] = wv["wfT"]
        wv["uoT"] = wv["ufT"]
        wstage = e(nc.sbuf_tensor(pfx + "wstage", [128, 1024], F32))
        xres = e(nc.sbuf_tensor(pfx + "xres", [128, 2, D], F32))
        xhat_tok = e(nc.sbuf_tensor(pfx + "xhat_tok", [128, 2, D], BF16))
        xFs = e(nc.sbuf_tensor(pfx + "xFs", [128, 2, NCH, NPOS], BF16))
        c_sb = e(nc.sbuf_tensor(pfx + "c_sb", [128, NCH, NPOS], F32))
        cbf = e(nc.sbuf_tensor(pfx + "cbf", [128, NCH, NPOS], BF16))
        gf = e(nc.sbuf_tensor(pfx + "gf", [128, NCH, NPOS], F32))
        gi = e(nc.sbuf_tensor(pfx + "gi", [128, NCH, NPOS], F32))
        gc = e(nc.sbuf_tensor(pfx + "gc", [128, NCH, NPOS], F32))
        cuo = gf
        hF = gi
        ones_t = e(nc.sbuf_tensor(pfx + "ones_t", [128, NPOS], F32))
        biasF = e(nc.sbuf_tensor(pfx + "biasF", [128, 5, NCH], F32))
        bvw = {k: biasF[:, i, :] for i, k in
               enumerate(["bf", "bi", "bc", "bo", "in_c"])}
        ln1 = new_ln(nc, ctx, pfx + "ln1", ntiles=NT4)
        ln2 = new_ln(nc, ctx, pfx + "ln2", ntiles=NT4)
        var_s = e(nc.sbuf_tensor(pfx + "var_s", [128, 1], F32))
        sq_scr = e(nc.sbuf_tensor(pfx + "sq_scr", [128, D], BF16))
        ident_f = e(nc.sbuf_tensor(pfx + "ident_f", [128, 128], F32))
        ident_b = e(nc.sbuf_tensor(pfx + "ident_b", [128, 128], BF16))
        outt = e(nc.sbuf_tensor(pfx + "outt", [128, 2, D], F32))
        psA = e(nc.psum_tensor(pfx + "psA", [128, 512], F32))
        psB = e(nc.psum_tensor(pfx + "psB", [128, 512], F32))
        psW = e(nc.psum_tensor(pfx + "psW", [128, 1024], F32))
        psT = e(nc.psum_tensor(pfx + "psT", [128, 1024], BF16))

        t_idf = make_identity(S_, ident_f[:, :])
        t_idb = make_identity(S_, ident_b[:, :])
        t_one = S_.on("gpsimd", lambda e_: e_.memset(ones_t[:, :], 1.0))
        for k in ["bf", "bi", "bc", "bo", "in_c"]:
            S_.dma("sync", bvw[k],
                   Bh[k][:].rearrange("(c p) -> p c", p=128),
                   allow_slow_non_contiguous=True)
        t_bias = S_.tok("dma")
        for k in ["wfT", "ufT", "wiT", "uiT", "wcT", "ucT"]:
            load_w(S_, Wh[k][:, :].rearrange("(c p) n -> p c n", p=128),
                   wstage, wv[k], NCH, D)
        t_wload = S_.tok("ve")
        for ch in range(NCH):
            S_.on("vector",
                  lambda e_, ch=ch: e_.tensor_scalar(
                      c_sb[:, ch, :], ones_t[:, 0:NPOS],
                      bvw["in_c"][:, ch:ch + 1], None, ALU.mult),
                  wait=[t_one, t_bias])
        t_c0 = S_.on("vector", lambda e_: e_.tensor_copy(
            cbf[:, :, :], c_sb[:, :, :]))

        def prep_xhat(ts, ln, rot):
            toks = []
            for half in range(2):
                tile = ts * 2 + half
                hrot = half
                t_x = S_.dma("sync", xres[:, hrot, :],
                             x_hbm[ts][half * 128:(half + 1) * 128, :],
                             wait=[S_.g(("xres", hrot))])
                if ln is ln1:
                    t_ln = ln_tile(S_, ln, tile, xres[:, hrot, :], sq_scr,
                                   var_s[:, :], wait=[t_x])
                else:
                    t_ln = t_x
                t_ap = S_.on("vector",
                             lambda e_, tile=tile, hrot=hrot: e_.tensor_scalar(
                                 xhat_tok[:, hrot, :], xres[:, hrot, :],
                                 ln1["r"][:, tile:tile + 1],
                                 ln1["nmr"][:, tile:tile + 1],
                                 ALU.mult, ALU.add),
                             wait=[t_ln, S_.g(("xht", hrot))])
                S_.setg(("xres", hrot), t_ap)

                def ftp(e_, hrot=hrot):
                    ins = None
                    for cc in range(NCH):
                        ins = e_.matmul(psT[:, cc * 128:(cc + 1) * 128],
                                        xhat_tok[:, hrot, cc * 128:(cc + 1) * 128],
                                        ident_b[:, :], is_transpose=True,
                                        start=(cc == 0), stop=(cc == NCH - 1),
                                        skip_group_check=True)
                    return ins
                t_tp = S_.on("tensor", ftp,
                             wait=[t_ap, t_idb, S_.g("psT")], inc="pe")
                S_.setg(("xht", hrot), t_tp)
                t_ev = S_.on("vector",
                             lambda e_, rot=rot, half=half: e_.tensor_copy(
                                 xFs[:, rot, :, half * 128:(half + 1) * 128],
                                 psT[:, :].rearrange("p (c n) -> p c n",
                                                     c=NCH)),
                             wait=[t_tp])
                S_.setg("psT", t_ev)
                toks.append(t_ev)
            return toks[-1]

        GATES = [("wfT", "ufT", "bf", AF.Sigmoid, gf),
                 ("wiT", "uiT", "bi", AF.Sigmoid, gi),
                 ("wcT", "ucT", "bc", AF.Tanh, gc)]
        t_cup = t_c0
        for ts in range(NSTEP if nstep is None else nstep):
            rot = ts % 2
            t_xf = prep_xhat(ts, ln1, rot)
            for (wk, uk, bk, act, dst) in GATES:
                for mc in range(NCH):
                    psu, key = (psA, "psA") if mc % 2 == 0 else (psB, "psB")
                    pairs = ([(wv[wk][:, kc, mc * 128:(mc + 1) * 128],
                               xFs[:, rot, kc, :]) for kc in range(NCH)] +
                             [(wv[uk][:, kc, mc * 128:(mc + 1) * 128],
                               cbf[:, kc, :]) for kc in range(NCH)])
                    t_mm = mm_chain(S_, psu[:, 0:NPOS], pairs,
                                    wait=[t_wload, t_xf, t_cup, S_.g(key)])
                    t_g = S_.on("scalar",
                                lambda e_, mc=mc, psu=psu, dst=dst, act=act,
                                bk=bk: e_.activation(
                                    dst[:, mc, :], psu[:, 0:NPOS], act,
                                    bias=bvw[bk][:, mc:mc + 1]),
                                wait=[t_mm, t_bias], inc="se")
                    S_.setg(key, t_g)
            t_gates = S_.tok("se")
            S_.on("vector", lambda e_: e_.tensor_tensor(
                gf[:, :, :], gf[:, :, :], c_sb[:, :, :], ALU.mult),
                wait=[t_gates])
            S_.on("vector", lambda e_: e_.tensor_tensor(
                gi[:, :, :], gi[:, :, :], gc[:, :, :], ALU.mult))
            S_.on("vector", lambda e_: e_.tensor_tensor(
                c_sb[:, :, :], gf[:, :, :], gi[:, :, :], ALU.add))
            t_cup = S_.on("vector", lambda e_: e_.tensor_copy(
                cbf[:, :, :], c_sb[:, :, :]))

        t_pe_done = S_.tok("pe")
        load_w(S_, Wh["woT"][:, :].rearrange("(c p) n -> p c n", p=128),
               wstage, wv["woT"], NCH, D, wait=[t_pe_done])
        load_w(S_, Wh["uoT"][:, :].rearrange("(c p) n -> p c n", p=128),
               wstage, wv["uoT"], NCH, D, wait=[t_pe_done])
        t_wo = S_.tok("ve")
        for mc in range(NCH):
            psu, key = (psA, "psA") if mc % 2 == 0 else (psB, "psB")
            pairs = [(wv["uoT"][:, kc, mc * 128:(mc + 1) * 128],
                      cbf[:, kc, :]) for kc in range(NCH)]
            t_mm = mm_chain(S_, psu[:, 0:NPOS], pairs,
                            wait=[t_wo, t_cup, S_.g(key)])
            t_cu = S_.on("vector",
                         lambda e_, mc=mc, psu=psu: e_.tensor_copy(
                             cuo[:, mc, :], psu[:, 0:NPOS]),
                         wait=[t_mm])
            S_.setg(key, t_cu)
        t_cuo = S_.tok("ve")
        for ts in range(NSTEP if nstep is None else nstep):
            rot = ts % 2
            t_xf = prep_xhat(ts, ln2, rot)
            for mc in range(NCH):
                psu, key = (psA, "psA") if mc % 2 == 0 else (psB, "psB")
                pairs = [(wv["woT"][:, kc, mc * 128:(mc + 1) * 128],
                          xFs[:, rot, kc, :]) for kc in range(NCH)]
                t_mm = mm_chain(S_, psu[:, 0:NPOS], pairs,
                                wait=[t_wo, t_xf, t_cuo, S_.g(key)])
                t_add = S_.on("vector",
                              lambda e_, mc=mc, psu=psu: e_.tensor_tensor(
                                  psu[:, 0:NPOS], psu[:, 0:NPOS],
                                  cuo[:, mc, :], ALU.add),
                              wait=[t_mm])
                t_h = S_.on("scalar",
                            lambda e_, mc=mc, psu=psu: e_.activation(
                                hF[:, mc, :], psu[:, 0:NPOS], AF.Tanh,
                                bias=bvw["bo"][:, mc:mc + 1]),
                            wait=[t_bias, t_add, S_.g("hF")], inc="se")
                S_.setg(key, t_h)
            t_hd = S_.tok("se")
            for half in range(2):
                tile = ts * 2 + half
                hrot = half

                def ftp2(e_, half=half):
                    ins = None
                    for cc in range(NCH):
                        ins = e_.matmul(psW[:, cc * 128:(cc + 1) * 128],
                                        hF[:, cc, half * 128:(half + 1) * 128],
                                        ident_f[:, :], is_transpose=True,
                                        start=(cc % 4 == 0),
                                        stop=(cc % 4 == 3),
                                        skip_group_check=True)
                    return ins
                t_tp = S_.on("tensor", ftp2,
                             wait=[t_hd, t_idf, S_.g("psW")], inc="pe")
                t_ln = ln_tile(S_, ln2, tile, psW[:, :], sq_scr, var_s[:, :],
                               wait=[t_tp], from_psum=True)
                t_x = S_.dma("sync", xres[:, hrot, :],
                             x_hbm[ts][half * 128:(half + 1) * 128, :],
                             wait=[S_.g(("xres", hrot))])
                S_.on("vector",
                      lambda e_, tile=tile, hrot=hrot: e_.tensor_scalar(
                          outt[:, hrot, :], psW[:, :],
                          ln2["r"][:, tile:tile + 1],
                          ln2["nmr"][:, tile:tile + 1],
                          ALU.mult, ALU.add),
                      wait=[t_ln, S_.g(("outt", hrot))])
                t_ap = S_.on("vector",
                             lambda e_, hrot=hrot: e_.tensor_tensor(
                                 outt[:, hrot, :], outt[:, hrot, :],
                                 xres[:, hrot, :], ALU.add),
                             wait=[t_x])
                S_.setg("psW", t_ap)
                S_.setg(("xres", hrot), t_ap)
                t_st = S_.dma("gpsimd", y_hbm[ts][half * 128:(half + 1) * 128, :],
                              outt[:, hrot, :], wait=[t_ap], inc="str")
                S_.setg(("outt", hrot), t_st)
            S_.setg("hF", S_.tok("pe"))


def build_fused(sim_debug=False):
    nc = bass.Bass("TRN2", target_bir_lowering=False,
                   detect_race_conditions=not sim_debug)
    io1 = {"x": nc.dram_tensor("x", [B_CORE, S, D], F32, kind="ExternalInput")}
    for k, sh in ATTN_W[True]:
        io1[k] = nc.dram_tensor("a1" + k, sh, F32, kind="ExternalInput")
    io2 = {}
    for k, sh in FFN0_W:
        io2[k] = nc.dram_tensor("f2" + k, sh, F32, kind="ExternalInput")
    io3 = {}
    for k, sh in ATTN_W[False]:
        io3[k] = nc.dram_tensor("a3" + k, sh, F32, kind="ExternalInput")
    io4 = {"y": nc.dram_tensor("y", [NSTEP, NPOS, D], F32,
                               kind="ExternalOutput")}
    for k in FFN1_WN:
        io4[k] = nc.dram_tensor("f4" + k, [D, D], F32, kind="ExternalInput")
    for k in FFN1_BN:
        io4[k] = nc.dram_tensor("f4" + k, [D], F32, kind="ExternalInput")
    x1_i = nc.dram_tensor("x1_i", [B_CORE, S, D], F32)
    x2_i = nc.dram_tensor("x2_i", [B_CORE, S, D], F32)
    cc_in = nc.dram_tensor("cc_in", [8, B_CORE * NPOS * D], F32)
    cc_out = nc.dram_tensor("cc_out", [8, B_CORE * NPOS * D], F32)
    io1["y"] = x1_i
    io2["x"] = x1_i
    io2["y"] = x2_i
    io3["x"] = x2_i
    io3["y"] = cc_in
    io4["x"] = cc_out.ap().rearrange("s (b p n) -> (s b) p n",
                                     b=B_CORE, p=NPOS)

    from concourse.aot_env import allow_ici_all_to_all
    with ExitStack() as ctx, allow_ici_all_to_all():
        S_ = Sched(nc, ctx)
        block = ctx.enter_context(nc.Block())
        S_.set_prefix("a1_")
        emit_attn(nc, S_, io1, mla=True, pfx="a1_")
        S_.barrier("f2_")
        emit_l0ffn(nc, S_, io2, pfx="f2_")
        S_.barrier("a3_")
        bnc = cc_in.ap().rearrange("j (b p n) -> j b p n", b=B_CORE, p=NPOS)

        def yv(bb, tt):
            return bnc[tt // 2, bb, (tt % 2) * 128:(tt % 2) * 128 + 128, :]
        emit_attn(nc, S_, io3, mla=False, pfx="a3_", y_view=yv)
        S_.barrier("cc_")
        S_.on("gpsimd",
              lambda e_: e_.collective_compute(
                  "AllToAll", ALU.bypass,
                  replica_groups=[list(range(8))],
                  ins=[cc_in.ap().opt()], outs=[cc_out.ap().opt()]),
              inc="cc", is_dma=True)
        S_.barrier("f4_")
        emit_l1ffn(nc, S_, io4, pfx="f4_")
        S_.barrier("zz_")
        S_.finalize(block)
    return nc


def kernel(**inp):
    from concourse.aot_env import allow_ici_all_to_all
    ncores = 8
    f = lambda a: np.ascontiguousarray(np.asarray(a, dtype=np.float32))
    tr = lambda a: np.ascontiguousarray(np.asarray(a, dtype=np.float32).T)
    x = f(inp["L0_attn_x"])
    w = {"a1wq": f(inp["L0_attn_W_q"]), "a1wk": f(inp["L0_attn_W_k"]),
         "a1wv": f(inp["L0_attn_W_v"]), "a1wo": f(inp["L0_attn_W_o"]),
         "a1wdn": f(inp["L0_attn_W_down_kv"]),
         "a1wuk": f(inp["L0_attn_W_up_k"]), "a1wuv": f(inp["L0_attn_W_up_v"]),
         "f2gT": tr(inp["L0_ffn_gate_w"]),
         "f2w1T": np.ascontiguousarray(
             np.asarray(inp["L0_ffn_w1"], np.float32).transpose(0, 2, 1)),
         "f2w2T": np.ascontiguousarray(
             np.asarray(inp["L0_ffn_w2"], np.float32).transpose(0, 2, 1)),
         "f2w3T": np.ascontiguousarray(
             np.asarray(inp["L0_ffn_w3"], np.float32).transpose(0, 2, 1)),
         "f2b1T": tr(inp["L0_ffn_b1"]), "f2b2T": tr(inp["L0_ffn_b2"]),
         "f2b3T": tr(inp["L0_ffn_b3"]),
         "f2sw1T": tr(inp["L0_ffn_sw1"]), "f2sw2T": tr(inp["L0_ffn_sw2"]),
         "f2sw3T": tr(inp["L0_ffn_sw3"]),
         "a3wqkv": tr(inp["L1_attn_W_qkv"]), "a3wo": tr(inp["L1_attn_W_o"]),
         "f4wfT": tr(inp["L1_ffn_wf"]), "f4ufT": tr(inp["L1_ffn_uf"]),
         "f4wiT": tr(inp["L1_ffn_wi"]), "f4uiT": tr(inp["L1_ffn_ui"]),
         "f4wcT": tr(inp["L1_ffn_wc"]), "f4ucT": tr(inp["L1_ffn_uc"]),
         "f4woT": tr(inp["L1_ffn_wo"]), "f4uoT": tr(inp["L1_ffn_uo"]),
         "f4bf": f(inp["L1_ffn_bf"]), "f4bi": f(inp["L1_ffn_bi"]),
         "f4bc": f(inp["L1_ffn_bc"]), "f4bo": f(inp["L1_ffn_bo"]),
         "f4in_c": f(inp["L1_ffn_in_c"])}
    nc = build_fused()
    in_maps = [{"x": x[2 * c:2 * c + 2], **w} for c in range(ncores)]
    with allow_ici_all_to_all():
        res = run_bass_kernel_spmd(nc, in_maps, core_ids=list(range(ncores)))
    out = np.concatenate([res.results[c]["y"] for c in range(ncores)], axis=1)
    return out


# version 12
# speedup vs baseline: 1.0507x; 1.0148x over previous
import math
from contextlib import ExitStack

import numpy as np

import concourse.bass as bass
import concourse.mybir as mybir
from concourse.bass_utils import run_bass_kernel_spmd

F32 = mybir.dt.float32
BF16 = mybir.dt.bfloat16
AF = mybir.ActivationFunctionType
ALU = mybir.AluOpType
AX = mybir.AxisListType

D = 1024
S = 2048
B_CORE = 2
NCH = 8
NT = S // 128
H = 16
HD = 64
DL = 256
LAM = 0.01
EPS = 1e-5


class Sched:
    ENGINES = ("sync", "tensor", "vector", "scalar", "gpsimd")

    LOCAL_SEMS = ("ve", "se", "gp", "pe", "dma", "dmb", "str")

    def __init__(self, nc, ctx):
        self.nc = nc
        self.ctx = ctx
        self.q = {e: [] for e in self.ENGINES}
        self._sems = {}
        self.guard = {}
        self.pfx = ""

    def set_prefix(self, p):
        self.pfx = p

    def _resolve(self, name):
        return (self.pfx + name) if name in self.LOCAL_SEMS else name

    def _sem(self, name):
        if name not in self._sems:
            self._sems[name] = [self.ctx.enter_context(self.nc.semaphore(name)), 0]
        return self._sems[name]

    def barrier(self, new_pfx=None, exclude=()):
        if new_pfx is not None:
            self.pfx = new_pfx
        toks = [(nm, ent[1]) for nm, ent in self._sems.items()
                if nm not in exclude]
        toks = [t for t in toks if t[1] > 0]
        for eng in self.ENGINES:
            if eng in ("sync", "tensor"):
                self.on(eng, lambda e: e.nop(), wait=toks,
                        inc=("pe" if eng == "tensor" else None), is_dma=True)
            else:
                self.on(eng,
                        lambda e: (e.engine_nop() if hasattr(e, "engine_nop")
                                   else e.nop()),
                        wait=toks)
        self.guard.clear()

    CHAINED = {"vector": "ve", "scalar": "se", "gpsimd": "gp"}

    def on(self, eng, fn, wait=(), inc=None, inc_by=1, is_dma=False):
        ch = self.CHAINED.get(eng)
        wait = list(wait)
        if inc is not None:
            inc = self._resolve(inc)
        if ch is not None and not is_dma:
            ch = self._resolve(ch)
            wait.append(self.tok(ch))
            if inc is None:
                inc, inc_by = ch, 1
            elif inc != ch:
                raise ValueError(f"{eng} ops must inc {ch}")
        tok = None
        if inc is not None:
            entry = self._sem(inc)
            entry[1] += inc_by
            tok = (inc, entry[1])
        wmax = {}
        for w in wait:
            if w is None:
                continue
            s, v = w
            if v > wmax.get(s, -1):
                wmax[s] = v
        whandles = [(self._sems[s][0], v) for s, v in wmax.items()]
        ihandle = self._sems[inc][0] if inc is not None else None

        def thunk(e, fn=fn, whandles=whandles, ihandle=ihandle, inc_by=inc_by):
            for (h, v) in whandles:
                e.wait_ge(h, v)
            ins = fn(e)
            if ihandle is not None:
                ins.then_inc(ihandle, inc_by)
        self.q[eng].append(thunk)
        return tok

    def dma(self, eng, out, in_, wait=(), inc="dma", **kw):
        return self.on(eng, lambda e: e.dma_start(out=out, in_=in_, **kw),
                       wait=wait, inc=inc, inc_by=16, is_dma=True)

    def tok(self, name):
        name = self._resolve(name)
        entry = self._sem(name)
        return (name, entry[1])

    def g(self, key):
        return self.guard.get(key)

    def setg(self, key, tok):
        self.guard[key] = tok

    def finalize(self, block):
        for name in self.ENGINES:
            thunks = self.q[name]
            if not thunks:
                continue

            def runner(e, ts=tuple(thunks)):
                for t in ts:
                    t(e)
            getattr(block, name)(runner)


def mm_chain(S_, psum_ap, pairs, wait=(), inc="pe", start=True, stop=True):
    n = len(pairs)

    def fn(e):
        ins = None
        for i, (l, r) in enumerate(pairs):
            ins = e.matmul(psum_ap, l, r,
                           start=(start and i == 0),
                           stop=(stop and i == n - 1),
                           skip_group_check=True)
        return ins
    return S_.on("tensor", fn, wait=wait, inc=inc)


def make_identity(S_, ident_ap):
    t0 = S_.on("gpsimd", lambda e: e.memset(ident_ap, 0.0), inc="gp")
    return S_.on("gpsimd",
                 lambda e: e.affine_select(out=ident_ap, in_=ident_ap,
                                           compare_op=ALU.not_equal, fill=1.0,
                                           base=0, pattern=[[-1, 128]],
                                           channel_multiplier=1),
                 wait=[t0], inc="gp")


def load_w_half(S_, w_dram, stages, wbf, chunks, dout, wait=(), into_off=0):
    wv = w_dram[:, :].rearrange("(c p) n -> p c n", p=128)
    t_c = None
    for c in range(chunks):
        half = c % 2
        q, sem = ("sync", "dma") if half == 0 else ("gpsimd", "dmb")
        sv = stages[half][:, 0:dout]
        t_d = S_.dma(q, sv, wv[:, c, :],
                     wait=list(wait) + [S_.g(("wsh", half))], inc=sem)
        t_c = S_.on("scalar",
                    lambda e, c=c, sv=sv: e.copy(
                        wbf[:, into_off + c, 0:dout], sv),
                    wait=[t_d], inc="se")
        S_.setg(("wsh", half), t_c)
    return t_c


def ln_tile(S_, st, t, x_ap, sq_scr, var_col, wait, from_psum=False):
    if not isinstance(sq_scr, bass.AP):
        sq_scr = sq_scr[:, :]
    if from_psum:
        t1 = S_.on("scalar",
                   lambda e: e.activation(sq_scr, x_ap, AF.Copy,
                                          accum_out=st["sum"][:, t:t + 1]),
                   wait=wait, inc="se")
    else:
        t1 = S_.on("vector",
                   lambda e: e.reduce_sum(st["sum"][:, t:t + 1], x_ap,
                                          axis=AX.X),
                   wait=wait, inc="ve")
    t2 = S_.on("scalar",
               lambda e: e.activation(sq_scr, x_ap, AF.Square,
                                      accum_out=st["sumq"][:, t:t + 1]),
               wait=wait, inc="se")

    S_.on("vector",
          lambda e: e.tensor_scalar(st["negm"][:, t:t + 1], st["sum"][:, t:t + 1],
                                    -1.0 / D, None, ALU.mult),
          wait=[t1])
    S_.on("vector",
          lambda e: e.tensor_scalar(var_col, st["negm"][:, t:t + 1],
                                    st["negm"][:, t:t + 1], -EPS,
                                    ALU.mult, ALU.add))
    t3 = S_.on("vector",
               lambda e: e.scalar_tensor_tensor(var_col, st["sumq"][:, t:t + 1],
                                                1.0 / D, var_col, ALU.mult,
                                                ALU.subtract),
               wait=[t2])
    t4 = S_.on("scalar",
               lambda e: e.activation(var_col, var_col, AF.Sqrt),
               wait=[t3], inc="se")
    S_.on("vector", lambda e: e.reciprocal(st["r"][:, t:t + 1], var_col),
          wait=[t4])
    return S_.on("vector",
                 lambda e: e.tensor_tensor(st["nmr"][:, t:t + 1],
                                           st["negm"][:, t:t + 1],
                                           st["r"][:, t:t + 1], ALU.mult))


def new_ln(nc, ctx, name, ntiles=NT):
    sb = lambda n, sh: ctx.enter_context(nc.sbuf_tensor(name + n, sh, F32))
    return {"sum": sb("_s", [128, ntiles]), "sumq": sb("_q", [128, ntiles]),
            "negm": sb("_m", [128, ntiles]), "r": sb("_r", [128, ntiles]),
            "nmr": sb("_n", [128, ntiles])}


ATTN_W = {True: [("wq", [D, D]), ("wk", [D, D]), ("wv", [D, D]),
                 ("wo", [D, D]), ("wdn", [D, DL]),
                 ("wuk", [DL, D]), ("wuv", [DL, D])],
          False: [("wqkv", [D, D]), ("wo", [D, D])]}


def build_attn(mla, sim_debug=False, phases=5, nbatch=None, debug_dump=()):
    nc = bass.Bass("TRN2", target_bir_lowering=False,
                   detect_race_conditions=not sim_debug)
    io = {"x": nc.dram_tensor("x", [B_CORE, S, D], F32, kind="ExternalInput"),
          "y": nc.dram_tensor("y", [B_CORE, S, D], F32, kind="ExternalOutput")}
    for k, sh in ATTN_W[mla]:
        io[k] = nc.dram_tensor(k, sh, F32, kind="ExternalInput")
    with ExitStack() as ctx:
        S_ = Sched(nc, ctx)
        block = ctx.enter_context(nc.Block())
        emit_attn(nc, S_, io, mla, phases=phases, nbatch=nbatch,
                  debug_dump=debug_dump)
        S_.on("gpsimd", lambda e_: e_.engine_nop(), wait=[S_.tok("str")])
        S_.finalize(block)
    return nc


def emit_attn(nc, S_, io, mla, pfx="", phases=5, nbatch=None, debug_dump=(),
              y_view=None):
    x_hbm = io["x"]
    y_hbm = io["y"]
    W = io

    with ExitStack() as ctx:
        def e(cm):
            return ctx.enter_context(cm)
        xres = e(nc.sbuf_tensor(pfx + "xres", [128, 2, D], F32))
        sq_scr = e(nc.sbuf_tensor(pfx + "sq_scr", [128, D], BF16))
        xhat_tok = e(nc.sbuf_tensor(pfx + "xhat_tok", [128, 2, D], BF16))
        xhatF = e(nc.sbuf_tensor(pfx + "xhatF", [128, NCH, S], BF16))
        qF = e(nc.sbuf_tensor(pfx + "qF", [128, NCH, S], BF16))
        poolA = e(nc.sbuf_tensor(pfx + "poolA", [128, 20736], BF16))
        latF = poolA[:, 0:4096].rearrange("p (c n) -> p c n", c=2)
        vTo = poolA[:, 4096:20736].rearrange("p (t h c) -> p t h c", t=NT, h=H)
        poolB = e(nc.sbuf_tensor(pfx + "poolB", [128, NCH, S], BF16))
        kvF = poolB
        OF = poolB
        wstage = e(nc.sbuf_tensor(pfx + "wstage", [128, 2, D], F32))
        wbf = e(nc.sbuf_tensor(pfx + "wbf", [128, NCH, D], BF16))
        ident_bf = e(nc.sbuf_tensor(pfx + "ident_bf", [128, 128], BF16))
        ident_f = e(nc.sbuf_tensor(pfx + "ident_f", [128, 128], F32))
        ones1 = e(nc.sbuf_tensor(pfx + "ones1", [1, 64], F32))
        ln1 = new_ln(nc, ctx, pfx + "ln1")
        ln2 = new_ln(nc, ctx, pfx + "ln2")
        var_s = e(nc.sbuf_tensor(pfx + "var_s", [128, 1], F32))
        decay = e(nc.sbuf_tensor(pfx + "decay", [128, 3968], F32)) if mla else None
        expS = e(nc.sbuf_tensor(pfx + "expS", [128, 2, 1024], BF16))
        rbc = e(nc.sbuf_tensor(pfx + "rbc", [64, 1024], F32))
        aos = e(nc.sbuf_tensor(pfx + "aos", [128, 2, D], F32))
        outt = aos
        psT = e(nc.psum_tensor(pfx + "psT", [128, 1024], BF16))
        psA = e(nc.psum_tensor(pfx + "psA", [128, 512], F32))
        psB = e(nc.psum_tensor(pfx + "psB", [128, 512], F32))
        psS = e(nc.psum_tensor(pfx + "psS", [128, 1024], F32))
        psAVe = e(nc.psum_tensor(pfx + "psAVe", [65, 512], F32))
        psAVo = e(nc.psum_tensor(pfx + "psAVo", [65, 512], F32))
        psBC = e(nc.psum_tensor(pfx + "psBC", [64, 512], F32))

        t_id = make_identity(S_, ident_bf[:, :])
        t_idf = make_identity(S_, ident_f[:, :])
        t_on = S_.on("gpsimd", lambda e_: e_.memset(ones1[:, :], 1.0), inc="gp")
        t_dec = None
        if mla:
            t_i = S_.on("gpsimd",
                        lambda e_: e_.iota(decay[:, :], [[1, 3968]], base=-1920,
                                           channel_multiplier=-1,
                                           allow_small_or_imprecise_dtypes=True),
                        inc="gp")
            t_ab = S_.on("scalar",
                         lambda e_: e_.activation(decay[:, :], decay[:, :], AF.Abs),
                         wait=[t_i], inc="se")
            t_dec = S_.on("scalar",
                          lambda e_: e_.activation(decay[:, :], decay[:, :],
                                                   AF.Exp, scale=-LAM),
                          wait=[t_ab], inc="se")

        for b in range(B_CORE if nbatch is None else nbatch):
            emit_attn_batch(S_, mla, b, x_hbm, y_hbm, W, locals(), phases,
                            y_view)

        if phases < 4.8 or (nbatch is not None and nbatch < B_CORE):
            t_z = S_.on("vector",
                        lambda e_: e_.tensor_copy(aos[:, 0, :], xres[:, 0, :]),
                        wait=[S_.tok("ve"), S_.tok("se"), S_.tok("pe"),
                              S_.tok("gp")], inc="ve")
            S_.dma("gpsimd", y_hbm[0][0:128, :], aos[:, 0, :], wait=[t_z],
                   inc="str")
        if debug_dump:
            allw = [S_.tok(s) for s in ("ve", "se", "pe", "gp", "dma")]
            Lx = locals()
            prev = None
            for nm in debug_dump:
                if "." in nm:
                    a_, b_ = nm.split(".")
                    ap = Lx[a_][b_]
                    nm = a_ + "_" + b_
                else:
                    ap = Lx[nm]
                if not isinstance(ap, bass.AP):
                    ap = ap[tuple(slice(None) for _ in ap.shape)]
                if len(ap.shape) > 2:
                    dims = " ".join(f"a{i}" for i in range(len(ap.shape) - 1))
                    ap = ap.rearrange(f"p {dims} -> p ({dims})")
                ncols = ap.shape[1]
                dbg = nc.dram_tensor(pfx + "dbg_" + nm, [128, ncols], F32,
                                     kind="ExternalOutput")
                for c0 in range(0, ncols, 2048):
                    cw = min(2048, ncols - c0)
                    t_c = S_.on("vector",
                                lambda e, ap=ap, c0=c0, cw=cw: e.tensor_copy(
                                    aos[:, 0, 0:cw] if cw < 2048 else
                                    aos[:, :, :].rearrange("p a b -> p (a b)")[:, 0:cw],
                                    ap[:, c0:c0 + cw]),
                                wait=allw + ([prev] if prev else []), inc="ve")
                    prev = S_.dma("gpsimd", dbg[:, c0:c0 + cw],
                                  aos[:, 0, 0:cw] if cw < 2048 else
                                  aos[:, :, :].rearrange("p a b -> p (a b)")[:, 0:cw],
                                  wait=[t_c], inc="str")


def emit_attn_batch(S_, mla, b, x_hbm, y_hbm, W, L, phases=5,
                    y_view=None):
    xres, sq_scr, xhat_tok = L["xres"], L["sq_scr"], L["xhat_tok"]
    xhatF, qF, latF, vTo = L["xhatF"], L["qF"], L["latF"], L["vTo"]
    kvF, OF = L["kvF"], L["OF"]
    wstage, wbf = L["wstage"], L["wbf"]
    ident_bf, ident_f, ones1 = L["ident_bf"], L["ident_f"], L["ones1"]
    ln1, ln2, var_s = L["ln1"], L["ln2"], L["var_s"]
    decay, expS, rbc = L["decay"], L["expS"], L["rbc"]
    aos, outt = L["aos"], L["outt"]
    psT, psA, psB, psS = L["psT"], L["psA"], L["psB"], L["psS"]
    psAVe, psAVo, psBC = L["psAVe"], L["psAVo"], L["psBC"]
    t_id, t_idf, t_on, t_dec = L["t_id"], L["t_idf"], L["t_on"], L["t_dec"]
    kRF = xhatF if mla else qF
    x_b = x_hbm[b]
    y_b = y_hbm[b]
    if y_view is None:
        y_view = lambda bb, tt: y_hbm[bb][tt * 128:(tt + 1) * 128, :]
    SCALE = (1.0 / 32.0) if mla else (1.0 / 8.0)

    for t in range(NT):
        rot = t % 2
        t_x = S_.dma("sync", xres[:, rot, :], x_b[t * 128:(t + 1) * 128, :],
                     wait=[S_.g(("xres", rot))])
        t_ln = ln_tile(S_, ln1, t, xres[:, rot, :], sq_scr[:, :], var_s[:, :],
                       wait=[t_x])
        t_ap = S_.on("vector",
                     lambda e, t=t, rot=rot: e.tensor_scalar(
                         xhat_tok[:, rot, :], xres[:, rot, :],
                         ln1["r"][:, t:t + 1], ln1["nmr"][:, t:t + 1],
                         ALU.mult, ALU.add),
                     wait=[t_ln, S_.g(("xh", rot))], inc="ve")
        S_.setg(("xres", rot), t_ap)

        def ftp(e, rot=rot):
            ins = None
            for c in range(NCH):
                ins = e.matmul(psT[:, c * 128:(c + 1) * 128],
                               xhat_tok[:, rot, c * 128:(c + 1) * 128],
                               ident_bf[:, :], is_transpose=True,
                               start=(c == 0), stop=(c == NCH - 1),
                               skip_group_check=True)
            return ins
        t_tp = S_.on("tensor", ftp, wait=[t_ap, t_id, S_.g("psT")], inc="pe")
        S_.setg(("xh", rot), t_tp)
        t_ev = S_.on("vector",
                     lambda e, t=t: e.tensor_copy(
                         xhatF[:, :, t * 128:(t + 1) * 128], psT[:, :].rearrange(
                             "p (c n) -> p c n", c=NCH)),
                     wait=[t_tp], inc="ve")
        S_.setg("psT", t_ev)
    t_xhatF = S_.tok("ve")
    if phases < 2:
        return

    def proj(dst, n_dout, wt_tok, src, src_nc, extra=(), accum=False):
        toks = []
        for mc in range(n_dout // 128):
            for nt4 in range(S // 512):
                psu, key = (psA, "psA") if (mc * 4 + nt4) % 2 == 0 else (psB, "psB")
                pairs = [(wbf[:, kc, mc * 128:(mc + 1) * 128],
                          src[:, kc, nt4 * 512:(nt4 + 1) * 512])
                         for kc in range(src_nc)]
                t_mm = mm_chain(S_, psu[:, :], pairs,
                                wait=[wt_tok, S_.g(key)] + list(extra))
                if accum:
                    t_e = S_.on("vector",
                                lambda e, dst=dst, mc=mc, nt4=nt4, psu=psu:
                                e.tensor_tensor(
                                    dst[:, mc, nt4 * 512:(nt4 + 1) * 512],
                                    dst[:, mc, nt4 * 512:(nt4 + 1) * 512],
                                    psu[:, :], ALU.add),
                                wait=[t_mm], inc="ve")
                else:
                    t_e = S_.on("vector",
                                lambda e, dst=dst, mc=mc, nt4=nt4, psu=psu:
                                e.tensor_copy(
                                    dst[:, mc, nt4 * 512:(nt4 + 1) * 512],
                                    psu[:, :]),
                                wait=[t_mm], inc="ve")
                S_.setg(key, t_e)
                toks.append(t_e)
        return toks

    if mla:
        stg = (wstage[:, 0, :], wstage[:, 1, :])
        t_w = load_w_half(S_, W["wq"], stg, wbf, NCH, D,
                          wait=[S_.g("wbf")])
        tq = proj(qF, D, t_w, xhatF, NCH, extra=[t_xhatF, S_.g("qF")])
        S_.setg("wbf", S_.tok("pe"))
        t_w = load_w_half(S_, W["wk"], stg, wbf, NCH, D, wait=[S_.g("wbf")])
        tk_ = proj(kvF, D, t_w, xhatF, NCH, extra=[S_.g("kvF")])
        S_.setg("wbf", S_.tok("pe"))
        t_w = load_w_half(S_, W["wv"], stg, wbf, NCH, D, wait=[S_.g("wbf")])
        tv = proj(kvF, D, t_w, xhatF, NCH, extra=[tk_[-1]], accum=True)
        S_.setg("wbf", S_.tok("pe"))
        t_w = load_w_half(S_, W["wdn"], stg, wbf, NCH, DL, wait=[S_.g("wbf")])
        tl = proj(latF, DL, t_w, kvF, NCH, extra=[tv[-1]])
        S_.setg("wbf", S_.tok("pe"))
        t_w = load_w_half(S_, W["wuk"], stg, wbf, 2, D, wait=[S_.g("wbf")])
        tkr = proj(kRF, D, t_w, latF, 2, extra=[tl[-1]])
        S_.setg("wbf", S_.tok("pe"))
        t_w = load_w_half(S_, W["wuv"], stg, wbf, 2, D, wait=[S_.g("wbf")])
        for tt in range(NT):
            for nh in range(2):
                psu, key = (psA, "psA") if (tt * 2 + nh) % 2 == 0 else (psB, "psB")
                pairs = [(latF[:, kc, tt * 128:(tt + 1) * 128],
                          wbf[:, kc, nh * 512:(nh + 1) * 512]) for kc in range(2)]
                t_mm = mm_chain(S_, psu[:, :], pairs,
                                wait=[t_w, tl[-1], S_.g(key)])
                t_e = S_.on("vector",
                            lambda e, tt=tt, nh=nh, psu=psu: e.tensor_copy(
                                vTo[:, tt, nh * 8:(nh + 1) * 8, 0:64],
                                psu[:, :].rearrange("p (h c) -> p h c", h=8)),
                            wait=[t_mm], inc="ve")
                S_.setg(key, t_e)
        t_keys_done = tkr[-1]
        t_qdone = tq[-1]
    else:
        stg = (wstage[:, 0, :], wstage[:, 1, :])
        t_w = load_w_half(S_, W["wqkv"], stg, wbf, NCH, D, wait=[S_.g("wbf")])
        tq = proj(qF, D, t_w, xhatF, NCH, extra=[t_xhatF, S_.g("qF")])
        S_.setg("wbf", S_.tok("pe"))
        for tt in range(NT):
            def ftv(e, tt=tt):
                ins = None
                for c in range(NCH):
                    ins = e.matmul(psT[:, c * 128:(c + 1) * 128],
                                   qF[:, c, tt * 128:(tt + 1) * 128],
                                   ident_bf[:, :], is_transpose=True,
                                   start=(c == 0), stop=(c == NCH - 1),
                                   skip_group_check=True)
                return ins
            t_tp = S_.on("tensor", ftv,
                         wait=[tq[-1], t_id, S_.g("psT")], inc="pe")
            t_e = S_.on("vector",
                        lambda e, tt=tt: e.tensor_copy(
                            vTo[:, tt, :, 0:64],
                            psT[:, :].rearrange("p (h c) -> p h c", h=H)),
                        wait=[t_tp], inc="ve")
            S_.setg("psT", t_e)
        t_keys_done = tq[-1]
        t_qdone = tq[-1]
    t_vto = S_.tok("ve")
    t_ones = S_.on("gpsimd", lambda e: e.memset(vTo[:, :, :, 64:65], 1.0),
                   wait=[t_vto], inc="gp")

    if phases < 3:
        return
    sbank = {(0, 0): (psS[:, 0:512], "psS0"),
             (0, 1): (psS[:, 512:1024], "psS1"),
             (1, 0): (psA[:, :], "psA"), (1, 1): (psB[:, :], "psB")}
    for pr in range(NCH):
        for qc in range(4):
            q0 = qc * 512
            t_es = {}

            def emit_score_chain(half, tki, pr=pr, q0=q0, t_es=t_es):
                k0 = tki * 128
                rot = tki % 2
                bank, bkey = sbank[(half, rot)]
                p0 = half * 64
                t_s = S_.on("tensor",
                            lambda e, bank=bank, p0=p0, k0=k0: e.matmul(
                                bank, kRF[p0:p0 + 64, pr, k0:k0 + 128],
                                qF[p0:p0 + 64, pr, q0:q0 + 512],
                                start=True, stop=True, skip_group_check=True),
                            wait=[t_keys_done, t_qdone, S_.g(bkey)], inc="pe")
                if mla:
                    dv = decay[:, q0 - k0 + 1920:q0 - k0 + 1920 + 512]
                    t_d = S_.on("vector",
                                lambda e, bank=bank, dv=dv: e.tensor_tensor(
                                    bank, bank, dv, ALU.mult),
                                wait=[t_s, t_dec], inc="ve")
                else:
                    t_d = t_s
                t_e = S_.on("scalar",
                            lambda e, bank=bank, rot=rot, half=half:
                            e.activation(
                                expS[:, rot, half * 512:half * 512 + 512],
                                bank, AF.Exp, scale=SCALE),
                            wait=[t_d, S_.g(("expS", half, rot))], inc="se")
                S_.setg(bkey, t_e)
                t_es[(half, tki)] = t_e

            def emit_av(half, tki, pr=pr, t_es=t_es):
                rot = tki % 2
                psu = psAVe if half == 0 else psAVo
                wait_av = [t_es[(half, tki)], t_ones]
                if tki == 0:
                    wait_av += [S_.g("psAV")]
                t_av = S_.on("tensor",
                             lambda e, psu=psu, tki=tki, rot=rot, half=half:
                             e.matmul(psu[:, :], vTo[:, tki, 2 * pr + half, :],
                                      expS[:, rot, half * 512:half * 512 + 512],
                                      start=(tki == 0), stop=(tki == NT - 1),
                                      skip_group_check=True),
                             wait=wait_av, inc="pe")
                S_.setg(("expS", half, rot), t_av)
                return t_av

            for tki in range(NT):
                emit_score_chain(0, tki)
                emit_score_chain(1, tki)
                if tki > 0:
                    emit_av(0, tki - 1)
                    emit_av(1, tki - 1)
            emit_av(0, NT - 1)
            t_av = emit_av(1, NT - 1)
            t_rc = S_.on("vector",
                         lambda e: (e.reciprocal(rbc[0:1, 0:512],
                                                 psAVe[64:65, :]),
                                    e.reciprocal(rbc[0:1, 512:1024],
                                                 psAVo[64:65, :]))[-1],
                         wait=[t_av], inc="ve")
            t_b1 = S_.on("tensor",
                         lambda e: e.matmul(psBC[:, :], ones1[:, :],
                                            rbc[0:1, 0:512], start=True,
                                            stop=True, skip_group_check=True),
                         wait=[t_rc, t_on, S_.g("psBC")], inc="pe")
            t_c1 = S_.on("scalar",
                         lambda e: e.copy(rbc[:, 0:512], psBC[:, :]),
                         wait=[t_b1, S_.g("rbc")], inc="se")
            t_b2 = S_.on("tensor",
                         lambda e: e.matmul(psBC[:, :], ones1[:, :],
                                            rbc[0:1, 512:1024], start=True,
                                            stop=True, skip_group_check=True),
                         wait=[t_c1], inc="pe")
            t_c2 = S_.on("scalar",
                         lambda e: e.copy(rbc[:, 512:1024], psBC[:, :]),
                         wait=[t_b2], inc="se")
            S_.setg("psBC", t_c2)

            def fnrm(e, pr=pr, qc=qc):
                e.tensor_tensor(OF[0:64, pr, qc * 512:(qc + 1) * 512],
                                psAVe[0:64, :], rbc[:, 0:512], ALU.mult)
                return e.tensor_tensor(OF[64:128, pr, qc * 512:(qc + 1) * 512],
                                       psAVo[0:64, :], rbc[:, 512:1024],
                                       ALU.mult)
            t_nm = S_.on("vector", fnrm, wait=[t_c1, t_c2], inc="ve")
            S_.setg("psAV", t_nm)
            S_.setg("rbc", t_nm)
    t_attn = S_.tok("ve")

    if phases < 4:
        return
    S_.setg("wbf", S_.tok("pe"))
    t_wo = load_w_half(S_, W["wo"], stg, wbf, NCH, D, wait=[S_.g("wbf")])
    for t in range(NT):
        rot = t % 2
        evs = []
        for half in range(2):
            psu, key = (psA, "psA") if half == 0 else (psB, "psB")

            def fwo(e, half=half, t=t, psu=psu):
                ins = None
                for mci in range(4):
                    mc = half * 4 + mci
                    for kc in range(NCH):
                        ins = e.matmul(psu[:, mci * 128:(mci + 1) * 128],
                                       wbf[:, kc, mc * 128:(mc + 1) * 128],
                                       OF[:, kc, t * 128:(t + 1) * 128],
                                       start=(kc == 0 and mci == 0),
                                       stop=(kc == NCH - 1 and mci == 3),
                                       skip_group_check=True)
                return ins
            t_mm = S_.on("tensor", fwo,
                         wait=[t_wo, t_attn, S_.g(key)], inc="pe")
            t_ev = S_.on("vector",
                         lambda e, half=half, rot=rot, psu=psu: e.tensor_copy(
                             aos[:, rot, half * 512:(half + 1) * 512],
                             psu[:, :]),
                         wait=[t_mm, S_.g(("aos", rot))], inc="ve")
            S_.setg(key, t_ev)
            evs.append(t_ev)
        if phases < 4.2:
            S_.setg(("aos", rot), evs[-1])
            continue

        def ftp2(e, rot=rot):
            ins = None
            for c in range(NCH):
                ins = e.matmul(psS[:, c * 128:(c + 1) * 128],
                               aos[:, rot, c * 128:(c + 1) * 128],
                               ident_f[:, :], is_transpose=True,
                               start=(c % 4 == 0), stop=(c % 4 == 3),
                               skip_group_check=True)
            return ins
        t_tp = S_.on("tensor", ftp2, wait=evs + [t_idf, S_.g("psS")], inc="pe")
        S_.setg(("aos", rot), t_tp)
        if phases < 4.4:
            S_.setg("psS", t_tp)
            continue
        t_ln = ln_tile(S_, ln2, t, psS[:, :], sq_scr[:, :], var_s[:, :],
                       wait=[t_tp], from_psum=True)
        if phases < 4.6:
            S_.setg("psS", t_ln)
            continue
        t_x = S_.dma("sync", xres[:, rot, :], x_b[t * 128:(t + 1) * 128, :],
                     wait=[S_.g(("xres", rot))])

        S_.on("vector",
              lambda e, t=t, rot=rot: e.tensor_scalar(
                  outt[:, rot, :], psS[:, :],
                  ln2["r"][:, t:t + 1], ln2["nmr"][:, t:t + 1],
                  ALU.mult, ALU.add),
              wait=[t_ln, S_.g(("outt", rot))])
        t_ap = S_.on("vector",
                     lambda e, rot=rot: e.tensor_tensor(
                         outt[:, rot, :], outt[:, rot, :],
                         xres[:, rot, :], ALU.add),
                     wait=[t_x])
        S_.setg("psS", t_ap)
        S_.setg(("xres", rot), t_ap)
        if phases < 4.8:
            S_.setg(("outt", rot), t_ap)
            S_.setg(("aos", rot), t_ap)
            continue
        t_st = S_.dma("gpsimd", y_view(b, t), outt[:, rot, :],
                      wait=[t_ap], inc="str")
        S_.setg(("outt", rot), t_st)
        S_.setg(("aos", rot), t_st)
    S_.setg("qF", S_.tok("pe"))
    S_.setg("kvF", S_.tok("pe"))


def load_w(S_, wv, wstage, wdst, chunks, dout, wait=(), stage_cols=1024):
    per = max(1, stage_cols // dout)
    t_c = None
    c0 = 0
    while c0 < chunks:
        cs = min(per, chunks - c0)
        sview = wstage[:, 0:cs * dout].rearrange("p (c n) -> p c n", c=cs)
        t_d = S_.dma("sync", sview, wv[:, c0:c0 + cs, :],
                     wait=list(wait) + [S_.g("wstage")])
        t_c = S_.on("vector",
                    lambda e, c0=c0, cs=cs, sview=sview: e.tensor_copy(
                        wdst[:, c0:c0 + cs, :], sview),
                    wait=[t_d])
        S_.setg("wstage", t_c)
        c0 += cs
    return t_c


def load_w2(S_, wv, stA, stB, wdst, chunks, wait=()):
    t_c = None
    for c in range(chunks):
        ap, key, q, sem = (stA, "wsA", "sync", "dma") if c % 2 == 0 else \
            (stB, "wsB", "gpsimd", "dmb")
        t_d = S_.dma(q, ap, wv[:, c, :], wait=list(wait) + [S_.g(key)],
                     inc=sem)
        t_c = S_.on("scalar",
                    lambda e, c=c, ap=ap: e.copy(wdst[:, c, :], ap),
                    wait=[t_d], inc="se")
        S_.setg(key, t_c)
    return t_c


FFN0_W = [("gT", [D, 8]), ("w1T", [8, D, D]), ("w2T", [8, D, D]),
          ("w3T", [8, D, D]), ("b1T", [D, 8]), ("b2T", [D, 8]),
          ("b3T", [D, 8]), ("sw1T", [D, 2 * D]), ("sw2T", [D, 2 * D]),
          ("sw3T", [2 * D, D])]


def build_l0ffn(sim_debug=False, nbatch=None, debug_dump=()):
    nc = bass.Bass("TRN2", target_bir_lowering=False,
                   detect_race_conditions=not sim_debug)
    io = {"x": nc.dram_tensor("x", [B_CORE, S, D], F32, kind="ExternalInput"),
          "y": nc.dram_tensor("y", [B_CORE, S, D], F32, kind="ExternalOutput")}
    for k, sh in FFN0_W:
        io[k] = nc.dram_tensor(k, sh, F32, kind="ExternalInput")
    with ExitStack() as ctx:
        S_ = Sched(nc, ctx)
        block = ctx.enter_context(nc.Block())
        emit_l0ffn(nc, S_, io, nbatch=nbatch, debug_dump=debug_dump)
        S_.on("gpsimd", lambda e_: e_.engine_nop(), wait=[S_.tok("str")])
        S_.finalize(block)
    return nc


def emit_l0ffn(nc, S_, io, pfx="", nbatch=None, debug_dump=()):
    x_hbm = io["x"]
    y_hbm = io["y"]
    gT_h, w1_h, w2_h, w3_h = io["gT"], io["w1T"], io["w2T"], io["w3T"]
    b1_h, b2_h, b3_h = io["b1T"], io["b2T"], io["b3T"]
    sw1_h, sw2_h, sw3_h = io["sw1T"], io["sw2T"], io["sw3T"]

    with ExitStack() as ctx:
        def e(cm):
            return ctx.enter_context(cm)
        AR = 16384 + 24576 + 4096 + 4096 + 2048 + 512 + 1024 + 4096 + 512 + 512
        arena = e(nc.sbuf_tensor(pfx + "arena", [128, AR], BF16))
        off = [0]

        def carve(n):
            o = off[0]
            off[0] += n
            return o
        o_xsF = carve(16384)
        xhatF = arena[:, o_xsF:o_xsF + 16384].rearrange("p (c n) -> p c n", c=NCH)
        o_w = carve(24576)
        wr1 = arena[:, o_w:o_w + 8192].rearrange("p (c n) -> p c n", c=NCH)
        wr3 = arena[:, o_w + 8192:o_w + 16384].rearrange("p (c n) -> p c n", c=NCH)
        wr2 = arena[:, o_w + 16384:o_w + 24576].rearrange("p (c n) -> p c n", c=NCH)
        swh = wr1
        g1 = arena[:, o_w + 8192:o_w + 24576].rearrange("p (c n) -> p c n", c=NCH)
        o_h = carve(4096)
        h_nt = arena[:, o_h:o_h + 4096].rearrange("p (c n) -> p c n", c=NCH)
        o_x3 = carve(4096)
        x3m = arena[:, o_x3:o_x3 + 4096].rearrange("p (c n) -> p c n", c=NCH)
        o_mf = carve(2048)
        maskF = arena[0:16, o_mf:o_mf + 2048]
        o_sel = carve(512)
        selb = arena[0:16, o_sel:o_sel + 512].rearrange(
            "p (s r n) -> p s r n", s=2, r=2)
        o_sq = carve(1024)
        sq_scr = arena[:, o_sq:o_sq + 1024]
        o_xscr = carve(4096)
        xs_scr = arena[:, o_xscr:o_xscr + 4096].rearrange("p (c n) -> p c n",
                                                          c=NCH)
        o_mb = carve(512)
        mb_sb = arena[:, o_mb:o_mb + 512]
        o_sbc = carve(512)
        sbc_sb = arena[:, o_sbc:o_sbc + 512]
        y_sb = e(nc.sbuf_tensor(pfx + "y_sb", [128, NCH, S], F32))
        xres = e(nc.sbuf_tensor(pfx + "xres", [128, 2, D], F32))
        scr32 = e(nc.sbuf_tensor(pfx + "scr32", [128, 3072], F32))
        xhat32 = scr32[:, 0:2048].rearrange("p (r n) -> p r n", r=2)
        xf32 = scr32[:, 2048:3072].rearrange("p (c n) -> p c n", c=NCH)
        outt = scr32[:, 0:2048].rearrange("p (r n) -> p r n", r=2)
        wstage = e(nc.sbuf_tensor(pfx + "wstage", [128, 1024], F32))
        wstgB = scr32[:, 0:1024]
        tmp32 = e(nc.sbuf_tensor(pfx + "tmp32", [128, 512], F32))
        logits = e(nc.sbuf_tensor(pfx + "logits", [128, NT, 8], F32))
        exps = e(nc.sbuf_tensor(pfx + "exps", [128, NT, 8], F32))
        cmb = e(nc.sbuf_tensor(pfx + "cmb", [128, NT, 16], BF16))
        ident_bf = e(nc.sbuf_tensor(pfx + "ident_bf", [128, 128], BF16))
        rst = e(nc.sbuf_tensor(pfx + "rst", [128, 6, NT], F32))
        nmx, den, m1, m2 = rst[:, 0, :], rst[:, 1, :], rst[:, 2, :], rst[:, 3, :]
        w1s, w2s = rst[:, 4, :], rst[:, 5, :]
        l8sb = e(nc.sbuf_tensor(pfx + "l8sb", [8, 128], F32))
        ident_f = e(nc.sbuf_tensor(pfx + "ident_f", [128, 128], F32))
        gT_sb = e(nc.sbuf_tensor(pfx + "gT_sb", [128, NCH, 8], F32))
        b1F = e(nc.sbuf_tensor(pfx + "b1F", [128, NCH, 8], F32))
        b2F = e(nc.sbuf_tensor(pfx + "b2F", [128, NCH, 8], F32))
        b3F = e(nc.sbuf_tensor(pfx + "b3F", [128, NCH, 8], F32))
        ln1 = new_ln(nc, ctx, pfx + "ln1")
        ln2 = ln1
        var_s = e(nc.sbuf_tensor(pfx + "var_s", [128, 1], F32))
        psA = e(nc.psum_tensor(pfx + "psA", [128, 512], F32))
        psB = e(nc.psum_tensor(pfx + "psB", [128, 512], F32))
        psC = e(nc.psum_tensor(pfx + "psC", [128, 512], F32))
        psD = e(nc.psum_tensor(pfx + "psD", [128, 512], F32))
        psE = e(nc.psum_tensor(pfx + "psE", [128, 512], F32))
        psF = e(nc.psum_tensor(pfx + "psF", [128, 512], F32))
        psW = e(nc.psum_tensor(pfx + "psW", [128, 1024], F32))

        t_idf = make_identity(S_, ident_f[:, :])
        t_idb = make_identity(S_, ident_bf[:, :])
        t_g = S_.dma("sync", gT_sb[:, :, :],
                     gT_h[:, :].rearrange("(c p) e -> p c e", p=128))
        t_b1 = S_.dma("sync", b1F[:, :, :],
                      b1_h[:, :].rearrange("(c p) e -> p c e", p=128))
        t_b2 = S_.dma("sync", b2F[:, :, :],
                      b2_h[:, :].rearrange("(c p) e -> p c e", p=128))
        t_b3 = S_.dma("sync", b3F[:, :, :],
                      b3_h[:, :].rearrange("(c p) e -> p c e", p=128))

        for b in range(B_CORE if nbatch is None else nbatch):
            x_b = x_hbm[b]
            y_b = y_hbm[b]
            for t in range(NT):
                rot = t % 2
                t_x = S_.dma("sync", xres[:, rot, :],
                             x_b[t * 128:(t + 1) * 128, :],
                             wait=[S_.g(("xres", rot))])
                t_ln = ln_tile(S_, ln1, t, xres[:, rot, :], sq_scr, var_s[:, :],
                               wait=[t_x])
                t_ap = S_.on("vector",
                             lambda e_, t=t, rot=rot: e_.tensor_scalar(
                                 xhat32[:, rot, :], xres[:, rot, :],
                                 ln1["r"][:, t:t + 1], ln1["nmr"][:, t:t + 1],
                                 ALU.mult, ALU.add),
                             wait=[t_ln, S_.g(("xh", rot)),
                                   S_.g(("outt", rot))])
                S_.setg(("xres", rot), t_ap)

                def ftp(e_, rot=rot):
                    ins = None
                    for c in range(NCH):
                        ins = e_.matmul(psW[:, c * 128:(c + 1) * 128],
                                        xhat32[:, rot, c * 128:(c + 1) * 128],
                                        ident_f[:, :], is_transpose=True,
                                        start=(c % 4 == 0), stop=(c % 4 == 3),
                                        skip_group_check=True)
                    return ins
                t_tp = S_.on("tensor", ftp,
                             wait=[t_ap, t_idf, S_.g("psW")], inc="pe")
                S_.setg(("xh", rot), t_tp)
                t_e1 = S_.on("vector",
                             lambda e_, t=t: e_.tensor_copy(
                                 xhatF[:, :, t * 128:(t + 1) * 128],
                                 psW[:, :].rearrange("p (c n) -> p c n", c=NCH)),
                             wait=[t_tp])
                t_e2 = S_.on("vector",
                             lambda e_: e_.tensor_copy(
                                 xf32[:, :, :],
                                 psW[:, :].rearrange("p (c n) -> p c n", c=NCH)),
                             wait=[S_.g("xf32")])
                S_.setg("psW", t_e2)
                pairs = [(gT_sb[:, kc, :], xf32[:, kc, :]) for kc in range(NCH)]
                t_rm = mm_chain(S_, psA[0:8, 0:128], pairs,
                                wait=[t_e2, t_g, S_.g("psA")])
                t_l8 = S_.on("scalar",
                             lambda e_: e_.copy(l8sb[:, :], psA[0:8, 0:128]),
                             wait=[t_rm], inc="se")
                S_.setg("psA", t_l8)
                t_lt = S_.on("tensor",
                             lambda e_: e_.matmul(psB[0:128, 0:8], l8sb[:, :],
                                                  ident_f[0:8, 0:8], start=True,
                                                  stop=True,
                                                  skip_group_check=True),
                             wait=[t_l8, t_idf, S_.g("psB")], inc="pe")
                t_lg = S_.on("vector",
                             lambda e_, t=t: e_.tensor_copy(
                                 logits[:, t, :], psB[0:128, 0:8]),
                             wait=[t_lt])
                S_.setg("psB", t_lg)
                S_.setg("xf32", t_rm)
            S_.setg("wsB", S_.tok("pe"))
            S_.on("vector", lambda e_: e_.reduce_max(nmx, logits[:, :, :],
                                                     axis=AX.X))
            S_.on("vector", lambda e_: e_.tensor_scalar(nmx, nmx, -1.0, None,
                                                        ALU.mult))
            tv = S_.tok("ve")
            for t in range(NT):
                S_.on("scalar",
                      lambda e_, t=t: e_.activation(
                          exps[:, t, :], logits[:, t, :], AF.Exp,
                          bias=nmx[:, t:t + 1],
                          accum_out=den[:, t:t + 1]),
                      wait=[tv], inc="se")
            tse = S_.tok("se")
            S_.on("vector", lambda e_: e_.reciprocal(den, den), wait=[tse])
            S_.on("vector", lambda e_: e_.reduce_max(m1, exps[:, :, :],
                                                     axis=AX.X))
            for t in range(NT):
                S_.on("vector",
                      lambda e_, t=t: e_.tensor_scalar(
                          cmb[:, t, 0:8], exps[:, t, :], m1[:, t:t + 1], None,
                          ALU.is_ge))
            S_.on("vector",
                  lambda e_: e_.scalar_tensor_tensor(
                      logits[:, :, :], cmb[:, :, 0:8], -1e30, exps[:, :, :],
                      ALU.mult, ALU.add))
            S_.on("vector", lambda e_: e_.reduce_max(m2, logits[:, :, :],
                                                     axis=AX.X))
            for t in range(NT):
                S_.on("vector",
                      lambda e_, t=t: e_.tensor_scalar(
                          cmb[:, t, 8:16], logits[:, t, :], m2[:, t:t + 1],
                          None, ALU.is_ge))
            S_.on("vector", lambda e_: e_.tensor_tensor(w1s, m1, den, ALU.mult))
            S_.on("vector", lambda e_: e_.tensor_tensor(w2s, m2, den, ALU.mult))
            for t in range(NT):
                S_.on("vector",
                      lambda e_, t=t: e_.tensor_scalar(
                          exps[:, t, :], cmb[:, t, 0:8], w1s[:, t:t + 1], None,
                          ALU.mult))
                S_.on("vector",
                      lambda e_, t=t: e_.tensor_tensor(
                          cmb[:, t, 0:8], cmb[:, t, 0:8], cmb[:, t, 8:16],
                          ALU.add))
                S_.on("vector",
                      lambda e_, t=t: e_.scalar_tensor_tensor(
                          cmb[:, t, 8:16], cmb[:, t, 8:16], w2s[:, t:t + 1],
                          exps[:, t, :], ALU.mult, ALU.add))
            t_rr = S_.tok("ve")
            for t in range(NT):
                t_mt = S_.on("tensor",
                             lambda e_, t=t: e_.matmul(
                                 psA[0:16, 0:128], cmb[:, t, :], ident_bf[:, :],
                                 start=True, stop=True, skip_group_check=True),
                             wait=[t_rr, S_.g("psA"), t_idb], inc="pe")
                t_me = S_.on("vector",
                             lambda e_, t=t: e_.tensor_copy(
                                 maskF[:, t * 128:(t + 1) * 128],
                                 psA[0:16, 0:128]),
                             wait=[t_mt])
                S_.setg("psA", t_me)
            t_mask = S_.tok("ve")
            for hh in range(2):
                w1v = sw1_h[:, :].rearrange("(c p) n -> p c n", p=128)[
                    :, :, hh * D:(hh + 1) * D]
                t_w = load_w2(S_, w1v, wstage[:, :], wstgB, swh, NCH,
                              wait=[S_.g("swh"), S_.tok("str")])
                for mcl in range(NCH):
                    for nt in range(4):
                        psu, key = (psC, "psC") if (mcl * 4 + nt) % 2 == 0 \
                            else (psD, "psD")
                        pairs = [(swh[:, kc, mcl * 128:(mcl + 1) * 128],
                                  xhatF[:, kc, nt * 512:(nt + 1) * 512])
                                 for kc in range(NCH)]
                        t_mm = mm_chain(S_, psu[:, :], pairs,
                                        wait=[t_w, S_.g(key)])
                        t_ev = S_.on("scalar",
                                     lambda e_, mcl=mcl, nt=nt, psu=psu:
                                     e_.activation(
                                         g1[:, mcl, nt * 512:(nt + 1) * 512],
                                         psu[:, :], AF.Silu),
                                     wait=[t_mm], inc="se")
                        S_.setg(key, t_ev)
                S_.setg("swh", S_.tok("pe"))
                w2v = sw2_h[:, :].rearrange("(c p) n -> p c n", p=128)[
                    :, :, hh * D:(hh + 1) * D]
                t_w = load_w2(S_, w2v, wstage[:, :], wstgB, swh, NCH,
                              wait=[S_.g("swh")])
                for mcl in range(NCH):
                    for nt in range(4):
                        psu, key = (psC, "psC") if (mcl * 4 + nt) % 2 == 0 \
                            else (psD, "psD")
                        pairs = [(swh[:, kc, mcl * 128:(mcl + 1) * 128],
                                  xhatF[:, kc, nt * 512:(nt + 1) * 512])
                                 for kc in range(NCH)]
                        t_mm = mm_chain(S_, psu[:, :], pairs,
                                        wait=[t_w, S_.g(key)])
                        t_ev = S_.on("vector",
                                     lambda e_, mcl=mcl, nt=nt, psu=psu:
                                     e_.tensor_tensor(
                                         g1[:, mcl, nt * 512:(nt + 1) * 512],
                                         g1[:, mcl, nt * 512:(nt + 1) * 512],
                                         psu[:, :], ALU.mult),
                                     wait=[t_mm])
                        S_.setg(key, t_ev)
                S_.setg("swh", S_.tok("pe"))
                w3v = sw3_h[:, :].rearrange("(c p) n -> p c n", p=128)[
                    :, hh * NCH:(hh + 1) * NCH, :]
                t_w = load_w2(S_, w3v, wstage[:, :], wstgB, swh, NCH,
                              wait=[S_.g("swh")])
                for mc in range(NCH):
                    for nt in range(4):
                        psu, key = (psC, "psC") if (mc * 4 + nt) % 2 == 0 \
                            else (psD, "psD")
                        pairs = [(swh[:, kc, mc * 128:(mc + 1) * 128],
                                  g1[:, kc, nt * 512:(nt + 1) * 512])
                                 for kc in range(NCH)]
                        t_mm = mm_chain(S_, psu[:, :], pairs,
                                        wait=[t_w, S_.g(key)])
                        if hh == 0:
                            t_ev = S_.on("vector",
                                         lambda e_, mc=mc, nt=nt, psu=psu:
                                         e_.tensor_copy(
                                             y_sb[:, mc, nt * 512:(nt + 1) * 512],
                                             psu[:, :]),
                                         wait=[t_mm])
                        else:
                            t_ev = S_.on("vector",
                                         lambda e_, mc=mc, nt=nt, psu=psu:
                                         e_.tensor_tensor(
                                             y_sb[:, mc, nt * 512:(nt + 1) * 512],
                                             y_sb[:, mc, nt * 512:(nt + 1) * 512],
                                             psu[:, :], ALU.add),
                                         wait=[t_mm])
                        S_.setg(key, t_ev)
                S_.setg("swh", S_.tok("pe"))
            t_xhat = S_.tok("ve")

            def rebuild_selb(ei):
                sl = ei % 2
                S_.setg(("selb", sl), S_.tok("pe"))
                t_s0 = S_.on("gpsimd",
                             lambda e_, sl=sl: e_.memset(selb[:, sl, :, :],
                                                         0.0),
                             wait=[S_.g(("selb", sl))])
                S_.on("gpsimd",
                      lambda e_, ei=ei, sl=sl: e_.affine_select(
                          out=selb[:, sl, 0, :], in_=selb[:, sl, 0, :],
                          compare_op=ALU.not_equal, fill=1.0, base=-ei,
                          pattern=[[0, 128]], channel_multiplier=1),
                      wait=[t_s0])
                return S_.on("gpsimd",
                             lambda e_, ei=ei, sl=sl: e_.affine_select(
                                 out=selb[:, sl, 1, :], in_=selb[:, sl, 1, :],
                                 compare_op=ALU.not_equal, fill=1.0,
                                 base=-(8 + ei), pattern=[[0, 128]],
                                 channel_multiplier=1))

            def bl_issue(ei, nt, t_sel):
                sl = ei % 2
                n0 = nt * 512
                t_mb = S_.on("tensor",
                             lambda e_, sl=sl, n0=n0: e_.matmul(
                                 psE[:, :], selb[:, sl, 0, :],
                                 maskF[:, n0:n0 + 512], start=True, stop=True,
                                 skip_group_check=True),
                             wait=[t_mask, t_sel, S_.g("psE")], inc="pe")
                t_mbe = S_.on("scalar",
                              lambda e_: e_.copy(mb_sb, psE[:, :]),
                              wait=[t_mb, S_.g("mb_sb")], inc="se")
                S_.setg("psE", t_mbe)
                t_sb = S_.on("tensor",
                             lambda e_, sl=sl, n0=n0: e_.matmul(
                                 psF[:, :], selb[:, sl, 1, :],
                                 maskF[:, n0:n0 + 512], start=True, stop=True,
                                 skip_group_check=True),
                             wait=[t_mask, t_sel, S_.g("psF")], inc="pe")
                t_sbe = S_.on("scalar",
                              lambda e_: e_.copy(sbc_sb, psF[:, :]),
                              wait=[t_sb, S_.g("sbc_sb")], inc="se")
                S_.setg("psF", t_sbe)
                t_bld = None
                for kc in range(NCH):
                    t_bld = S_.on("gpsimd",
                                  lambda e_, kc=kc, n0=n0: e_.tensor_tensor(
                                      xs_scr[:, kc, :],
                                      xhatF[:, kc, n0:n0 + 512],
                                      sbc_sb, ALU.mult),
                                  wait=[t_sbe, t_xhat, S_.g("xs_scr")])
                S_.setg("sbc_sb", t_bld)
                return t_mbe, t_bld

            def loads_w13(ei):
                t_w1 = load_w2(S_, w1_h[ei].rearrange("(c p) n -> p c n",
                                                      p=128),
                               wstage[:, :], wstgB, wr1, NCH,
                               wait=[S_.g("wr1"), S_.g("swh")])
                t_w3 = load_w2(S_, w3_h[ei].rearrange("(c p) n -> p c n",
                                                      p=128),
                               wstage[:, :], wstgB, wr3, NCH,
                               wait=[S_.g("wr3"), S_.g("swh")])
                return t_w1, t_w3

            t_sel_cur = rebuild_selb(0)
            t_sel_nxt = None
            pend = None
            tw13 = loads_w13(0)
            tw13_nxt = None
            for ei in range(8):
                t_w1, t_w3 = tw13
                t_w2 = load_w2(S_, w2_h[ei].rearrange("(c p) n -> p c n",
                                                      p=128),
                               wstage[:, :], wstgB, wr2, NCH,
                               wait=[S_.g("wr2"), S_.g("swh")])
                if pend is None:
                    pend = bl_issue(ei, 0, t_sel_cur)
                for nt in range(4):
                    n0 = nt * 512
                    t_mbe, t_bld = pend
                    t_hlast = None
                    for mc in range(NCH):
                        psu, key = (psA, "psA") if mc % 2 == 0 else (psB, "psB")
                        pairs = [(wr1[:, kc, mc * 128:(mc + 1) * 128],
                                  xs_scr[:, kc, :])
                                 for kc in range(NCH)]
                        t_mm = mm_chain(S_, psu[:, :], pairs,
                                        wait=[t_w1, t_bld, S_.g(key)])
                        t_hlast = S_.on("scalar",
                                        lambda e_, mc=mc, psu=psu, ei=ei:
                                        e_.activation(
                                            h_nt[:, mc, :], psu[:, :], AF.Silu,
                                            bias=b1F[:, mc, ei:ei + 1]),
                                        wait=[t_mm, t_b1, S_.g("h_nt")],
                                        inc="se")
                        S_.setg(key, t_hlast)
                    if nt == 3:
                        S_.setg("wr1", S_.tok("pe"))
                    for mc in range(NCH):
                        psu, key = (psC, "psC") if mc % 2 == 0 else (psD, "psD")
                        pairs = [(wr3[:, kc, mc * 128:(mc + 1) * 128],
                                  xs_scr[:, kc, :])
                                 for kc in range(NCH)]
                        t_mm = mm_chain(S_, psu[:, :], pairs,
                                        wait=[t_w3, t_bld, S_.g(key)])
                        t_x3m = S_.on("vector",
                                      lambda e_, mc=mc, psu=psu, ei=ei:
                                      e_.scalar_tensor_tensor(
                                          x3m[:, mc, :], psu[:, :],
                                          b3F[:, mc, ei:ei + 1], mb_sb,
                                          ALU.add, ALU.mult),
                                      wait=[t_mm, t_b3, t_mbe,
                                            S_.g("x3m")])
                        S_.setg(key, t_x3m)
                    S_.setg("mb_sb", S_.tok("ve"))
                    S_.setg("xs_scr", S_.tok("pe"))
                    if nt == 3:
                        S_.setg("wr3", S_.tok("pe"))
                    if nt == 0 and ei < 7:
                        t_sel_nxt = rebuild_selb(ei + 1)
                    if nt < 3:
                        pend = bl_issue(ei, nt + 1, t_sel_cur)
                    elif ei < 7:
                        pend = bl_issue(ei + 1, 0, t_sel_nxt)
                        t_sel_cur = t_sel_nxt
                        tw13_nxt = loads_w13(ei + 1)
                    else:
                        pend = None
                    for mc in range(NCH):
                        psu = psW[:, 0:512] if mc % 2 == 0 else psW[:, 512:1024]
                        key = "psW0" if mc % 2 == 0 else "psW1"
                        pairs = [(wr2[:, kc, mc * 128:(mc + 1) * 128],
                                  h_nt[:, kc, :])
                                 for kc in range(NCH)]
                        t_mm = mm_chain(S_, psu, pairs,
                                        wait=[t_w2, S_.g(key), t_hlast])
                        t_st = S_.on("vector",
                                     lambda e_, mc=mc, psu=psu, ei=ei:
                                     e_.scalar_tensor_tensor(
                                         tmp32[:, :], psu,
                                         b2F[:, mc, ei:ei + 1],
                                         x3m[:, mc, :], ALU.add, ALU.mult),
                                     wait=[t_mm, t_b2])
                        t_ya = S_.on("vector",
                                     lambda e_, mc=mc, n0=n0:
                                     e_.tensor_tensor(
                                         y_sb[:, mc, n0:n0 + 512],
                                         y_sb[:, mc, n0:n0 + 512],
                                         tmp32[:, :], ALU.add))
                        S_.setg(key, t_ya)
                    S_.setg("h_nt", S_.tok("pe"))
                    S_.setg("x3m", S_.tok("ve"))
                S_.setg("wr2", S_.tok("pe"))
                tw13 = tw13_nxt
            t_y = S_.tok("ve")
            for t in range(NT):
                rot = t % 2

                def ftp2(e_, t=t):
                    ins = None
                    for c in range(NCH):
                        ins = e_.matmul(psW[:, c * 128:(c + 1) * 128],
                                        y_sb[:, c, t * 128:(t + 1) * 128],
                                        ident_f[:, :], is_transpose=True,
                                        start=(c % 4 == 0), stop=(c % 4 == 3),
                                        skip_group_check=True)
                    return ins
                t_tp = S_.on("tensor", ftp2, wait=[t_y, t_idf, S_.g("psW")],
                             inc="pe")
                t_ln = ln_tile(S_, ln2, t, psW[:, :], sq_scr, var_s[:, :],
                               wait=[t_tp], from_psum=True)
                t_x = S_.dma("sync", xres[:, rot, :],
                             x_b[t * 128:(t + 1) * 128, :],
                             wait=[S_.g(("xres", rot))])
                S_.on("vector",
                      lambda e_, t=t: e_.tensor_scalar(
                          outt[:, t % 2, :], psW[:, :],
                          ln2["r"][:, t:t + 1], ln2["nmr"][:, t:t + 1],
                          ALU.mult, ALU.add),
                      wait=[t_ln, S_.g(("outt", rot))])
                t_ap = S_.on("vector",
                             lambda e_, rot=rot: e_.tensor_tensor(
                                 outt[:, rot, :], outt[:, rot, :],
                                 xres[:, rot, :], ALU.add),
                             wait=[t_x])
                S_.setg("psW", t_ap)
                S_.setg(("xres", rot), t_ap)
                t_st = S_.dma("gpsimd", y_b[t * 128:(t + 1) * 128, :],
                              outt[:, rot, :], wait=[t_ap], inc="str")
                S_.setg(("outt", rot), t_st)
            S_.setg("y_sb", S_.tok("ve"))
            S_.setg(("xh", 0), S_.tok("pe"))
            S_.setg(("xh", 1), S_.tok("pe"))

        if debug_dump:
            allw = [S_.tok(s) for s in ("ve", "se", "pe", "gp", "dma")]
            Lx = locals()
            prev = None
            for nm in debug_dump:
                if "." in nm:
                    a_, b_ = nm.split(".")
                    ap = Lx[a_][b_]
                    nm = a_ + "_" + b_
                else:
                    ap = Lx[nm]
                if not isinstance(ap, bass.AP):
                    ap = ap[tuple(slice(None) for _ in ap.shape)]
                if len(ap.shape) > 2:
                    dims = " ".join(f"a{i}" for i in range(len(ap.shape) - 1))
                    ap = ap.rearrange(f"p {dims} -> p ({dims})")
                ncols = ap.shape[1]
                nparts = ap.shape[0]
                dbg = nc.dram_tensor(pfx + "dbg_" + nm, [nparts, ncols], F32,
                                     kind="ExternalOutput")
                for c0 in range(0, ncols, 2048):
                    cw = min(2048, ncols - c0)
                    t_c = S_.on("vector",
                                lambda e_, ap=ap, c0=c0, cw=cw, nparts=nparts:
                                e_.tensor_copy(tmp32[0:nparts, 0:min(cw, 512)]
                                               if cw <= 512 else
                                               y_sb[0:nparts, 0, 0:cw],
                                               ap[:, c0:c0 + cw]),
                                wait=allw + ([prev] if prev else []))
                    prev = S_.dma("gpsimd", dbg[:, c0:c0 + cw],
                                  tmp32[0:nparts, 0:min(cw, 512)]
                                  if cw <= 512 else y_sb[0:nparts, 0, 0:cw],
                                  wait=[t_c], inc="str")


NPOS = 256
NSTEP = 16
NT4 = 32


FFN1_WN = ["wfT", "ufT", "wiT", "uiT", "wcT", "ucT", "woT", "uoT"]
FFN1_BN = ["bf", "bi", "bc", "bo", "in_c"]


def build_l1ffn(sim_debug=False, nstep=None, debug_dump=()):
    nc = bass.Bass("TRN2", target_bir_lowering=False,
                   detect_race_conditions=not sim_debug)
    io = {"x": nc.dram_tensor("x", [NSTEP, NPOS, D], F32, kind="ExternalInput"),
          "y": nc.dram_tensor("y", [NSTEP, NPOS, D], F32, kind="ExternalOutput")}
    for k in FFN1_WN:
        io[k] = nc.dram_tensor(k, [D, D], F32, kind="ExternalInput")
    for k in FFN1_BN:
        io[k] = nc.dram_tensor(k, [D], F32, kind="ExternalInput")
    with ExitStack() as ctx:
        S_ = Sched(nc, ctx)
        block = ctx.enter_context(nc.Block())
        emit_l1ffn(nc, S_, io, nstep=nstep, debug_dump=debug_dump)
        S_.on("gpsimd", lambda e_: e_.engine_nop(), wait=[S_.tok("str")])
        S_.finalize(block)
    return nc


def emit_l1ffn(nc, S_, io, pfx="", nstep=None, debug_dump=(), x_wait=()):
    x_hbm = io["x"]
    y_hbm = io["y"]
    Wh = io
    Bh = io

    with ExitStack() as ctx:
        def e(cm):
            return ctx.enter_context(cm)
        wreg = e(nc.sbuf_tensor(pfx + "wreg", [128, 6, NCH, D], BF16))
        wv = {k: wreg[:, i, :, :] for i, k in
              enumerate(["wfT", "ufT", "wiT", "uiT", "wcT", "ucT"])}
        wv["woT"] = wv["wfT"]
        wv["uoT"] = wv["ufT"]
        wstage = e(nc.sbuf_tensor(pfx + "wstage", [128, 1024], F32))
        xres = e(nc.sbuf_tensor(pfx + "xres", [128, 2, D], F32))
        xhat_tok = e(nc.sbuf_tensor(pfx + "xhat_tok", [128, 2, D], BF16))
        xFs = e(nc.sbuf_tensor(pfx + "xFs", [128, 2, NCH, NPOS], BF16))
        c_sb = e(nc.sbuf_tensor(pfx + "c_sb", [128, NCH, NPOS], F32))
        cbf = e(nc.sbuf_tensor(pfx + "cbf", [128, NCH, NPOS], BF16))
        gf = e(nc.sbuf_tensor(pfx + "gf", [128, NCH, NPOS], F32))
        gi = e(nc.sbuf_tensor(pfx + "gi", [128, NCH, NPOS], F32))
        gc = e(nc.sbuf_tensor(pfx + "gc", [128, NCH, NPOS], F32))
        cuo = gf
        hF = gi
        ones_t = e(nc.sbuf_tensor(pfx + "ones_t", [128, NPOS], F32))
        biasF = e(nc.sbuf_tensor(pfx + "biasF", [128, 5, NCH], F32))
        bvw = {k: biasF[:, i, :] for i, k in
               enumerate(["bf", "bi", "bc", "bo", "in_c"])}
        ln1 = new_ln(nc, ctx, pfx + "ln1", ntiles=NT4)
        ln2 = new_ln(nc, ctx, pfx + "ln2", ntiles=NT4)
        var_s = e(nc.sbuf_tensor(pfx + "var_s", [128, 1], F32))
        sq_scr = e(nc.sbuf_tensor(pfx + "sq_scr", [128, D], BF16))
        ident_f = e(nc.sbuf_tensor(pfx + "ident_f", [128, 128], F32))
        ident_b = e(nc.sbuf_tensor(pfx + "ident_b", [128, 128], BF16))
        outt = e(nc.sbuf_tensor(pfx + "outt", [128, 2, D], F32))
        psA = e(nc.psum_tensor(pfx + "psA", [128, 512], F32))
        psB = e(nc.psum_tensor(pfx + "psB", [128, 512], F32))
        psW = e(nc.psum_tensor(pfx + "psW", [128, 1024], F32))
        psT = e(nc.psum_tensor(pfx + "psT", [128, 1024], BF16))

        t_idf = make_identity(S_, ident_f[:, :])
        t_idb = make_identity(S_, ident_b[:, :])
        t_one = S_.on("gpsimd", lambda e_: e_.memset(ones_t[:, :], 1.0))
        for k in ["bf", "bi", "bc", "bo", "in_c"]:
            S_.dma("sync", bvw[k],
                   Bh[k][:].rearrange("(c p) -> p c", p=128),
                   allow_slow_non_contiguous=True)
        t_bias = S_.tok("dma")
        stg = (wstage[:, :], outt[:, 0, :])
        t_wl = None
        for k in ["wfT", "ufT", "wiT", "uiT", "wcT", "ucT"]:
            t_wl = load_w_half(S_, Wh[k][:, :], stg, wv[k], NCH, D)
        t_wload = t_wl
        S_.setg(("outt", 0), t_wl)
        for ch in range(NCH):
            S_.on("vector",
                  lambda e_, ch=ch: e_.tensor_scalar(
                      c_sb[:, ch, :], ones_t[:, 0:NPOS],
                      bvw["in_c"][:, ch:ch + 1], None, ALU.mult),
                  wait=[t_one, t_bias])
        t_c0 = S_.on("vector", lambda e_: e_.tensor_copy(
            cbf[:, :, :], c_sb[:, :, :]))

        def prep_xhat(ts, ln, rot):
            toks = []
            for half in range(2):
                tile = ts * 2 + half
                hrot = half
                t_x = S_.dma("sync", xres[:, hrot, :],
                             x_hbm[ts][half * 128:(half + 1) * 128, :],
                             wait=[S_.g(("xres", hrot))] + list(x_wait))
                if ln is ln1:
                    t_ln = ln_tile(S_, ln, tile, xres[:, hrot, :], sq_scr,
                                   var_s[:, :], wait=[t_x])
                else:
                    t_ln = t_x
                t_ap = S_.on("vector",
                             lambda e_, tile=tile, hrot=hrot: e_.tensor_scalar(
                                 xhat_tok[:, hrot, :], xres[:, hrot, :],
                                 ln1["r"][:, tile:tile + 1],
                                 ln1["nmr"][:, tile:tile + 1],
                                 ALU.mult, ALU.add),
                             wait=[t_ln, S_.g(("xht", hrot))])
                S_.setg(("xres", hrot), t_ap)

                def ftp(e_, hrot=hrot):
                    ins = None
                    for cc in range(NCH):
                        ins = e_.matmul(psT[:, cc * 128:(cc + 1) * 128],
                                        xhat_tok[:, hrot, cc * 128:(cc + 1) * 128],
                                        ident_b[:, :], is_transpose=True,
                                        start=(cc == 0), stop=(cc == NCH - 1),
                                        skip_group_check=True)
                    return ins
                t_tp = S_.on("tensor", ftp,
                             wait=[t_ap, t_idb, S_.g("psT")], inc="pe")
                S_.setg(("xht", hrot), t_tp)
                t_ev = S_.on("vector",
                             lambda e_, rot=rot, half=half: e_.tensor_copy(
                                 xFs[:, rot, :, half * 128:(half + 1) * 128],
                                 psT[:, :].rearrange("p (c n) -> p c n",
                                                     c=NCH)),
                             wait=[t_tp])
                S_.setg("psT", t_ev)
                toks.append(t_ev)
            return toks[-1]

        GATES = [("wfT", "ufT", "bf", AF.Sigmoid, gf),
                 ("wiT", "uiT", "bi", AF.Sigmoid, gi),
                 ("wcT", "ucT", "bc", AF.Tanh, gc)]
        t_cup = t_c0
        for ts in range(NSTEP if nstep is None else nstep):
            rot = ts % 2
            t_xf = prep_xhat(ts, ln1, rot)
            for (wk, uk, bk, act, dst) in GATES:
                for mc in range(NCH):
                    psu, key = (psA, "psA") if mc % 2 == 0 else (psB, "psB")
                    pairs = ([(wv[wk][:, kc, mc * 128:(mc + 1) * 128],
                               xFs[:, rot, kc, :]) for kc in range(NCH)] +
                             [(wv[uk][:, kc, mc * 128:(mc + 1) * 128],
                               cbf[:, kc, :]) for kc in range(NCH)])
                    t_mm = mm_chain(S_, psu[:, 0:NPOS], pairs,
                                    wait=[t_wload, t_xf, t_cup, S_.g(key)])
                    t_g = S_.on("scalar",
                                lambda e_, mc=mc, psu=psu, dst=dst, act=act,
                                bk=bk: e_.activation(
                                    dst[:, mc, :], psu[:, 0:NPOS], act,
                                    bias=bvw[bk][:, mc:mc + 1]),
                                wait=[t_mm, t_bias], inc="se")
                    S_.setg(key, t_g)
            t_gates = S_.tok("se")
            S_.on("vector", lambda e_: e_.tensor_tensor(
                gf[:, :, :], gf[:, :, :], c_sb[:, :, :], ALU.mult),
                wait=[t_gates])
            S_.on("vector", lambda e_: e_.tensor_tensor(
                gi[:, :, :], gi[:, :, :], gc[:, :, :], ALU.mult))
            S_.on("vector", lambda e_: e_.tensor_tensor(
                c_sb[:, :, :], gf[:, :, :], gi[:, :, :], ALU.add))
            t_cup = S_.on("vector", lambda e_: e_.tensor_copy(
                cbf[:, :, :], c_sb[:, :, :]))

        t_pe_done = S_.tok("pe")
        load_w(S_, Wh["woT"][:, :].rearrange("(c p) n -> p c n", p=128),
               wstage, wv["woT"], NCH, D, wait=[t_pe_done])
        load_w(S_, Wh["uoT"][:, :].rearrange("(c p) n -> p c n", p=128),
               wstage, wv["uoT"], NCH, D, wait=[t_pe_done])
        t_wo = S_.tok("ve")
        for mc in range(NCH):
            psu, key = (psA, "psA") if mc % 2 == 0 else (psB, "psB")
            pairs = [(wv["uoT"][:, kc, mc * 128:(mc + 1) * 128],
                      cbf[:, kc, :]) for kc in range(NCH)]
            t_mm = mm_chain(S_, psu[:, 0:NPOS], pairs,
                            wait=[t_wo, t_cup, S_.g(key)])
            t_cu = S_.on("vector",
                         lambda e_, mc=mc, psu=psu: e_.tensor_copy(
                             cuo[:, mc, :], psu[:, 0:NPOS]),
                         wait=[t_mm])
            S_.setg(key, t_cu)
        t_cuo = S_.tok("ve")
        for ts in range(NSTEP if nstep is None else nstep):
            rot = ts % 2
            t_xf = prep_xhat(ts, ln2, rot)
            for mc in range(NCH):
                psu, key = (psA, "psA") if mc % 2 == 0 else (psB, "psB")
                pairs = [(wv["woT"][:, kc, mc * 128:(mc + 1) * 128],
                          xFs[:, rot, kc, :]) for kc in range(NCH)]
                t_mm = mm_chain(S_, psu[:, 0:NPOS], pairs,
                                wait=[t_wo, t_xf, t_cuo, S_.g(key)])
                t_add = S_.on("vector",
                              lambda e_, mc=mc, psu=psu: e_.tensor_tensor(
                                  psu[:, 0:NPOS], psu[:, 0:NPOS],
                                  cuo[:, mc, :], ALU.add),
                              wait=[t_mm])
                t_h = S_.on("scalar",
                            lambda e_, mc=mc, psu=psu: e_.activation(
                                hF[:, mc, :], psu[:, 0:NPOS], AF.Tanh,
                                bias=bvw["bo"][:, mc:mc + 1]),
                            wait=[t_bias, t_add, S_.g("hF")], inc="se")
                S_.setg(key, t_h)
            t_hd = S_.tok("se")
            for half in range(2):
                tile = ts * 2 + half
                hrot = half

                def ftp2(e_, half=half):
                    ins = None
                    for cc in range(NCH):
                        ins = e_.matmul(psW[:, cc * 128:(cc + 1) * 128],
                                        hF[:, cc, half * 128:(half + 1) * 128],
                                        ident_f[:, :], is_transpose=True,
                                        start=(cc % 4 == 0),
                                        stop=(cc % 4 == 3),
                                        skip_group_check=True)
                    return ins
                t_tp = S_.on("tensor", ftp2,
                             wait=[t_hd, t_idf, S_.g("psW")], inc="pe")
                t_ln = ln_tile(S_, ln2, tile, psW[:, :], sq_scr, var_s[:, :],
                               wait=[t_tp], from_psum=True)
                t_x = S_.dma("sync", xres[:, hrot, :],
                             x_hbm[ts][half * 128:(half + 1) * 128, :],
                             wait=[S_.g(("xres", hrot))])
                S_.on("vector",
                      lambda e_, tile=tile, hrot=hrot: e_.tensor_scalar(
                          outt[:, hrot, :], psW[:, :],
                          ln2["r"][:, tile:tile + 1],
                          ln2["nmr"][:, tile:tile + 1],
                          ALU.mult, ALU.add),
                      wait=[t_ln, S_.g(("outt", hrot))])
                t_ap = S_.on("vector",
                             lambda e_, hrot=hrot: e_.tensor_tensor(
                                 outt[:, hrot, :], outt[:, hrot, :],
                                 xres[:, hrot, :], ALU.add),
                             wait=[t_x])
                S_.setg("psW", t_ap)
                S_.setg(("xres", hrot), t_ap)
                t_st = S_.dma("gpsimd", y_hbm[ts][half * 128:(half + 1) * 128, :],
                              outt[:, hrot, :], wait=[t_ap], inc="str")
                S_.setg(("outt", hrot), t_st)
            S_.setg("hF", S_.tok("pe"))


def build_fused(sim_debug=False):
    nc = bass.Bass("TRN2", target_bir_lowering=False,
                   detect_race_conditions=not sim_debug)
    io1 = {"x": nc.dram_tensor("x", [B_CORE, S, D], F32, kind="ExternalInput")}
    for k, sh in ATTN_W[True]:
        io1[k] = nc.dram_tensor("a1" + k, sh, F32, kind="ExternalInput")
    io2 = {}
    for k, sh in FFN0_W:
        io2[k] = nc.dram_tensor("f2" + k, sh, F32, kind="ExternalInput")
    io3 = {}
    for k, sh in ATTN_W[False]:
        io3[k] = nc.dram_tensor("a3" + k, sh, F32, kind="ExternalInput")
    io4 = {"y": nc.dram_tensor("y", [NSTEP, NPOS, D], F32,
                               kind="ExternalOutput")}
    for k in FFN1_WN:
        io4[k] = nc.dram_tensor("f4" + k, [D, D], F32, kind="ExternalInput")
    for k in FFN1_BN:
        io4[k] = nc.dram_tensor("f4" + k, [D], F32, kind="ExternalInput")
    x1_i = nc.dram_tensor("x1_i", [B_CORE, S, D], F32)
    x2_i = nc.dram_tensor("x2_i", [B_CORE, S, D], F32)
    cc_in = nc.dram_tensor("cc_in", [8, B_CORE * NPOS * D], F32)
    cc_out = nc.dram_tensor("cc_out", [8, B_CORE * NPOS * D], F32)
    io1["y"] = x1_i
    io2["x"] = x1_i
    io2["y"] = x2_i
    io3["x"] = x2_i
    io3["y"] = cc_in
    io4["x"] = cc_out.ap().rearrange("s (b p n) -> (s b) p n",
                                     b=B_CORE, p=NPOS)

    from concourse.aot_env import allow_ici_all_to_all
    with ExitStack() as ctx, allow_ici_all_to_all():
        S_ = Sched(nc, ctx)
        block = ctx.enter_context(nc.Block())
        S_.set_prefix("a1_")
        emit_attn(nc, S_, io1, mla=True, pfx="a1_")
        S_.barrier("f2_")
        emit_l0ffn(nc, S_, io2, pfx="f2_")
        S_.barrier("a3_")
        bnc = cc_in.ap().rearrange("j (b p n) -> j b p n", b=B_CORE, p=NPOS)

        def yv(bb, tt):
            return bnc[tt // 2, bb, (tt % 2) * 128:(tt % 2) * 128 + 128, :]
        emit_attn(nc, S_, io3, mla=False, pfx="a3_", y_view=yv)
        S_.barrier("cc_")
        S_.on("gpsimd",
              lambda e_: e_.collective_compute(
                  "AllToAll", ALU.bypass,
                  replica_groups=[list(range(8))],
                  ins=[cc_in.ap().opt()], outs=[cc_out.ap().opt()]),
              inc="cc", is_dma=True)
        t_cc = S_.tok("cc")
        S_.barrier("f4_", exclude=("cc",))
        emit_l1ffn(nc, S_, io4, pfx="f4_", x_wait=[t_cc])
        S_.barrier("zz_")
        S_.finalize(block)
    return nc


def kernel(**inp):
    from concourse.aot_env import allow_ici_all_to_all
    ncores = 8
    f = lambda a: np.ascontiguousarray(np.asarray(a, dtype=np.float32))
    tr = lambda a: np.ascontiguousarray(np.asarray(a, dtype=np.float32).T)
    x = f(inp["L0_attn_x"])
    w = {"a1wq": f(inp["L0_attn_W_q"]), "a1wk": f(inp["L0_attn_W_k"]),
         "a1wv": f(inp["L0_attn_W_v"]), "a1wo": f(inp["L0_attn_W_o"]),
         "a1wdn": f(inp["L0_attn_W_down_kv"]),
         "a1wuk": f(inp["L0_attn_W_up_k"]), "a1wuv": f(inp["L0_attn_W_up_v"]),
         "f2gT": tr(inp["L0_ffn_gate_w"]),
         "f2w1T": np.ascontiguousarray(
             np.asarray(inp["L0_ffn_w1"], np.float32).transpose(0, 2, 1)),
         "f2w2T": np.ascontiguousarray(
             np.asarray(inp["L0_ffn_w2"], np.float32).transpose(0, 2, 1)),
         "f2w3T": np.ascontiguousarray(
             np.asarray(inp["L0_ffn_w3"], np.float32).transpose(0, 2, 1)),
         "f2b1T": tr(inp["L0_ffn_b1"]), "f2b2T": tr(inp["L0_ffn_b2"]),
         "f2b3T": tr(inp["L0_ffn_b3"]),
         "f2sw1T": tr(inp["L0_ffn_sw1"]), "f2sw2T": tr(inp["L0_ffn_sw2"]),
         "f2sw3T": tr(inp["L0_ffn_sw3"]),
         "a3wqkv": tr(inp["L1_attn_W_qkv"]), "a3wo": tr(inp["L1_attn_W_o"]),
         "f4wfT": tr(inp["L1_ffn_wf"]), "f4ufT": tr(inp["L1_ffn_uf"]),
         "f4wiT": tr(inp["L1_ffn_wi"]), "f4uiT": tr(inp["L1_ffn_ui"]),
         "f4wcT": tr(inp["L1_ffn_wc"]), "f4ucT": tr(inp["L1_ffn_uc"]),
         "f4woT": tr(inp["L1_ffn_wo"]), "f4uoT": tr(inp["L1_ffn_uo"]),
         "f4bf": f(inp["L1_ffn_bf"]), "f4bi": f(inp["L1_ffn_bi"]),
         "f4bc": f(inp["L1_ffn_bc"]), "f4bo": f(inp["L1_ffn_bo"]),
         "f4in_c": f(inp["L1_ffn_in_c"])}
    nc = build_fused()
    in_maps = [{"x": x[2 * c:2 * c + 2], **w} for c in range(ncores)]
    with allow_ici_all_to_all():
        res = run_bass_kernel_spmd(nc, in_maps, core_ids=list(range(ncores)))
    out = np.concatenate([res.results[c]["y"] for c in range(ncores)], axis=1)
    return out
